# version 06
# baseline (speedup 1.0000x reference)
import numpy as np
import ml_dtypes
from contextlib import ExitStack

import concourse.bass as bass
import concourse.mybir as mybir
from concourse.bass_utils import run_bass_kernel_spmd

F32 = mybir.dt.float32
BF16 = mybir.dt.bfloat16
ALU = mybir.AluOpType
AF = mybir.ActivationFunctionType
AX = mybir.AxisListType.X

RND_C = float(1.5 * 2.0 ** 23)
D = 1024
NT = 4096
DI = 2816
NEC = 8
NIC = 22
NTT = 32
NTG = 8
LN_EPS = 1e-5
BN_EPS = 1e-5
NC_N = 8


class Arena:
    def __init__(self, tensor, cols):
        self.t, self.cols, self.off, self.marks = tensor, cols, 0, []

    def alloc(self, n):
        n = (n + 15) // 16 * 16
        assert self.off + n <= self.cols, f"arena overflow {self.off}+{n}>{self.cols}"
        o = self.off
        self.off += n
        return self.t[:, o:o + n]

    def push(self):
        self.marks.append(self.off)

    def pop(self):
        self.off = self.marks.pop()


def build_nc(stop_after=99, debug=False):
    nc = bass.Bass(target_bir_lowering=False, detect_race_conditions=False)
    x_in = nc.dram_tensor("x", [NT, D], F32, kind="ExternalInput")
    Wq = nc.dram_tensor("Wq", [D, D], F32, kind="ExternalInput")
    Wk = nc.dram_tensor("Wk", [D, D], F32, kind="ExternalInput")
    Wv = nc.dram_tensor("Wv", [D, D], F32, kind="ExternalInput")
    Wp = nc.dram_tensor("Wp", [D, D], F32, kind="ExternalInput")
    gq = nc.dram_tensor("gq", [D], F32, kind="ExternalInput")
    bq = nc.dram_tensor("bq", [D], F32, kind="ExternalInput")
    gk = nc.dram_tensor("gk", [D], F32, kind="ExternalInput")
    bk = nc.dram_tensor("bk", [D], F32, kind="ExternalInput")
    gv = nc.dram_tensor("gv", [D], F32, kind="ExternalInput")
    bv = nc.dram_tensor("bv", [D], F32, kind="ExternalInput")
    bp = nc.dram_tensor("bp", [D], F32, kind="ExternalInput")
    clip_x = nc.dram_tensor("clip_x", [1], F32, kind="ExternalInput")
    clip_dn = nc.dram_tensor("clip_dn", [1], F32, kind="ExternalInput")
    w_gate = nc.dram_tensor("w_gate", [DI, D], F32, kind="ExternalInput")
    w_up = nc.dram_tensor("w_up", [DI, D], F32, kind="ExternalInput")
    w_down = nc.dram_tensor("w_down", [D, DI], F32, kind="ExternalInput")
    ones_r = nc.dram_tensor("ones_r", [1, 128], F32, kind="ExternalInput")
    out_t = nc.dram_tensor("out", [NT, D], F32, kind="ExternalOutput")

    sk = "ExternalOutput" if debug else "Internal"
    xln_s = nc.dram_tensor("xln_s", [NT, D], BF16, kind=sk)
    h_s = nc.dram_tensor("h_s", [3, D, NT], F32, kind=sk)
    v_s = nc.dram_tensor("v_s", [D, NT], BF16, kind="Internal")
    x1_s = nc.dram_tensor("x1_s", [NT, D], F32, kind=sk)
    xq_s = nc.dram_tensor("xq_s", [NT, D], BF16, kind=sk)
    hf_s = nc.dram_tensor("hf_s", [DI, NT], F32, kind=sk)
    wbf_s = nc.dram_tensor("wbf_s", [4, D, D], BF16, kind="Internal")
    wgq_s = nc.dram_tensor("wgq_s", [DI, D], BF16, kind=sk)
    wuq_s = nc.dram_tensor("wuq_s", [DI, D], BF16, kind="Internal")
    wdq_s = nc.dram_tensor("wdq_s", [D, DI], BF16, kind="Internal")
    st_i = nc.dram_tensor("st_i", [128, 48], F32, kind="Internal")
    st_o = nc.dram_tensor("st_o", [128, 48], F32, kind="Internal")
    mx_i = nc.dram_tensor("mx_i", [128], F32, kind="Internal")
    mx_o = nc.dram_tensor("mx_o", [128], F32, kind="Internal")
    mh_i = nc.dram_tensor("mh_i", [128], F32, kind="Internal")
    mh_o = nc.dram_tensor("mh_o", [128], F32, kind="Internal")
    swd_d = nc.dram_tensor("swd_d", [D], F32, kind="Internal")

    es = ExitStack()
    sbuf = lambda name, shape, dt: es.enter_context(nc.sbuf_tensor(name, shape, dt))
    psum = lambda name, shape: es.enter_context(nc.psum_tensor(name, shape, F32))
    sem = lambda name: es.enter_context(nc.semaphore(name))
    sem2 = lambda name: [es.enter_context(nc.semaphore(name + "0")),
                         es.enter_context(nc.semaphore(name + "1"))]

    A16 = Arena(sbuf("A16", [128, 61440], BF16), 61440)
    A32 = Arena(sbuf("A32", [128, 13312], F32), 13312)

    gam = sbuf("gam", [128, 24], F32)
    bet = sbuf("bet", [128, 24], F32)
    abn = sbuf("abn", [128, 24], F32)
    bbn = sbuf("bbn", [128, 24], F32)
    bnsl = sbuf("bnsl", [128, 24 * 64], F32)
    stat = sbuf("stat", [128, 64], F32)
    gst = sbuf("gst", [128, 64], F32)
    scr = sbuf("scr", [128, 64], F32)
    scr2 = sbuf("scr2", [128, 64], F32)
    ln2m = sbuf("ln2m", [128, NTT], F32)
    ln2r = sbuf("ln2r", [128, NTT], F32)
    mxa = sbuf("mxa", [128, 4], F32)
    onesb = sbuf("onesb", [1, 128], F32)
    rowb = sbuf("rowb", [1, 144], F32)
    sc_x = sbuf("sc_x", [128, 4], F32)
    bp_row = sbuf("bp_row", [1, D], F32)
    bp_bc = sbuf("bp_bc", [128, D], F32)
    swd_row = sbuf("swd_row", [1, D], F32)
    swd_bc = sbuf("swd_bc", [128, D], F32)
    sw_g = sbuf("sw_g", [128, NIC], F32)
    sw_u = sbuf("sw_u", [128, NIC], F32)
    sw_d = sbuf("sw_d", [128, NEC], F32)
    scg = sbuf("scg", [128, NIC], F32)
    scu = sbuf("scu", [128, NIC], F32)
    clp = sbuf("clp", [1, 16], F32)
    mxsl = sbuf("mxsl", [128, 192], F32)

    pA = psum("pA", [128, 512])
    pB = psum("pB", [128, 512])
    pC = psum("pC", [128, 512])
    pD = psum("pD", [128, 512])
    pG = psum("pG", [128, 1024])
    pM = psum("pM", [128, 1024])
    pbank = [pA, pB, pC, pD]

    dX = sem2("dX")
    dW = sem2("dW")
    dHL = sem2("dHL")
    dVT = sem2("dVT")
    dX6 = sem2("dX6")
    dX1S = sem2("dX1S")
    dX1L = sem2("dX1L")
    dXQS = sem2("dXQS")
    dWF = sem2("dWF")
    dWGT = sem2("dWGT")
    dHFL = sem2("dHFL")
    dHF = sem2("dHF")
    dOUT = sem2("dOUT")
    dVS = sem2("dVS")
    dH = sem2("dH")
    dWS = sem2("dWS");  dWQS = sem2("dWQS")
    dPrm = sem("dPrm");  dWT = sem("dWT");  dWPT = sem("dWPT")
    dXLN = sem("dXLN");  dXLT = sem("dXLT");  dST = sem("dST");  dXQT = sem("dXQT")
    dWDS = sem("dWDS");  dWDT = sem("dWDT");  dMX = sem("dMX")
    dSW = sem("dSW");  dSWR = sem("dSWR")
    cCC = sem("cCC")
    sSq = sem("sSq");  vVar = sem("vVar");  sSqrt = sem("sSqrt");  vApp = sem("vApp")
    vCast = sem("vCast")
    tMM = sem("tMM");  sCp = sem2("sCp");  vBn = sem2("vBn")
    vSt = sem("vSt");  vSt2 = sem("vSt2");  sSq3 = sem("sSq3");  vAB = sem("vAB")
    sSig = sem("sSig");  vQk = sem("vQk");  tMM4 = sem("tMM4")
    vVV = sem("vVV");  tM5 = sem("tM5");  vM5 = sem("vM5");  tBp = sem("tBp");  vBp = sem("vBp")
    gXp = sem("gXp");  tP6 = sem("tP6");  vX1 = sem("vX1")
    sSq6 = sem("sSq6");  vVar6 = sem("vVar6");  sSqrt6 = sem("sSqrt6");  vMx = sem("vMx")
    sClp = sem("sClp");  vSc = sem("vSc");  tSc = sem("tSc");  vSc2 = sem("vSc2")
    vXq = sem("vXq");  vWq = sem("vWq");  vCg = sem("vCg")
    tFF = sem("tFF");  sSil = sem("sSil");  vFF = sem("vFF")
    vSc3 = sem("vSc3");  tSc3 = sem("tSc3");  vSc4 = sem("vSc4");  vSwd = sem("vSwd")
    tSw = sem("tSw");  vSw = sem("vSw")
    vHq = sem("vHq");  tDn = sem("tDn");  vX2 = sem("vX2")

    xb = [A32.alloc(1024), A32.alloc(1024)]
    sqscr = A32.alloc(1024)
    _r0 = A32.off
    hb = [A32.alloc(512), A32.alloc(512)]
    hq_b = [A32.alloc(512), A32.alloc(512)]
    hk_b = [A32.alloc(512), A32.alloc(512)]
    hv_b = [A32.alloc(512), A32.alloc(512)]
    xpre_b = [A32.alloc(1024), A32.alloc(1024)]
    assert A32.off - _r0 >= 2 * 2816
    wf_b = [A32.t[:, _r0:_r0 + 2816], A32.t[:, _r0 + 2816:_r0 + 5632]]
    x1_b = [A32.alloc(1024), A32.alloc(1024)]
    xlnscr = A32.alloc(1024)
    sgf = [hq_b[0], hq_b[1], hk_b[0], hk_b[1]]
    hfb = [hv_b[0], hv_b[1]]

    wpT_sb = A16.alloc(8 * 1024)
    A16.push()
    wt_sb = [A16.alloc(64 * 128) for _ in range(3)]
    xlt_b = A16.alloc(8 * 512)
    xln_b = [A16.alloc(1024), A16.alloc(1024)]
    wb_b = [A16.alloc(1024), A16.alloc(1024)]
    A16.pop()
    A16.push()
    qk_res = [A16.alloc(4096) for _ in range(8)]
    qb_b = A16.alloc(512)
    kb_b = A16.alloc(512)
    vb_b = [A16.alloc(512), A16.alloc(512)]
    vt_b = [A16.alloc(1024), A16.alloc(1024)]
    vv_sb = A16.alloc(1024)
    M_sb = [A16.alloc(1024) for _ in range(8)]
    xq_b = [A16.alloc(1024), A16.alloc(1024)]
    wqg_b = [A16.alloc(1024), A16.alloc(1024)]
    wqd_b = A16.alloc(2816)
    A16.pop()
    A16.push()
    xqT_sb = [A16.alloc(4096) for _ in range(8)]
    wgT_b = [A16.alloc(8 * 128), A16.alloc(8 * 128)]
    wuT_b = [A16.alloc(8 * 128), A16.alloc(8 * 128)]
    A16.pop()
    A16.push()
    wdT_sb = A16.alloc(44 * 512)
    hint_sb = [[A16.alloc(512) for _ in range(NIC)], [A16.alloc(512) for _ in range(NIC)]]
    A16.pop()

    SA = stop_after
    N_PRM = 16 * (6 * NEC + 4)


    def spacer(eng):
        eng.tensor_copy(scr[:, 30:31], scr[:, 31:32])

    def emit_ln_stats(eng, xsrc, mean_slot):
        eng.bn_stats(scr[:, 32:38], xsrc[:, 0:512])
        eng.bn_stats(scr[:, 40:46], xsrc[:, 512:1024])
        eng.tensor_add(scr[:, 48:49], scr[:, 33:34], scr[:, 36:37])
        eng.tensor_add(scr[:, 49:50], scr[:, 34:35], scr[:, 37:38])
        eng.tensor_tensor(scr[:, 50:51], scr[:, 33:34], scr[:, 33:34], ALU.mult)
        eng.tensor_tensor(scr[:, 51:52], scr[:, 36:37], scr[:, 36:37], ALU.mult)
        eng.tensor_add(scr[:, 52:53], scr[:, 41:42], scr[:, 44:45])
        eng.tensor_add(scr[:, 53:54], scr[:, 42:43], scr[:, 45:46])
        eng.tensor_tensor(scr[:, 54:55], scr[:, 41:42], scr[:, 41:42], ALU.mult)
        eng.tensor_tensor(scr[:, 55:56], scr[:, 44:45], scr[:, 44:45], ALU.mult)
        eng.tensor_add(scr[:, 56:57], scr[:, 48:49], scr[:, 52:53])
        eng.tensor_add(scr[:, 57:58], scr[:, 50:51], scr[:, 51:52])
        eng.tensor_add(scr[:, 58:59], scr[:, 54:55], scr[:, 55:56])
        eng.tensor_add(scr[:, 59:60], scr[:, 49:50], scr[:, 53:54])
        eng.tensor_scalar(mean_slot, scr[:, 56:57], 0.25, None, ALU.mult)
        eng.tensor_add(scr[:, 60:61], scr[:, 57:58], scr[:, 58:59])
        eng.tensor_tensor(scr[:, 61:62], mean_slot, mean_slot, ALU.mult)
        eng.scalar_tensor_tensor(scr[:, 62:63], scr[:, 60:61], 256.0,
                                 scr[:, 59:60], ALU.mult, ALU.add)
        spacer(eng)
        eng.tensor_scalar(scr[:, 63:64], scr[:, 62:63], 1.0 / D, None, ALU.mult)
        spacer(eng)
        eng.scalar_tensor_tensor(scr[:, 6:7], scr[:, 63:64], 1.0,
                                 scr[:, 61:62], ALU.bypass, ALU.subtract)
        eng.tensor_scalar(scr[:, 6:7], scr[:, 6:7], LN_EPS, None, ALU.add) if False else None
        spacer(eng)
        eng.tensor_scalar(scr[:, 6:7], scr[:, 6:7], LN_EPS, None, ALU.add)

    with nc.Block() as block:

        @block.sync
        def _(eng):
            for i, vec in enumerate([gq, gk, gv]):
                for ec in range(NEC):
                    eng.dma_start(out=gam[:, i * 8 + ec:i * 8 + ec + 1],
                                  in_=vec[ec * 128:(ec + 1) * 128]).then_inc(dPrm, 16)
            for i, vec in enumerate([bq, bk, bv]):
                for ec in range(NEC):
                    eng.dma_start(out=bet[:, i * 8 + ec:i * 8 + ec + 1],
                                  in_=vec[ec * 128:(ec + 1) * 128]).then_inc(dPrm, 16)
            eng.dma_start(out=bp_row[:, :], in_=bp.ap()).then_inc(dPrm, 16)
            eng.dma_start(out=onesb[:, :], in_=ones_r[:, :]).then_inc(dPrm, 16)
            eng.dma_start(out=clp[0:1, 0:1], in_=clip_x.ap()).then_inc(dPrm, 16)
            eng.dma_start(out=clp[0:1, 1:2], in_=clip_dn.ap()).then_inc(dPrm, 16)
            for t in range(NTT):
                p = t % 2
                if t >= 2:
                    eng.wait_ge(vApp, t - 1)
                eng.dma_start(out=xb[p], in_=x_in[t * 128:(t + 1) * 128, :]).then_inc(dX[p], 16)
            for idx in range(4 * NEC):
                p = idx % 2
                if idx >= 2:
                    eng.wait_ge(vCast, idx - 1)
                W = [Wq, Wk, Wv, Wp][idx // NEC]
                ch = idx % NEC
                eng.dma_start(out=wf_b[p][:, 0:1024],
                              in_=W[ch * 128:(ch + 1) * 128, :]).then_inc(dW[p], 16)
            eng.wait_ge(dWS[0], 16 * 16)
            eng.wait_ge(dWS[1], 16 * 16)
            for br in range(3):
                for dc in range(NEC):
                    for ec in range(NEC):
                        eng.dma_start(
                            out=wt_sb[br][:, (dc * 8 + ec) * 128:(dc * 8 + ec + 1) * 128],
                            in_=wbf_s[br, ec * 128:(ec + 1) * 128, dc * 128:(dc + 1) * 128],
                            transpose=True).then_inc(dWT, 16)
            for ecp in range(NEC):
                eng.dma_start(out=wpT_sb[:, ecp * 1024:(ecp + 1) * 1024],
                              in_=wbf_s[3, 0:D, ecp * 128:(ecp + 1) * 128],
                              transpose=True).then_inc(dWPT, 16)
            if SA < 2:
                return
            eng.wait_ge(dXLN, 16 * NTT)
            for g in range(NTG):
                if g >= 1:
                    eng.wait_ge(tMM, 24 * g)
                for dc in range(NEC):
                    eng.dma_start(out=xlt_b[:, dc * 512:(dc + 1) * 512],
                                  in_=xln_s[g * 512:(g + 1) * 512, dc * 128:(dc + 1) * 128],
                                  transpose=True).then_inc(dXLT, 16)
            if SA < 4:
                return
            eng.wait_ge(dST, 32)
            for g in range(NTG):
                for ec in range(NEC):
                    i = g * NEC + ec
                    p = i % 2
                    if i >= 2:
                        eng.wait_ge(sSig, 3 * (i - 1))
                    eng.dma_start(out=hq_b[p][:, 0:512],
                                  in_=h_s[0, ec * 128:(ec + 1) * 128,
                                          g * 512:(g + 1) * 512]).then_inc(dHL[p], 16)
                    eng.dma_start(out=hk_b[p][:, 0:512],
                                  in_=h_s[1, ec * 128:(ec + 1) * 128,
                                          g * 512:(g + 1) * 512]).then_inc(dHL[p], 16)
                    eng.dma_start(out=hv_b[p][:, 0:512],
                                  in_=h_s[2, ec * 128:(ec + 1) * 128,
                                          g * 512:(g + 1) * 512]).then_inc(dHL[p], 16)
                eng.wait_ge(dVS[0], 16 * 4 * (g + 1))
                eng.wait_ge(dVS[1], 16 * 4 * (g + 1))
                for tci in range(4):
                    tc = g * 4 + tci
                    p = tc % 2
                    if tc >= 2:
                        eng.wait_ge(tMM4, tc - 1)
                    eng.dma_start(out=vt_b[p][:, 0:1024],
                                  in_=v_s[0:D, tc * 128:(tc + 1) * 128],
                                  transpose=True).then_inc(dVT[p], 16)
            if SA < 6:
                return
            for tc in range(NTT):
                p = tc % 2
                if tc >= 2:
                    eng.wait_ge(gXp, tc - 1)
                eng.dma_start(out=xb[p], in_=x_in[tc * 128:(tc + 1) * 128, :]).then_inc(dX6[p], 16)
            if SA < 8:
                return
            eng.wait_ge(vSc2, 1)
            for tc in range(NTT):
                p = tc % 2
                if tc >= 2:
                    eng.wait_ge(vXq, tc - 1)
                eng.dma_start(out=x1_b[p],
                              in_=x1_s[tc * 128:(tc + 1) * 128, :]).then_inc(dX1L[p], 16)
            for idx in range(2 * NIC + NEC):
                if idx < 2 * NIC:
                    ic2, w_i = idx // 2, idx % 2
                    p = w_i
                    if idx >= 2:
                        eng.wait_ge(vWq, idx - 1)
                    Wt_ = w_gate if w_i == 0 else w_up
                    eng.dma_start(out=wf_b[p][:, 0:1024],
                                  in_=Wt_[ic2 * 128:(ic2 + 1) * 128, :]).then_inc(dWF[p], 16)
                else:
                    ch = idx - 2 * NIC
                    p = idx % 2
                    eng.wait_ge(vWq, idx - 1)
                    eng.dma_start(out=wf_b[p][:, 0:2816],
                                  in_=w_down[ch * 128:(ch + 1) * 128, :]).then_inc(dWF[p], 16)
            if SA < 9:
                return
            for g in range(NTG):
                eng.wait_ge(dXQS[0], 16 * min(2 * (g + 1), 16))
                eng.wait_ge(dXQS[1], 16 * min(2 * (g + 1), 16))
                for hc in range(NEC):
                    eng.dma_start(out=xqT_sb[hc][:, g * 512:(g + 1) * 512],
                                  in_=xq_s[g * 512:(g + 1) * 512, hc * 128:(hc + 1) * 128],
                                  transpose=True).then_inc(dXQT, 16)
            for ic in range(NIC):
                p = ic % 2
                eng.wait_ge(dWQS[0], 16 * (ic + 1))
                eng.wait_ge(dWQS[1], 16 * (ic + 1))
                if ic >= 2:
                    eng.wait_ge(tFF, NTG * (ic - 1))
                for hc in range(NEC):
                    eng.dma_start(out=wgT_b[p][:, hc * 128:(hc + 1) * 128],
                                  in_=wgq_s[ic * 128:(ic + 1) * 128,
                                            hc * 128:(hc + 1) * 128],
                                  transpose=True).then_inc(dWGT[p], 16)
                    eng.dma_start(out=wuT_b[p][:, hc * 128:(hc + 1) * 128],
                                  in_=wuq_s[ic * 128:(ic + 1) * 128,
                                            hc * 128:(hc + 1) * 128],
                                  transpose=True).then_inc(dWGT[p], 16)
            if SA < 11:
                return
            eng.wait_ge(tFF, NIC * NTG)
            eng.wait_ge(dWDS, 16 * NEC)
            for ic in range(NIC):
                for oh in range(2):
                    eng.dma_start(out=wdT_sb[:, (ic * 2 + oh) * 512:(ic * 2 + oh + 1) * 512],
                                  in_=wdq_s[oh * 512:(oh + 1) * 512,
                                            ic * 128:(ic + 1) * 128],
                                  transpose=True).then_inc(dWDT, 16)
            eng.wait_ge(dSW, 16 * NEC)
            eng.dma_start(out=swd_row[:, :], in_=swd_d.ap()).then_inc(dSWR, 16)
            eng.wait_ge(vSc4, 1)
            for B in range(NTG + 1):
                if B < NTG:
                    for ic in range(NIC):
                        i = B * NIC + ic
                        p = i % 2
                        if i >= 2:
                            eng.wait_ge(vHq, i - 1)
                        eng.dma_start(out=hfb[p][:, 0:512],
                                      in_=hf_s[ic * 128:(ic + 1) * 128,
                                               B * 512:(B + 1) * 512]).then_inc(dHFL[p], 16)
                if B >= 1:
                    for tci in range(4):
                        tc = (B - 1) * 4 + tci
                        p = tc % 2
                        if tc >= 2:
                            eng.wait_ge(vX2, tc - 1)
                        eng.dma_start(out=x1_b[p],
                                      in_=x1_s[tc * 128:(tc + 1) * 128, :]).then_inc(dX1L[p], 16)
            eng.wait_ge(dOUT[0], 16 * 16)
            eng.wait_ge(dOUT[1], 16 * 16)

        @block.tensor
        def _(eng):
            if SA < 2:
                return
            eng.wait_ge(dWT, 16 * 192)
            for g in range(NTG):
                eng.wait_ge(dXLT, 16 * 8 * (g + 1))
                for br in range(3):
                    for ec in range(NEC):
                        idx = g * 24 + br * 8 + ec
                        bk_ = idx % 2
                        if idx >= 2:
                            eng.wait_ge(sCp[bk_], idx // 2)
                        for dc in range(NEC):
                            mm = eng.matmul(pbank[bk_][:, :],
                                       wt_sb[br][:, (dc * 8 + ec) * 128:(dc * 8 + ec + 1) * 128],
                                       xlt_b[:, dc * 512:(dc + 1) * 512],
                                       start=(dc == 0), stop=(dc == NEC - 1))
                        mm.then_inc(tMM, 1)
            if SA < 4:
                return
            for tc in range(NTT):
                p = tc % 2
                eng.wait_ge(dVT[p], 16 * (tc // 2 + 1))
                for hp in range(8):
                    mm = eng.matmul(pG[:, hp * 128:(hp + 1) * 128],
                               vt_b[p][:, hp * 128:(hp + 1) * 128],
                               vt_b[p][:, hp * 128:(hp + 1) * 128],
                               start=(tc == 0 and hp in (0, 4)),
                               stop=(tc == NTT - 1),
                               skip_group_check=True)
                mm.then_inc(tMM4, 1)
            if SA < 5:
                return
            eng.wait_ge(dWPT, 16 * 8)
            eng.wait_ge(vVV, 1)
            for hp in range(8):
                if hp >= 1:
                    eng.wait_ge(vM5, hp)
                for w in range(2):
                    po = w * 64
                    lhs = vv_sb[po:po + 64, hp * 128 + po:hp * 128 + po + 64]
                    for oh in range(2):
                        mm = eng.matmul(pM[po:po + 64, oh * 512:(oh + 1) * 512],
                                   lhs,
                                   wpT_sb[po:po + 64, hp * 1024 + oh * 512:hp * 1024 + (oh + 1) * 512],
                                   start=True, stop=True,
                                   skip_group_check=True)
                mm.then_inc(tM5, 1)
            eng.wait_ge(dPrm, N_PRM)
            eng.matmul(pG[:, 0:512], onesb[0:1, :], bp_row[0:1, 0:512], start=True, stop=True)
            eng.matmul(pG[:, 512:1024], onesb[0:1, :], bp_row[0:1, 512:1024],
                       start=True, stop=True).then_inc(tBp, 1)
            if SA < 6:
                return
            eng.wait_ge(vM5, 8)
            eng.wait_ge(vQk, 64)
            for tc in range(NTT):
                b0 = (tc % 2) * 2
                if tc >= 2:
                    eng.wait_ge(vX1, tc - 1)
                for oh in range(2):
                    for cc in range(NEC):
                        mm = eng.matmul(pbank[b0 + oh][:, :],
                                   qk_res[cc][:, tc * 128:(tc + 1) * 128],
                                   M_sb[cc][:, oh * 512:(oh + 1) * 512],
                                   start=(cc == 0), stop=(cc == NEC - 1))
                mm.then_inc(tP6, 1)
            if SA < 8:
                return
            eng.wait_ge(vSc, 1)
            eng.matmul(pM[:, 0:2], onesb[0:1, :], rowb[0:1, 128:130],
                       start=True, stop=True).then_inc(tSc, 1)
            if SA < 9:
                return
            eng.wait_ge(vCg, 1)
            for ic in range(NIC):
                p = ic % 2
                eng.wait_ge(dWGT[p], 16 * 16 * (ic // 2 + 1))
                for B in range(NTG):
                    u = ic * NTG + B
                    gbank = pbank[0 + 2 * (u % 2)]
                    ubank = pbank[1 + 2 * (u % 2)]
                    eng.wait_ge(dXQT, 16 * 8 * (B + 1))
                    if u >= 2:
                        eng.wait_ge(sSil, u - 1)
                        eng.wait_ge(vFF, u - 1)
                    for hc in range(NEC):
                        eng.matmul(gbank[:, :], wgT_b[p][:, hc * 128:(hc + 1) * 128],
                                   xqT_sb[hc][:, B * 512:(B + 1) * 512],
                                   start=(hc == 0), stop=(hc == NEC - 1))
                    for hc in range(NEC):
                        mm = eng.matmul(ubank[:, :], wuT_b[p][:, hc * 128:(hc + 1) * 128],
                                   xqT_sb[hc][:, B * 512:(B + 1) * 512],
                                   start=(hc == 0), stop=(hc == NEC - 1))
                    mm.then_inc(tFF, 1)
            if SA < 10:
                return
            eng.wait_ge(vSc3, 1)
            eng.matmul(pM[:, 0:2], onesb[0:1, :], rowb[0:1, 130:132],
                       start=True, stop=True).then_inc(tSc3, 1)
            eng.wait_ge(vSwd, 1)
            eng.wait_ge(dSWR, 16)
            eng.matmul(pM[:, 0:512], onesb[0:1, :], swd_row[0:1, 0:512], start=True, stop=True)
            eng.matmul(pM[:, 512:1024], onesb[0:1, :], swd_row[0:1, 512:1024],
                       start=True, stop=True).then_inc(tSw, 1)
            if SA < 11:
                return
            eng.wait_ge(dWDT, 16 * 44)
            for B in range(NTG):
                eng.wait_ge(vHq, NIC * (B + 1))
                hsb = hint_sb[B % 2]
                for tci in range(4):
                    tc = B * 4 + tci
                    for oh in range(2):
                        grp = tc * 2 + oh
                        bk_ = grp % 4
                        if grp >= 4:
                            eng.wait_ge(vX2, (grp - 4) // 2 + 1)
                        for ic in range(NIC):
                            mm = eng.matmul(pbank[bk_][:, :],
                                       hsb[ic][:, tci * 128:(tci + 1) * 128],
                                       wdT_sb[:, (ic * 2 + oh) * 512:(ic * 2 + oh + 1) * 512],
                                       start=(ic == 0), stop=(ic == NIC - 1))
                    mm.then_inc(tDn, 1)

        @block.vector
        def _(eng):
            eng.memset(scr[:, 28:32], 0.0)
            for t in range(NTT):
                p = t % 2
                eng.wait_ge(dX[p], 16 * (t // 2 + 1))
                emit_ln_stats(eng, xb[p], scr[:, 1:2])
                spacer(eng)
                spacer(eng).then_inc(vVar, 1) if False else None
                eng.tensor_copy(scr[:, 30:31], scr[:, 31:32]).then_inc(vVar, 1)
                eng.wait_ge(sSqrt, t + 1)
                eng.reciprocal(scr[:, 8:9], scr[:, 7:8])
                spacer(eng)
                if t >= 2:
                    eng.wait_ge(dXLN, 16 * (t - 1))
                eng.tensor_scalar(xln_b[p], xb[p], scr[:, 1:2], scr[:, 8:9],
                                  ALU.subtract, ALU.mult).then_inc(vApp, 1)
            for idx in range(4 * NEC):
                p = idx % 2
                eng.wait_ge(dW[p], 16 * (idx // 2 + 1))
                if idx >= 2:
                    eng.wait_ge(dWS[p], 16 * (idx // 2))
                eng.tensor_copy(wb_b[p], wf_b[p][:, 0:1024]).then_inc(vCast, 1)
            if SA < 2:
                return
            for idx in range(NTG * 24):
                s = idx % 2
                g = idx // 24
                r = idx % 24
                br, ec = r // 8, r % 8
                eng.wait_ge(sCp[s], idx // 2 + 1)
                base = (br * 8 + ec) * 64 + g * 8
                eng.bn_stats(bnsl[:, base:base + 6], hb[s][:, 0:512]).then_inc(vBn[s], 1)
            if SA < 3:
                return
            for br in range(3):
                for ec in range(NEC):
                    base = (br * 8 + ec) * 64
                    bme = bass.AP(bnsl, base + 1, [[1536, 128], [8, 8]])
                    bmo = bass.AP(bnsl, base + 4, [[1536, 128], [8, 8]])
                    bce = bass.AP(bnsl, base + 2, [[1536, 128], [8, 8]])
                    bco = bass.AP(bnsl, base + 5, [[1536, 128], [8, 8]])
                    eng.tensor_tensor(scr[:, 16:24], bme, bmo, ALU.add)
                    eng.tensor_tensor(scr[:, 32:40], bce, bco, ALU.add)
                    eng.tensor_tensor(scr[:, 40:48], bme, bme, ALU.mult)
                    eng.tensor_tensor(scr[:, 48:56], bmo, bmo, ALU.mult)
                    eng.reduce_sum(scr[:, 24:25], scr[:, 16:24], axis=AX)
                    eng.reduce_sum(scr[:, 25:26], scr[:, 32:40], axis=AX)
                    eng.tensor_add(scr[:, 56:64], scr[:, 40:48], scr[:, 48:56])
                    col = br * 8 + ec
                    eng.tensor_scalar(stat[:, col:col + 1], scr[:, 24:25], 256.0,
                                      None, ALU.mult)
                    eng.reduce_sum(scr[:, 26:27], scr[:, 56:64], axis=AX)
                    spacer(eng)
                    eng.scalar_tensor_tensor(stat[:, 24 + col:25 + col], scr[:, 26:27], 256.0,
                                             scr[:, 25:26], ALU.mult, ALU.add)
            spacer(eng)
            eng.tensor_copy(scr[:, 30:31], scr[:, 31:32]).then_inc(vSt, 1)
            eng.wait_ge(dST, 32)
            NTOT = float(NC_N * NT)
            eng.tensor_scalar(scr2[:, 0:24], gst[:, 0:24], 1.0 / NTOT, None, ALU.mult)
            eng.tensor_scalar(scr2[:, 24:48], gst[:, 24:48], 1.0 / NTOT, None, ALU.mult)
            eng.tensor_tensor(scr[:, 0:24], scr2[:, 0:24], scr2[:, 0:24], ALU.mult)
            spacer(eng)
            eng.scalar_tensor_tensor(scr2[:, 40:64], scr2[:, 24:48], 1.0,
                                     scr[:, 0:24], ALU.bypass, ALU.subtract)
            spacer(eng)
            eng.tensor_scalar(scr2[:, 40:64], scr2[:, 40:64], BN_EPS, None, ALU.add)
            spacer(eng)
            eng.tensor_copy(scr[:, 30:31], scr[:, 31:32]).then_inc(vSt2, 1)
            eng.wait_ge(sSq3, 1)
            eng.reciprocal(scr[:, 0:24], scr2[:, 40:64])
            spacer(eng)
            eng.scalar_tensor_tensor(abn[:, 0:24], gam[:, 0:24], 4.0,
                                     scr[:, 0:24], ALU.mult, ALU.mult)
            spacer(eng)
            eng.tensor_tensor(scr2[:, 40:64], abn[:, 0:24], scr2[:, 0:24], ALU.mult)
            spacer(eng)
            eng.scalar_tensor_tensor(bbn[:, 0:24], bet[:, 0:24], 4.0,
                                     scr2[:, 40:64], ALU.mult, ALU.subtract)
            spacer(eng)
            eng.tensor_scalar(bbn[:, 0:24], bbn[:, 0:24], 4.0, None, ALU.subtract)
            spacer(eng)
            eng.tensor_copy(scr[:, 30:31], scr[:, 31:32]).then_inc(vAB, 1)
            if SA < 4:
                return
            for i in range(NTG * NEC):
                g, ec = i // NEC, i % NEC
                eng.wait_ge(sSig, 3 * i + 2)
                eng.tensor_tensor(qk_res[ec][:, g * 512:(g + 1) * 512],
                                  qb_b, kb_b, ALU.mult).then_inc(vQk, 1)
            if SA < 5:
                return
            eng.wait_ge(tMM4, NTT)
            eng.tensor_scalar(vv_sb, pG[:, :], 0.125, None, ALU.mult).then_inc(vVV, 1)
            for hp in range(8):
                eng.wait_ge(tM5, hp + 1)
                eng.tensor_copy(M_sb[hp], pM[:, :]).then_inc(vM5, 1)
            eng.wait_ge(tBp, 1)
            eng.tensor_copy(bp_bc[:, :], pG[:, :]).then_inc(vBp, 1)
            if SA < 6:
                return
            for tc in range(NTT):
                p = tc % 2
                b0 = (tc % 2) * 2
                eng.wait_ge(tP6, tc + 1)
                eng.wait_ge(gXp, tc + 1)
                if tc >= 2:
                    eng.wait_ge(dX1S[p], 16 * (tc // 2))
                eng.tensor_tensor(x1_b[p][:, 0:512], pbank[b0][:, :],
                                  xpre_b[p][:, 0:512], ALU.add)
                eng.tensor_tensor(x1_b[p][:, 512:1024], pbank[b0 + 1][:, :],
                                  xpre_b[p][:, 512:1024], ALU.add).then_inc(vX1, 1)
                emit_ln_stats(eng, x1_b[p], ln2m[:, tc:tc + 1])
                spacer(eng)
                eng.tensor_copy(scr[:, 30:31], scr[:, 31:32]).then_inc(vVar6, 1)
                eng.wait_ge(sSqrt6, tc + 1)
                eng.reciprocal(ln2r[:, tc:tc + 1], scr[:, 7:8])
                spacer(eng)
                eng.tensor_scalar(xlnscr, x1_b[p], ln2m[:, tc:tc + 1], ln2r[:, tc:tc + 1],
                                  ALU.subtract, ALU.mult)
                eng.reduce_max(mxsl[:, tc:tc + 1], xlnscr, axis=AX,
                               apply_absolute_value=True).then_inc(vMx, 1)
            spacer(eng)
            eng.reduce_max(mxa[:, 0:1], mxsl[:, 0:NTT], axis=AX)
            spacer(eng)
            eng.tensor_copy(scr[:, 30:31], scr[:, 31:32]).then_inc(vMx, 1)
            if SA < 8:
                return
            eng.wait_ge(dMX, 32)
            eng.reduce_max(rowb[0:1, 132:133], rowb[0:1, 0:128], axis=AX)
            eng.wait_ge(sClp, 2)
            spacer(eng)
            eng.tensor_tensor(rowb[0:1, 133:134], rowb[0:1, 132:133], clp[0:1, 2:3], ALU.mult)
            spacer(eng)
            eng.reciprocal(rowb[0:1, 134:135], rowb[0:1, 133:134])
            eng.tensor_scalar(rowb[0:1, 129:130], rowb[0:1, 133:134], 1.0 / 127.0, None,
                              ALU.mult)
            eng.tensor_scalar(rowb[0:1, 128:129], rowb[0:1, 134:135], 127.0, None, ALU.mult)
            spacer(eng)
            eng.tensor_copy(scr[:, 30:31], scr[:, 31:32]).then_inc(vSc, 1)
            eng.wait_ge(tSc, 1)
            eng.tensor_copy(sc_x[:, 0:2], pM[:, 0:2]).then_inc(vSc2, 1)
            spacer(eng)
            eng.tensor_scalar(scr2[:, 0:NTT], ln2r[:, 0:NTT], sc_x[:, 0:1], None, ALU.mult)
            spacer(eng)
            eng.scalar_tensor_tensor(scr2[:, 32:32 + NTT], ln2m[:, 0:NTT], -1.0,
                                     scr2[:, 0:NTT], ALU.mult, ALU.mult)
            spacer(eng)
            spacer(eng)
            for tc in range(NTT):
                p = tc % 2
                eng.wait_ge(dX1L[p], 16 * (tc // 2 + 1))
                if tc >= 2:
                    eng.wait_ge(dXQS[p], 16 * (tc // 2))
                eng.tensor_scalar(xlnscr, x1_b[p], scr2[:, tc:tc + 1], scr2[:, 32 + tc:33 + tc],
                                  ALU.mult, ALU.add)
                eng.tensor_scalar(xlnscr, xlnscr, RND_C, RND_C, ALU.add, ALU.subtract)
                eng.tensor_scalar(xq_b[p], xlnscr, 127.0, -128.0, ALU.min,
                                  ALU.max).then_inc(vXq, 1)
            for idx in range(2 * NIC + NEC):
                p = idx % 2
                wd_mode = idx >= 2 * NIC
                wid = 2816 if wd_mode else 1024
                buf = wqd_b if wd_mode else wqg_b[p]
                eng.wait_ge(dWF[p], 16 * (idx // 2 + 1))
                if wd_mode:
                    if idx > 2 * NIC:
                        eng.wait_ge(dWDS, 16 * (idx - 2 * NIC))
                elif idx >= 2:
                    eng.wait_ge(dWQS[idx % 2], 16 * (idx // 2))
                eng.reduce_max(scr[:, 12:13], wf_b[p][:, 0:wid], axis=AX,
                               apply_absolute_value=True)
                spacer(eng)
                eng.reciprocal(scr[:, 13:14], scr[:, 12:13])
                if idx < 2 * NIC:
                    dst = (sw_g if idx % 2 == 0 else sw_u)[:, idx // 2:idx // 2 + 1]
                else:
                    dst = sw_d[:, idx - 2 * NIC:idx - 2 * NIC + 1]
                eng.tensor_scalar(dst, scr[:, 12:13], 1.0 / 127.0, None, ALU.mult)
                eng.tensor_scalar(wf_b[p][:, 0:wid], wf_b[p][:, 0:wid],
                                  scr[:, 13:14], 127.0, ALU.mult, ALU.mult)
                eng.tensor_scalar(buf[:, 0:wid], wf_b[p][:, 0:wid],
                                  RND_C, RND_C, ALU.add, ALU.subtract).then_inc(vWq, 1)
            if SA < 9:
                return
            eng.wait_ge(vSc2, 1)
            eng.tensor_scalar(scg[:, :], sw_g[:, :], sc_x[:, 1:2], None, ALU.mult)
            eng.tensor_scalar(scu[:, :], sw_u[:, :], sc_x[:, 1:2], None,
                              ALU.mult).then_inc(vCg, 1)
            for u in range(NIC * NTG):
                ic = u // NTG
                B = u % NTG
                sp = u % 4
                hp_ = u % 2
                eng.wait_ge(sSil, u + 1)
                if u >= 2:
                    eng.wait_ge(dHF[hp_], 16 * (u // 2))
                ubank = pbank[1 + 2 * (u % 2)]
                eng.scalar_tensor_tensor(hfb[hp_][:, 0:512], ubank[:, :],
                                         scu[:, ic:ic + 1], sgf[sp][:, 0:512],
                                         ALU.mult, ALU.mult)
                eng.reduce_max(mxsl[:, u:u + 1], hfb[hp_][:, 0:512], axis=AX,
                               apply_absolute_value=True).then_inc(vFF, 1)
            spacer(eng)
            eng.reduce_max(mxa[:, 1:2], mxsl[:, 0:NIC * NTG], axis=AX)
            spacer(eng)
            eng.tensor_copy(scr[:, 30:31], scr[:, 31:32]).then_inc(vFF, 1)
            if SA < 10:
                return
            eng.wait_ge(dMX, 64)
            eng.reduce_max(rowb[0:1, 135:136], rowb[0:1, 0:128], axis=AX)
            spacer(eng)
            eng.tensor_tensor(rowb[0:1, 136:137], rowb[0:1, 135:136], clp[0:1, 3:4], ALU.mult)
            spacer(eng)
            eng.reciprocal(rowb[0:1, 137:138], rowb[0:1, 136:137])
            eng.tensor_scalar(rowb[0:1, 131:132], rowb[0:1, 136:137], 1.0 / 127.0, None,
                              ALU.mult)
            eng.tensor_scalar(rowb[0:1, 130:131], rowb[0:1, 137:138], 127.0, None, ALU.mult)
            spacer(eng)
            eng.tensor_copy(scr[:, 30:31], scr[:, 31:32]).then_inc(vSc3, 1)
            eng.wait_ge(tSc3, 1)
            eng.tensor_copy(sc_x[:, 2:4], pM[:, 0:2])
            spacer(eng)
            eng.tensor_scalar(scr[:, 16:24], sw_d[:, :], sc_x[:, 3:4], None, ALU.mult)
            spacer(eng)
            eng.tensor_copy(scr[:, 30:31], scr[:, 31:32]).then_inc(vSc4, 1)
            eng.tensor_copy(scr[:, 31:32], scr[:, 30:31]).then_inc(vSwd, 1)
            eng.wait_ge(tSw, 1)
            eng.tensor_copy(swd_bc[:, :], pM[:, :]).then_inc(vSw, 1)
            if SA < 11:
                return
            eng.wait_ge(vSw, 1)
            for B in range(NTG + 1):
                if B < NTG:
                    hsb = hint_sb[B % 2]
                    if B >= 2:
                        eng.wait_ge(tDn, 4 * (B - 1))
                    for ic in range(NIC):
                        i = B * NIC + ic
                        p = i % 2
                        eng.wait_ge(dHFL[p], 16 * (i // 2 + 1))
                        eng.tensor_scalar(xlnscr[:, 0:512], hfb[p][:, 0:512],
                                          sc_x[:, 2:3], RND_C, ALU.mult, ALU.add)
                        eng.tensor_scalar(xlnscr[:, 0:512], xlnscr[:, 0:512],
                                          RND_C, 127.0, ALU.subtract, ALU.min)
                        eng.tensor_scalar(hsb[ic][:, 0:512], xlnscr[:, 0:512],
                                          -128.0, None, ALU.max).then_inc(vHq, 1)
                if B >= 1:
                    for tci in range(4):
                        tc = (B - 1) * 4 + tci
                        p = tc % 2
                        eng.wait_ge(tDn, tc + 1)
                        eng.wait_ge(dX1L[p], 16 * (16 + tc // 2 + 1))
                        if tc >= 2:
                            eng.wait_ge(dOUT[p], 16 * (tc // 2))
                        for oh in range(2):
                            grp = tc * 2 + oh
                            bk_ = grp % 4
                            eng.tensor_tensor(xpre_b[p][:, oh * 512:(oh + 1) * 512],
                                              pbank[bk_][:, :],
                                              swd_bc[:, oh * 512:(oh + 1) * 512], ALU.mult)
                        eng.tensor_add(xpre_b[p], xpre_b[p], x1_b[p]).then_inc(vX2, 1)

        @block.scalar
        def _(eng):
            for t in range(NTT):
                eng.wait_ge(vVar, t + 1)
                eng.activation(scr[:, 7:8], scr[:, 6:7], AF.Sqrt)
                eng.copy(scr[:, 28:29], scr[:, 29:30]).then_inc(sSqrt, 1)
            if SA < 2:
                return
            for idx in range(NTG * 24):
                s = idx % 2
                eng.wait_ge(tMM, idx + 1)
                if idx >= 2:
                    eng.wait_ge(vBn[s], idx // 2)
                    eng.wait_ge(dH[s], 16 * (idx // 2))
                eng.copy(hb[s][:, 0:512], pbank[s][:, :]).then_inc(sCp[s], 1)
            if SA < 3:
                return
            eng.wait_ge(vSt2, 1)
            eng.activation(scr2[:, 40:64], scr2[:, 40:64], AF.Sqrt)
            eng.copy(scr[:, 28:29], scr[:, 29:30]).then_inc(sSq3, 1)
            if SA < 4:
                return
            eng.wait_ge(vAB, 1)
            for i in range(NTG * NEC):
                ec = i % NEC
                p = i % 2
                eng.wait_ge(dHL[p], 48 * (i // 2 + 1))
                if i >= 1:
                    eng.wait_ge(vQk, i)
                eng.activation(qb_b, hq_b[p][:, 0:512], AF.Sigmoid,
                               bias=bbn[:, ec:ec + 1], scale=abn[:, ec:ec + 1])
                eng.activation(kb_b, hk_b[p][:, 0:512], AF.Sigmoid,
                               bias=bbn[:, 8 + ec:8 + ec + 1],
                               scale=abn[:, 8 + ec:8 + ec + 1])
                if i >= 2:
                    eng.wait_ge(dVS[p], 16 * (i // 2))
                eng.activation(vb_b[p], hv_b[p][:, 0:512], AF.Sigmoid,
                               bias=bbn[:, 16 + ec:16 + ec + 1],
                               scale=abn[:, 16 + ec:16 + ec + 1]).then_inc(sSig, 3)
            if SA < 6:
                return
            for tc in range(NTT):
                eng.wait_ge(vVar6, tc + 1)
                eng.activation(scr[:, 7:8], scr[:, 6:7], AF.Sqrt)
                eng.copy(scr[:, 28:29], scr[:, 29:30]).then_inc(sSqrt6, 1)
            if SA < 8:
                return
            eng.wait_ge(dPrm, N_PRM)
            eng.activation(clp[0:1, 2:3], clp[0:1, 0:1], AF.Sigmoid)
            eng.activation(clp[0:1, 3:4], clp[0:1, 1:2], AF.Sigmoid)
            eng.copy(scr[:, 28:29], scr[:, 29:30]).then_inc(sClp, 2)
            if SA < 9:
                return
            eng.wait_ge(vCg, 1)
            for u in range(NIC * NTG):
                ic = u // NTG
                sp = u % 4
                eng.wait_ge(tFF, u + 1)
                if u >= 4:
                    eng.wait_ge(vFF, u - 3)
                gbank = pbank[0 + 2 * (u % 2)]
                eng.activation(sgf[sp][:, 0:512], gbank[:, :], AF.Silu,
                               scale=scg[:, ic:ic + 1]).then_inc(sSil, 1)

        @block.gpsimd
        def _(eng):
            for t in range(NTT):
                p = t % 2
                eng.wait_ge(vApp, t + 1)
                eng.dma_start(out=xln_s[t * 128:(t + 1) * 128, :],
                              in_=xln_b[p]).then_inc(dXLN, 16)
            for idx in range(4 * NEC):
                p = idx % 2
                eng.wait_ge(vCast, idx + 1)
                w_i, ch = idx // NEC, idx % NEC
                eng.dma_start(out=wbf_s[w_i, ch * 128:(ch + 1) * 128, :],
                              in_=wb_b[p]).then_inc(dWS[p], 16)
            if SA < 2:
                return
            for idx in range(NTG * 24):
                s = idx % 2
                g, r = idx // 24, idx % 24
                br, ec = r // 8, r % 8
                eng.wait_ge(sCp[s], idx // 2 + 1)
                eng.dma_start(out=h_s[br, ec * 128:(ec + 1) * 128, g * 512:(g + 1) * 512],
                              in_=hb[s][:, 0:512]).then_inc(dH[s], 16)
            if SA < 3:
                return
            eng.wait_ge(vSt, 1)
            eng.dma_start(out=st_i[:, :], in_=stat[:, 0:48]).then_inc(dST, 16)
            eng.wait_ge(dST, 16)
            eng.collective_compute("AllReduce", ALU.add,
                                   replica_groups=[list(range(NC_N))],
                                   ins=[st_i.ap()], outs=[st_o.ap()]).then_inc(cCC, 1)
            eng.wait_ge(cCC, 1)
            eng.dma_start(out=gst[:, 0:48], in_=st_o[:, :]).then_inc(dST, 16)
            if SA < 4:
                return
            for i in range(NTG * NEC):
                g, ec = i // NEC, i % NEC
                p = i % 2
                eng.wait_ge(sSig, 3 * (i + 1))
                eng.dma_start(out=v_s[ec * 128:(ec + 1) * 128, g * 512:(g + 1) * 512],
                              in_=vb_b[p][:, 0:512]).then_inc(dVS[p], 16)
            if SA < 6:
                return
            eng.wait_ge(vBp, 1)
            for tc in range(NTT):
                p = tc % 2
                eng.wait_ge(dX6[p], 16 * (tc // 2 + 1))
                if tc >= 2:
                    eng.wait_ge(vX1, tc - 1)
                eng.tensor_tensor(xpre_b[p], xb[p], bp_bc[:, :], ALU.add).then_inc(gXp, 1)
                eng.wait_ge(vX1, tc + 1)
                eng.wait_ge(vVar6, tc + 1)
                eng.dma_start(out=x1_s[tc * 128:(tc + 1) * 128, :],
                              in_=x1_b[p]).then_inc(dX1S[p], 16)
            if SA < 8:
                return
            eng.wait_ge(vMx, NTT + 1)
            eng.dma_start(out=mx_i.ap(), in_=mxa[:, 0:1]).then_inc(dMX, 16)
            eng.wait_ge(dMX, 16)
            eng.collective_compute("AllReduce", ALU.max,
                                   replica_groups=[list(range(NC_N))],
                                   ins=[mx_i.ap()], outs=[mx_o.ap()]).then_inc(cCC, 1)
            eng.wait_ge(cCC, 2)
            eng.dma_start(out=rowb[0:1, 0:128], in_=mx_o.ap()).then_inc(dMX, 16)
            for tc in range(NTT):
                p = tc % 2
                eng.wait_ge(vXq, tc + 1)
                eng.dma_start(out=xq_s[tc * 128:(tc + 1) * 128, :],
                              in_=xq_b[p]).then_inc(dXQS[p], 16)
            for idx in range(2 * NIC + NEC):
                p = idx % 2
                eng.wait_ge(vWq, idx + 1)
                if idx < 2 * NIC:
                    ic2 = idx // 2
                    tgt = wgq_s if idx % 2 == 0 else wuq_s
                    eng.dma_start(out=tgt[ic2 * 128:(ic2 + 1) * 128, :],
                                  in_=wqg_b[p][:, 0:1024]).then_inc(dWQS[p], 16)
                elif False:
                    pass
                else:
                    ch = idx - 2 * NIC
                    eng.dma_start(out=wdq_s[ch * 128:(ch + 1) * 128, :],
                                  in_=wqd_b[:, 0:2816]).then_inc(dWDS, 16)
            if SA < 9:
                return
            for u in range(NIC * NTG):
                ic, B = u // NTG, u % NTG
                hp_ = u % 2
                eng.wait_ge(vFF, u + 1)
                eng.dma_start(out=hf_s[ic * 128:(ic + 1) * 128, B * 512:(B + 1) * 512],
                              in_=hfb[hp_][:, 0:512]).then_inc(dHF[hp_], 16)
            if SA < 10:
                return
            eng.wait_ge(vFF, NIC * NTG + 1)
            eng.dma_start(out=mh_i.ap(), in_=mxa[:, 1:2]).then_inc(dMX, 16)
            eng.wait_ge(dMX, 48)
            eng.collective_compute("AllReduce", ALU.max,
                                   replica_groups=[list(range(NC_N))],
                                   ins=[mh_i.ap()], outs=[mh_o.ap()]).then_inc(cCC, 1)
            eng.wait_ge(cCC, 3)
            eng.dma_start(out=rowb[0:1, 0:128], in_=mh_o.ap()).then_inc(dMX, 16)
            eng.wait_ge(vSwd, 1)
            for hc in range(NEC):
                eng.dma_start(out=swd_d[hc * 128:(hc + 1) * 128],
                              in_=scr[:, 16 + hc:17 + hc]).then_inc(dSW, 16)
            if SA < 11:
                return
            for tc in range(NTT):
                p = tc % 2
                eng.wait_ge(vX2, tc + 1)
                eng.dma_start(out=out_t[tc * 128:(tc + 1) * 128, :],
                              in_=xpre_b[p]).then_inc(dOUT[p], 16)

    es.close()
    return nc


_NC_CACHE = {}


def _get_nc():
    if "nc" not in _NC_CACHE:
        _NC_CACHE["nc"] = build_nc()
    return _NC_CACHE["nc"]


def kernel(L0_attn_x, L0_attn_Wq, L0_attn_gq, L0_attn_bq, L0_attn_Wk, L0_attn_gk,
           L0_attn_bk, L0_attn_Wv, L0_attn_gv, L0_attn_bv, L0_attn_Wp, L0_attn_bp,
           L0_ffn_clip_x, L0_ffn_clip_down, L0_ffn_w_gate, L0_ffn_w_up, L0_ffn_w_down):
    nc = _get_nc()
    f32 = np.float32
    common = {
        "Wq": np.ascontiguousarray(L0_attn_Wq, f32),
        "Wk": np.ascontiguousarray(L0_attn_Wk, f32),
        "Wv": np.ascontiguousarray(L0_attn_Wv, f32),
        "Wp": np.ascontiguousarray(L0_attn_Wp, f32),
        "gq": np.ascontiguousarray(L0_attn_gq, f32),
        "bq": np.ascontiguousarray(L0_attn_bq, f32),
        "gk": np.ascontiguousarray(L0_attn_gk, f32),
        "bk": np.ascontiguousarray(L0_attn_bk, f32),
        "gv": np.ascontiguousarray(L0_attn_gv, f32),
        "bv": np.ascontiguousarray(L0_attn_bv, f32),
        "bp": np.ascontiguousarray(L0_attn_bp, f32),
        "clip_x": np.ascontiguousarray(L0_ffn_clip_x, f32),
        "clip_dn": np.ascontiguousarray(L0_ffn_clip_down, f32),
        "w_gate": np.ascontiguousarray(L0_ffn_w_gate, f32),
        "w_up": np.ascontiguousarray(L0_ffn_w_up, f32),
        "w_down": np.ascontiguousarray(L0_ffn_w_down, f32),
        "ones_r": np.ones((1, 128), f32),
    }
    xs = np.ascontiguousarray(L0_attn_x, f32)
    in_maps = [dict(common, x=xs[b]) for b in range(NC_N)]
    res = run_bass_kernel_spmd(nc, in_maps, core_ids=list(range(NC_N)))
    return np.stack([np.asarray(r["out"]) for r in res.results], axis=0)


# version 07
# speedup vs baseline: 1.0643x; 1.0643x over previous
import numpy as np
import ml_dtypes
from contextlib import ExitStack

import concourse.bass as bass
import concourse.mybir as mybir
from concourse.bass_utils import run_bass_kernel_spmd

F32 = mybir.dt.float32
BF16 = mybir.dt.bfloat16
ALU = mybir.AluOpType
AF = mybir.ActivationFunctionType
AX = mybir.AxisListType.X

RND_C = float(1.5 * 2.0 ** 23)
D = 1024
NT = 4096
DI = 2816
NEC = 8
NIC = 22
NTT = 32
NTG = 8
LN_EPS = 1e-5
BN_EPS = 1e-5
NC_N = 8


class Arena:
    def __init__(self, tensor, cols):
        self.t, self.cols, self.off, self.marks = tensor, cols, 0, []

    def alloc(self, n):
        n = (n + 15) // 16 * 16
        assert self.off + n <= self.cols, f"arena overflow {self.off}+{n}>{self.cols}"
        o = self.off
        self.off += n
        return self.t[:, o:o + n]

    def push(self):
        self.marks.append(self.off)

    def pop(self):
        self.off = self.marks.pop()


def build_nc(stop_after=99, debug=False):
    nc = bass.Bass(target_bir_lowering=False, detect_race_conditions=False)
    x_in = nc.dram_tensor("x", [NT, D], F32, kind="ExternalInput")
    Wq = nc.dram_tensor("Wq", [D, D], F32, kind="ExternalInput")
    Wk = nc.dram_tensor("Wk", [D, D], F32, kind="ExternalInput")
    Wv = nc.dram_tensor("Wv", [D, D], F32, kind="ExternalInput")
    Wp = nc.dram_tensor("Wp", [D, D], F32, kind="ExternalInput")
    gq = nc.dram_tensor("gq", [D], F32, kind="ExternalInput")
    bq = nc.dram_tensor("bq", [D], F32, kind="ExternalInput")
    gk = nc.dram_tensor("gk", [D], F32, kind="ExternalInput")
    bk = nc.dram_tensor("bk", [D], F32, kind="ExternalInput")
    gv = nc.dram_tensor("gv", [D], F32, kind="ExternalInput")
    bv = nc.dram_tensor("bv", [D], F32, kind="ExternalInput")
    bp = nc.dram_tensor("bp", [D], F32, kind="ExternalInput")
    clip_x = nc.dram_tensor("clip_x", [1], F32, kind="ExternalInput")
    clip_dn = nc.dram_tensor("clip_dn", [1], F32, kind="ExternalInput")
    w_gate = nc.dram_tensor("w_gate", [DI, D], F32, kind="ExternalInput")
    w_up = nc.dram_tensor("w_up", [DI, D], F32, kind="ExternalInput")
    w_down = nc.dram_tensor("w_down", [D, DI], F32, kind="ExternalInput")
    ones_r = nc.dram_tensor("ones_r", [1, 128], F32, kind="ExternalInput")
    out_t = nc.dram_tensor("out", [NT, D], F32, kind="ExternalOutput")

    sk = "ExternalOutput" if debug else "Internal"
    xln_s = nc.dram_tensor("xln_s", [NT, D], BF16, kind=sk)
    h_s = nc.dram_tensor("h_s", [3, D, NT], F32, kind=sk)
    v_s = nc.dram_tensor("v_s", [D, NT], BF16, kind="Internal")
    x1_s = nc.dram_tensor("x1_s", [NT, D], F32, kind=sk)
    xq_s = nc.dram_tensor("xq_s", [NT, D], BF16, kind=sk)
    hf_s = nc.dram_tensor("hf_s", [DI, NT], F32, kind=sk)
    wbf_s = nc.dram_tensor("wbf_s", [4, D, D], BF16, kind="Internal")
    wgq_s = nc.dram_tensor("wgq_s", [DI, D], BF16, kind=sk)
    wuq_s = nc.dram_tensor("wuq_s", [DI, D], BF16, kind="Internal")
    wdq_s = nc.dram_tensor("wdq_s", [D, DI], BF16, kind="Internal")
    st_i = nc.dram_tensor("st_i", [128, 48], F32, kind="Internal")
    st_o = nc.dram_tensor("st_o", [128, 48], F32, kind="Internal")
    mx_i = nc.dram_tensor("mx_i", [128], F32, kind="Internal")
    mx_o = nc.dram_tensor("mx_o", [128], F32, kind="Internal")
    mh_i = nc.dram_tensor("mh_i", [128], F32, kind="Internal")
    mh_o = nc.dram_tensor("mh_o", [128], F32, kind="Internal")
    swd_d = nc.dram_tensor("swd_d", [D], F32, kind="Internal")

    es = ExitStack()
    sbuf = lambda name, shape, dt: es.enter_context(nc.sbuf_tensor(name, shape, dt))
    psum = lambda name, shape: es.enter_context(nc.psum_tensor(name, shape, F32))
    sem = lambda name: es.enter_context(nc.semaphore(name))
    sem2 = lambda name: [es.enter_context(nc.semaphore(name + "0")),
                         es.enter_context(nc.semaphore(name + "1"))]

    A16 = Arena(sbuf("A16", [128, 61440], BF16), 61440)
    A32 = Arena(sbuf("A32", [128, 13312], F32), 13312)

    gam = sbuf("gam", [128, 24], F32)
    bet = sbuf("bet", [128, 24], F32)
    abn = sbuf("abn", [128, 24], F32)
    bbn = sbuf("bbn", [128, 24], F32)
    bnsl = sbuf("bnsl", [128, 24 * 64], F32)
    stat = sbuf("stat", [128, 64], F32)
    gst = sbuf("gst", [128, 64], F32)
    scr = sbuf("scr", [128, 64], F32)
    scr2 = sbuf("scr2", [128, 64], F32)
    ln2m = sbuf("ln2m", [128, NTT], F32)
    ln2r = sbuf("ln2r", [128, NTT], F32)
    mxa = sbuf("mxa", [128, 4], F32)
    onesb = sbuf("onesb", [1, 128], F32)
    rowb = sbuf("rowb", [1, 144], F32)
    sc_x = sbuf("sc_x", [128, 4], F32)
    bp_row = sbuf("bp_row", [1, D], F32)
    bp_bc = sbuf("bp_bc", [128, D], F32)
    swd_row = sbuf("swd_row", [1, D], F32)
    swd_bc = sbuf("swd_bc", [128, D], F32)
    sw_g = sbuf("sw_g", [128, NIC], F32)
    sw_u = sbuf("sw_u", [128, NIC], F32)
    sw_d = sbuf("sw_d", [128, NEC], F32)
    scg = sbuf("scg", [128, NIC], F32)
    scu = sbuf("scu", [128, NIC], F32)
    clp = sbuf("clp", [1, 16], F32)
    mxsl = sbuf("mxsl", [128, 192], F32)

    pA = psum("pA", [128, 512])
    pB = psum("pB", [128, 512])
    pC = psum("pC", [128, 512])
    pD = psum("pD", [128, 512])
    pG = psum("pG", [128, 1024])
    pM = psum("pM", [128, 1024])
    pbank = [pA, pB, pC, pD]

    dX = sem2("dX")
    dW = sem2("dW")
    dHL = sem2("dHL")
    dVT = sem2("dVT")
    dX1S = sem2("dX1S")
    dX1L = sem2("dX1L")
    dXQS = sem2("dXQS")
    dWF = sem2("dWF")
    dWGT = sem2("dWGT")
    dHF = sem2("dHF")
    dOUT = sem2("dOUT")
    dVS = sem2("dVS")
    dWS = sem2("dWS");  dWQS = sem2("dWQS")
    dPrm = sem("dPrm");  dWT = sem("dWT");  dWPT = sem("dWPT")
    dXLN = sem("dXLN");  dXLT = sem("dXLT");  dST = sem("dST");  dXQT = sem("dXQT")
    dWDS = sem("dWDS");  dWDT = sem("dWDT");  dMX = sem("dMX")
    dSW = sem("dSW");  dSWR = sem("dSWR")
    cCC = sem("cCC")
    vVar = sem("vVar");  sSqrt = sem("sSqrt");  vApp = sem("vApp")
    vCast = sem("vCast")
    tMM = sem("tMM");  _unused = None
    sCp4 = [sem(f"sCp4_{i}") for i in range(4)]
    vBn4 = [sem(f"vBn4_{i}") for i in range(4)]
    dH4 = [sem(f"dH4_{i}") for i in range(4)]
    dHFL6 = [sem(f"dHFL6_{i}") for i in range(6)]
    vSt = sem("vSt");  vSt2 = sem("vSt2");  sSq3 = sem("sSq3");  vAB = sem("vAB")
    sSig = sem("sSig");  vQk = sem("vQk");  tMM4 = sem("tMM4")
    vVV = sem("vVV");  tM5 = sem("tM5");  vM5 = sem("vM5");  tBp = sem("tBp");  vBp = sem("vBp")
    gXp = sem("gXp");  tP6 = sem("tP6");  vX1 = sem("vX1")
    vMx = sem("vMx")
    sClp = sem("sClp");  vSc = sem("vSc");  tSc = sem("tSc");  vSc2 = sem("vSc2")
    vXq = sem("vXq");  vWq = sem("vWq");  vCg = sem("vCg")
    tFF = sem("tFF");  sSil = sem("sSil");  vFF = sem("vFF")
    vSc4 = sem("vSc4");  vSwd = sem("vSwd")
    tSw = sem("tSw");  vSw = sem("vSw")
    vHq = sem("vHq");  tDn = sem("tDn");  vX2 = sem("vX2")
    dX6 = dX
    vVar6 = vVar
    sSqrt6 = sSqrt
    vSc3 = vSc
    tSc3 = tSc

    xb = [A32.alloc(1024), A32.alloc(1024)]
    sqscr = A32.alloc(1024)
    _r0 = A32.off
    hb = [A32.alloc(512), A32.alloc(512)]
    hq_b = [A32.alloc(512), A32.alloc(512)]
    hk_b = [A32.alloc(512), A32.alloc(512)]
    hv_b = [A32.alloc(512), A32.alloc(512)]
    xpre_b = [A32.alloc(1024), A32.alloc(1024)]
    assert A32.off - _r0 >= 2 * 2816
    wf_b = [A32.t[:, _r0:_r0 + 2816], A32.t[:, _r0 + 2816:_r0 + 5632]]
    x1_b = [A32.alloc(1024), A32.alloc(1024)]
    xlnscr = A32.alloc(1024)
    sgf = [hq_b[0], hq_b[1], hk_b[0], hk_b[1]]
    hb4 = [hb[0], hb[1], hq_b[0], hq_b[1]]
    hfb6 = [hb[0], hb[1], hq_b[0], hq_b[1], hk_b[0], hk_b[1]]
    hfb = [hv_b[0], hv_b[1]]

    wpT_sb = A16.alloc(8 * 1024)
    A16.push()
    wt_sb = [A16.alloc(64 * 128) for _ in range(3)]
    xlt_b = A16.alloc(8 * 512)
    xln_b = [A16.alloc(1024), A16.alloc(1024)]
    wb_b = [A16.alloc(1024), A16.alloc(1024)]
    A16.pop()
    A16.push()
    qk_res = [A16.alloc(4096) for _ in range(8)]
    qb_b = A16.alloc(512)
    kb_b = A16.alloc(512)
    vb_b = [A16.alloc(512), A16.alloc(512)]
    vt_b = [A16.alloc(1024), A16.alloc(1024)]
    vv_sb = A16.alloc(1024)
    M_sb = [A16.alloc(1024) for _ in range(8)]
    xq_b = [A16.alloc(1024), A16.alloc(1024)]
    wqg_b = [A16.alloc(1024), A16.alloc(1024)]
    wqd_b = A16.alloc(2816)
    A16.pop()
    A16.push()
    xqT_sb = [A16.alloc(4096) for _ in range(8)]
    wgT_b = [A16.alloc(8 * 128), A16.alloc(8 * 128)]
    wuT_b = [A16.alloc(8 * 128), A16.alloc(8 * 128)]
    A16.pop()
    A16.push()
    wdT_sb = A16.alloc(44 * 512)
    hint_sb = [[A16.alloc(512) for _ in range(NIC)], [A16.alloc(512) for _ in range(NIC)]]
    A16.pop()

    SA = stop_after
    N_PRM = 16 * (6 * NEC + 4)


    def spacer(eng):
        eng.tensor_copy(scr[:, 30:31], scr[:, 31:32])

    def emit_ln_stats(eng, xsrc, mean_slot):
        eng.bn_stats(scr[:, 32:38], xsrc[:, 0:512])
        eng.bn_stats(scr[:, 40:46], xsrc[:, 512:1024])
        eng.tensor_add(scr[:, 48:49], scr[:, 33:34], scr[:, 36:37])
        eng.tensor_add(scr[:, 49:50], scr[:, 34:35], scr[:, 37:38])
        eng.tensor_tensor(scr[:, 50:51], scr[:, 33:34], scr[:, 33:34], ALU.mult)
        eng.tensor_tensor(scr[:, 51:52], scr[:, 36:37], scr[:, 36:37], ALU.mult)
        eng.tensor_add(scr[:, 52:53], scr[:, 41:42], scr[:, 44:45])
        eng.tensor_add(scr[:, 53:54], scr[:, 42:43], scr[:, 45:46])
        eng.tensor_tensor(scr[:, 54:55], scr[:, 41:42], scr[:, 41:42], ALU.mult)
        eng.tensor_tensor(scr[:, 55:56], scr[:, 44:45], scr[:, 44:45], ALU.mult)
        eng.tensor_add(scr[:, 56:57], scr[:, 48:49], scr[:, 52:53])
        eng.tensor_add(scr[:, 57:58], scr[:, 50:51], scr[:, 51:52])
        eng.tensor_add(scr[:, 58:59], scr[:, 54:55], scr[:, 55:56])
        eng.tensor_add(scr[:, 59:60], scr[:, 49:50], scr[:, 53:54])
        eng.tensor_scalar(mean_slot, scr[:, 56:57], 0.25, None, ALU.mult)
        eng.tensor_add(scr[:, 60:61], scr[:, 57:58], scr[:, 58:59])
        eng.tensor_tensor(scr[:, 61:62], mean_slot, mean_slot, ALU.mult)
        eng.scalar_tensor_tensor(scr[:, 62:63], scr[:, 60:61], 256.0,
                                 scr[:, 59:60], ALU.mult, ALU.add)
        spacer(eng)
        eng.tensor_scalar(scr[:, 63:64], scr[:, 62:63], 1.0 / D, None, ALU.mult)
        spacer(eng)
        eng.scalar_tensor_tensor(scr[:, 6:7], scr[:, 63:64], 1.0,
                                 scr[:, 61:62], ALU.bypass, ALU.subtract)
        eng.tensor_scalar(scr[:, 6:7], scr[:, 6:7], LN_EPS, None, ALU.add) if False else None
        spacer(eng)
        eng.tensor_scalar(scr[:, 6:7], scr[:, 6:7], LN_EPS, None, ALU.add)

    with nc.Block() as block:

        @block.sync
        def _(eng):
            for i, vec in enumerate([gq, gk, gv]):
                for ec in range(NEC):
                    eng.dma_start(out=gam[:, i * 8 + ec:i * 8 + ec + 1],
                                  in_=vec[ec * 128:(ec + 1) * 128]).then_inc(dPrm, 16)
            for i, vec in enumerate([bq, bk, bv]):
                for ec in range(NEC):
                    eng.dma_start(out=bet[:, i * 8 + ec:i * 8 + ec + 1],
                                  in_=vec[ec * 128:(ec + 1) * 128]).then_inc(dPrm, 16)
            eng.dma_start(out=bp_row[:, :], in_=bp.ap()).then_inc(dPrm, 16)
            eng.dma_start(out=onesb[:, :], in_=ones_r[:, :]).then_inc(dPrm, 16)
            eng.dma_start(out=clp[0:1, 0:1], in_=clip_x.ap()).then_inc(dPrm, 16)
            eng.dma_start(out=clp[0:1, 1:2], in_=clip_dn.ap()).then_inc(dPrm, 16)
            for t in range(NTT):
                p = t % 2
                if t >= 2:
                    eng.wait_ge(vApp, t - 1)
                eng.dma_start(out=xb[p], in_=x_in[t * 128:(t + 1) * 128, :]).then_inc(dX[p], 16)
            for idx in range(4 * NEC):
                p = idx % 2
                if idx >= 2:
                    eng.wait_ge(vCast, idx - 1)
                W = [Wq, Wk, Wv, Wp][idx // NEC]
                ch = idx % NEC
                eng.dma_start(out=wf_b[p][:, 0:1024],
                              in_=W[ch * 128:(ch + 1) * 128, :]).then_inc(dW[p], 16)
            eng.wait_ge(dWS[0], 16 * 16)
            eng.wait_ge(dWS[1], 16 * 16)
            for br in range(3):
                for dc in range(NEC):
                    for ec in range(NEC):
                        eng.dma_start(
                            out=wt_sb[br][:, (dc * 8 + ec) * 128:(dc * 8 + ec + 1) * 128],
                            in_=wbf_s[br, ec * 128:(ec + 1) * 128, dc * 128:(dc + 1) * 128],
                            transpose=True).then_inc(dWT, 16)
            for ecp in range(NEC):
                eng.dma_start(out=wpT_sb[:, ecp * 1024:(ecp + 1) * 1024],
                              in_=wbf_s[3, 0:D, ecp * 128:(ecp + 1) * 128],
                              transpose=True).then_inc(dWPT, 16)
            if SA < 2:
                return
            eng.wait_ge(dXLN, 16 * NTT)
            for g in range(NTG):
                if g >= 1:
                    eng.wait_ge(tMM, 24 * g)
                for dc in range(NEC):
                    eng.dma_start(out=xlt_b[:, dc * 512:(dc + 1) * 512],
                                  in_=xln_s[g * 512:(g + 1) * 512, dc * 128:(dc + 1) * 128],
                                  transpose=True).then_inc(dXLT, 16)
            if SA < 4:
                return
            eng.wait_ge(vBn4[2], 48)
            eng.wait_ge(vBn4[3], 48)
            eng.wait_ge(dH4[2], 16 * 48)
            eng.wait_ge(dH4[3], 16 * 48)
            for g in range(NTG):
                for ec in range(NEC):
                    i = g * NEC + ec
                    p = i % 2
                    if i >= 2:
                        eng.wait_ge(sSig, 3 * (i - 1))
                    eng.dma_start(out=hq_b[p][:, 0:512],
                                  in_=h_s[0, ec * 128:(ec + 1) * 128,
                                          g * 512:(g + 1) * 512]).then_inc(dHL[p], 16)
                    eng.dma_start(out=hk_b[p][:, 0:512],
                                  in_=h_s[1, ec * 128:(ec + 1) * 128,
                                          g * 512:(g + 1) * 512]).then_inc(dHL[p], 16)
                    eng.dma_start(out=hv_b[p][:, 0:512],
                                  in_=h_s[2, ec * 128:(ec + 1) * 128,
                                          g * 512:(g + 1) * 512]).then_inc(dHL[p], 16)
                eng.wait_ge(dVS[0], 16 * 4 * (g + 1))
                eng.wait_ge(dVS[1], 16 * 4 * (g + 1))
                for tci in range(4):
                    tc = g * 4 + tci
                    p = tc % 2
                    if tc >= 2:
                        eng.wait_ge(tMM4, tc - 1)
                    eng.dma_start(out=vt_b[p][:, 0:1024],
                                  in_=v_s[0:D, tc * 128:(tc + 1) * 128],
                                  transpose=True).then_inc(dVT[p], 16)
            if SA < 6:
                return
            for tc in range(NTT):
                p = tc % 2
                if tc >= 2:
                    eng.wait_ge(gXp, tc - 1)
                eng.dma_start(out=xb[p], in_=x_in[tc * 128:(tc + 1) * 128, :]).then_inc(dX6[p], 16)
            if SA < 8:
                return
            eng.wait_ge(vSc2, 1)
            for tc in range(NTT):
                p = tc % 2
                if tc >= 2:
                    eng.wait_ge(vXq, tc - 1)
                eng.dma_start(out=x1_b[p],
                              in_=x1_s[tc * 128:(tc + 1) * 128, :]).then_inc(dX1L[p], 16)
            for idx in range(2 * NIC + NEC):
                if idx < 2 * NIC:
                    ic2, w_i = idx // 2, idx % 2
                    p = w_i
                    if idx >= 2:
                        eng.wait_ge(vWq, idx - 1)
                    Wt_ = w_gate if w_i == 0 else w_up
                    eng.dma_start(out=wf_b[p][:, 0:1024],
                                  in_=Wt_[ic2 * 128:(ic2 + 1) * 128, :]).then_inc(dWF[p], 16)
                else:
                    ch = idx - 2 * NIC
                    p = idx % 2
                    eng.wait_ge(vWq, idx - 1)
                    eng.dma_start(out=wf_b[p][:, 0:2816],
                                  in_=w_down[ch * 128:(ch + 1) * 128, :]).then_inc(dWF[p], 16)
            if SA < 9:
                return
            for g in range(NTG):
                eng.wait_ge(dXQS[0], 16 * min(2 * (g + 1), 16))
                eng.wait_ge(dXQS[1], 16 * min(2 * (g + 1), 16))
                for hc in range(NEC):
                    eng.dma_start(out=xqT_sb[hc][:, g * 512:(g + 1) * 512],
                                  in_=xq_s[g * 512:(g + 1) * 512, hc * 128:(hc + 1) * 128],
                                  transpose=True).then_inc(dXQT, 16)
            for ic in range(NIC):
                p = ic % 2
                eng.wait_ge(dWQS[0], 16 * (ic + 1))
                eng.wait_ge(dWQS[1], 16 * (ic + 1))
                if ic >= 2:
                    eng.wait_ge(tFF, NTG * (ic - 1))
                for hc in range(NEC):
                    eng.dma_start(out=wgT_b[p][:, hc * 128:(hc + 1) * 128],
                                  in_=wgq_s[ic * 128:(ic + 1) * 128,
                                            hc * 128:(hc + 1) * 128],
                                  transpose=True).then_inc(dWGT[p], 16)
                    eng.dma_start(out=wuT_b[p][:, hc * 128:(hc + 1) * 128],
                                  in_=wuq_s[ic * 128:(ic + 1) * 128,
                                            hc * 128:(hc + 1) * 128],
                                  transpose=True).then_inc(dWGT[p], 16)
            if SA < 11:
                return
            eng.wait_ge(tFF, NIC * NTG)
            eng.wait_ge(dWDS, 16 * NEC)
            for ic in range(NIC):
                for oh in range(2):
                    eng.dma_start(out=wdT_sb[:, (ic * 2 + oh) * 512:(ic * 2 + oh + 1) * 512],
                                  in_=wdq_s[oh * 512:(oh + 1) * 512,
                                            ic * 128:(ic + 1) * 128],
                                  transpose=True).then_inc(dWDT, 16)
            eng.wait_ge(dSW, 16 * NEC)
            eng.dma_start(out=swd_row[:, :], in_=swd_d.ap()).then_inc(dSWR, 16)
            eng.wait_ge(vSc4, 1)
            for B in range(NTG + 1):
                if B < NTG:
                    for ic in range(NIC):
                        i = B * NIC + ic
                        p = i % 6
                        if i >= 6:
                            eng.wait_ge(vHq, i - 5)
                        eng.dma_start(out=hfb6[p][:, 0:512],
                                      in_=hf_s[ic * 128:(ic + 1) * 128,
                                               B * 512:(B + 1) * 512]).then_inc(dHFL6[p], 16)
                if B >= 1:
                    for tci in range(4):
                        tc = (B - 1) * 4 + tci
                        p = tc % 2
                        if tc >= 2:
                            eng.wait_ge(vX2, tc - 1)
                        eng.dma_start(out=x1_b[p],
                                      in_=x1_s[tc * 128:(tc + 1) * 128, :]).then_inc(dX1L[p], 16)
            eng.wait_ge(dOUT[0], 16 * 16)
            eng.wait_ge(dOUT[1], 16 * 16)

        @block.tensor
        def _(eng):
            if SA < 2:
                return
            eng.wait_ge(dWT, 16 * 192)
            for g in range(NTG):
                eng.wait_ge(dXLT, 16 * 8 * (g + 1))
                for br in range(3):
                    for ec in range(NEC):
                        idx = g * 24 + br * 8 + ec
                        bk_ = idx % 4
                        if idx >= 4:
                            eng.wait_ge(sCp4[bk_], idx // 4)
                        for dc in range(NEC):
                            mm = eng.matmul(pbank[bk_][:, :],
                                       wt_sb[br][:, (dc * 8 + ec) * 128:(dc * 8 + ec + 1) * 128],
                                       xlt_b[:, dc * 512:(dc + 1) * 512],
                                       start=(dc == 0), stop=(dc == NEC - 1))
                        mm.then_inc(tMM, 1)
            if SA < 4:
                return
            for tc in range(NTT):
                p = tc % 2
                eng.wait_ge(dVT[p], 16 * (tc // 2 + 1))
                for hp in range(8):
                    mm = eng.matmul(pG[:, hp * 128:(hp + 1) * 128],
                               vt_b[p][:, hp * 128:(hp + 1) * 128],
                               vt_b[p][:, hp * 128:(hp + 1) * 128],
                               start=(tc == 0 and hp in (0, 4)),
                               stop=(tc == NTT - 1),
                               skip_group_check=True)
                mm.then_inc(tMM4, 1)
            if SA < 5:
                return
            eng.wait_ge(dWPT, 16 * 8)
            eng.wait_ge(vVV, 1)
            for hp in range(8):
                if hp >= 1:
                    eng.wait_ge(vM5, hp)
                for w in range(2):
                    po = w * 64
                    lhs = vv_sb[po:po + 64, hp * 128 + po:hp * 128 + po + 64]
                    for oh in range(2):
                        mm = eng.matmul(pM[po:po + 64, oh * 512:(oh + 1) * 512],
                                   lhs,
                                   wpT_sb[po:po + 64, hp * 1024 + oh * 512:hp * 1024 + (oh + 1) * 512],
                                   start=True, stop=True,
                                   skip_group_check=True)
                mm.then_inc(tM5, 1)
            eng.wait_ge(dPrm, N_PRM)
            eng.matmul(pG[:, 0:512], onesb[0:1, :], bp_row[0:1, 0:512], start=True, stop=True)
            eng.matmul(pG[:, 512:1024], onesb[0:1, :], bp_row[0:1, 512:1024],
                       start=True, stop=True).then_inc(tBp, 1)
            if SA < 6:
                return
            eng.wait_ge(vM5, 8)
            eng.wait_ge(vQk, 64)
            for tc in range(NTT):
                b0 = (tc % 2) * 2
                if tc >= 2:
                    eng.wait_ge(vX1, tc - 1)
                for oh in range(2):
                    for cc in range(NEC):
                        mm = eng.matmul(pbank[b0 + oh][:, :],
                                   qk_res[cc][:, tc * 128:(tc + 1) * 128],
                                   M_sb[cc][:, oh * 512:(oh + 1) * 512],
                                   start=(cc == 0), stop=(cc == NEC - 1))
                mm.then_inc(tP6, 1)
            if SA < 8:
                return
            eng.wait_ge(vSc, 1)
            eng.matmul(pM[:, 0:2], onesb[0:1, :], rowb[0:1, 128:130],
                       start=True, stop=True).then_inc(tSc, 1)
            if SA < 9:
                return
            for ic in range(NIC):
                p = ic % 2
                eng.wait_ge(dWGT[p], 16 * 16 * (ic // 2 + 1))
                for B in range(NTG):
                    u = ic * NTG + B
                    gbank = pbank[0 + 2 * (u % 2)]
                    ubank = pbank[1 + 2 * (u % 2)]
                    eng.wait_ge(dXQT, 16 * 8 * (B + 1))
                    if u >= 2:
                        eng.wait_ge(sSil, u - 1)
                        eng.wait_ge(vFF, u - 1)
                    for hc in range(NEC):
                        eng.matmul(gbank[:, :], wgT_b[p][:, hc * 128:(hc + 1) * 128],
                                   xqT_sb[hc][:, B * 512:(B + 1) * 512],
                                   start=(hc == 0), stop=(hc == NEC - 1))
                    for hc in range(NEC):
                        mm = eng.matmul(ubank[:, :], wuT_b[p][:, hc * 128:(hc + 1) * 128],
                                   xqT_sb[hc][:, B * 512:(B + 1) * 512],
                                   start=(hc == 0), stop=(hc == NEC - 1))
                    mm.then_inc(tFF, 1)
            if SA < 10:
                return
            eng.wait_ge(vSc3, 2)
            eng.matmul(pM[:, 0:2], onesb[0:1, :], rowb[0:1, 130:132],
                       start=True, stop=True).then_inc(tSc3, 1)
            eng.wait_ge(vSwd, 1)
            eng.wait_ge(dSWR, 16)
            eng.matmul(pM[:, 0:512], onesb[0:1, :], swd_row[0:1, 0:512], start=True, stop=True)
            eng.matmul(pM[:, 512:1024], onesb[0:1, :], swd_row[0:1, 512:1024],
                       start=True, stop=True).then_inc(tSw, 1)
            if SA < 11:
                return
            eng.wait_ge(dWDT, 16 * 44)
            for B in range(NTG):
                eng.wait_ge(vHq, NIC * (B + 1))
                hsb = hint_sb[B % 2]
                for tci in range(4):
                    tc = B * 4 + tci
                    for oh in range(2):
                        grp = tc * 2 + oh
                        bk_ = grp % 4
                        if grp >= 4:
                            eng.wait_ge(vX2, (grp - 4) // 2 + 1)
                        for ic in range(NIC):
                            mm = eng.matmul(pbank[bk_][:, :],
                                       hsb[ic][:, tci * 128:(tci + 1) * 128],
                                       wdT_sb[:, (ic * 2 + oh) * 512:(ic * 2 + oh + 1) * 512],
                                       start=(ic == 0), stop=(ic == NIC - 1))
                    mm.then_inc(tDn, 1)

        @block.vector
        def _(eng):
            eng.memset(scr[:, 28:32], 0.0)
            for t in range(NTT):
                p = t % 2
                eng.wait_ge(dX[p], 16 * (t // 2 + 1))
                emit_ln_stats(eng, xb[p], scr[:, 1:2])
                spacer(eng)
                spacer(eng).then_inc(vVar, 1) if False else None
                eng.tensor_copy(scr[:, 30:31], scr[:, 31:32]).then_inc(vVar, 1)
                eng.wait_ge(sSqrt, t + 1)
                eng.reciprocal(scr[:, 8:9], scr[:, 7:8])
                spacer(eng)
                if t >= 2:
                    eng.wait_ge(dXLN, 16 * (t - 1))
                eng.tensor_scalar(xln_b[p], xb[p], scr[:, 1:2], scr[:, 8:9],
                                  ALU.subtract, ALU.mult).then_inc(vApp, 1)
            for idx in range(4 * NEC):
                p = idx % 2
                eng.wait_ge(dW[p], 16 * (idx // 2 + 1))
                if idx >= 2:
                    eng.wait_ge(dWS[p], 16 * (idx // 2))
                eng.tensor_copy(wb_b[p], wf_b[p][:, 0:1024]).then_inc(vCast, 1)
            if SA < 2:
                return
            for idx in range(NTG * 24):
                sl = idx % 4
                g = idx // 24
                r = idx % 24
                br, ec = r // 8, r % 8
                eng.wait_ge(sCp4[sl], idx // 4 + 1)
                base = (br * 8 + ec) * 64 + g * 8
                eng.bn_stats(bnsl[:, base:base + 6], hb4[sl][:, 0:512]).then_inc(vBn4[sl], 1)
            if SA < 3:
                return
            for br in range(3):
                for ec in range(NEC):
                    base = (br * 8 + ec) * 64
                    bme = bass.AP(bnsl, base + 1, [[1536, 128], [8, 8]])
                    bmo = bass.AP(bnsl, base + 4, [[1536, 128], [8, 8]])
                    bce = bass.AP(bnsl, base + 2, [[1536, 128], [8, 8]])
                    bco = bass.AP(bnsl, base + 5, [[1536, 128], [8, 8]])
                    eng.tensor_tensor(scr[:, 16:24], bme, bmo, ALU.add)
                    eng.tensor_tensor(scr[:, 32:40], bce, bco, ALU.add)
                    eng.tensor_tensor(scr[:, 40:48], bme, bme, ALU.mult)
                    eng.tensor_tensor(scr[:, 48:56], bmo, bmo, ALU.mult)
                    eng.reduce_sum(scr[:, 24:25], scr[:, 16:24], axis=AX)
                    eng.reduce_sum(scr[:, 25:26], scr[:, 32:40], axis=AX)
                    eng.tensor_add(scr[:, 56:64], scr[:, 40:48], scr[:, 48:56])
                    col = br * 8 + ec
                    eng.tensor_scalar(stat[:, col:col + 1], scr[:, 24:25], 256.0,
                                      None, ALU.mult)
                    eng.reduce_sum(scr[:, 26:27], scr[:, 56:64], axis=AX)
                    spacer(eng)
                    eng.scalar_tensor_tensor(stat[:, 24 + col:25 + col], scr[:, 26:27], 256.0,
                                             scr[:, 25:26], ALU.mult, ALU.add)
            spacer(eng)
            eng.tensor_copy(scr[:, 30:31], scr[:, 31:32]).then_inc(vSt, 1)
            eng.wait_ge(dST, 32)
            NTOT = float(NC_N * NT)
            eng.tensor_scalar(scr2[:, 0:24], gst[:, 0:24], 1.0 / NTOT, None, ALU.mult)
            eng.tensor_scalar(scr2[:, 24:48], gst[:, 24:48], 1.0 / NTOT, None, ALU.mult)
            eng.tensor_tensor(scr[:, 0:24], scr2[:, 0:24], scr2[:, 0:24], ALU.mult)
            spacer(eng)
            eng.scalar_tensor_tensor(scr2[:, 40:64], scr2[:, 24:48], 1.0,
                                     scr[:, 0:24], ALU.bypass, ALU.subtract)
            spacer(eng)
            eng.tensor_scalar(scr2[:, 40:64], scr2[:, 40:64], BN_EPS, None, ALU.add)
            spacer(eng)
            eng.tensor_copy(scr[:, 30:31], scr[:, 31:32]).then_inc(vSt2, 1)
            eng.wait_ge(sSq3, 1)
            eng.reciprocal(scr[:, 0:24], scr2[:, 40:64])
            spacer(eng)
            eng.scalar_tensor_tensor(abn[:, 0:24], gam[:, 0:24], 4.0,
                                     scr[:, 0:24], ALU.mult, ALU.mult)
            spacer(eng)
            eng.tensor_tensor(scr2[:, 40:64], abn[:, 0:24], scr2[:, 0:24], ALU.mult)
            spacer(eng)
            eng.scalar_tensor_tensor(bbn[:, 0:24], bet[:, 0:24], 4.0,
                                     scr2[:, 40:64], ALU.mult, ALU.subtract)
            spacer(eng)
            eng.tensor_scalar(bbn[:, 0:24], bbn[:, 0:24], 4.0, None, ALU.subtract)
            spacer(eng)
            eng.tensor_copy(scr[:, 30:31], scr[:, 31:32]).then_inc(vAB, 1)
            if SA < 4:
                return
            for i in range(NTG * NEC):
                g, ec = i // NEC, i % NEC
                eng.wait_ge(sSig, 3 * i + 2)
                eng.tensor_tensor(qk_res[ec][:, g * 512:(g + 1) * 512],
                                  qb_b, kb_b, ALU.mult).then_inc(vQk, 1)
            if SA < 5:
                return
            eng.wait_ge(tMM4, NTT)
            eng.tensor_scalar(vv_sb, pG[:, :], 0.125, None, ALU.mult).then_inc(vVV, 1)
            for hp in range(8):
                eng.wait_ge(tM5, hp + 1)
                eng.tensor_copy(M_sb[hp], pM[:, :]).then_inc(vM5, 1)
            eng.wait_ge(tBp, 1)
            eng.tensor_copy(bp_bc[:, :], pG[:, :]).then_inc(vBp, 1)
            if SA < 6:
                return
            for tc in range(NTT):
                p = tc % 2
                b0 = (tc % 2) * 2
                eng.wait_ge(tP6, tc + 1)
                eng.wait_ge(gXp, tc + 1)
                if tc >= 2:
                    eng.wait_ge(dX1S[p], 16 * (tc // 2))
                eng.tensor_tensor(x1_b[p][:, 0:512], pbank[b0][:, :],
                                  xpre_b[p][:, 0:512], ALU.add)
                eng.tensor_tensor(x1_b[p][:, 512:1024], pbank[b0 + 1][:, :],
                                  xpre_b[p][:, 512:1024], ALU.add).then_inc(vX1, 1)
                emit_ln_stats(eng, x1_b[p], ln2m[:, tc:tc + 1])
                spacer(eng)
                eng.tensor_copy(scr[:, 30:31], scr[:, 31:32]).then_inc(vVar6, 1)
                eng.wait_ge(sSqrt6, NTT + tc + 1)
                eng.reciprocal(ln2r[:, tc:tc + 1], scr[:, 7:8])
                spacer(eng)
                eng.tensor_scalar(xlnscr, x1_b[p], ln2m[:, tc:tc + 1], ln2r[:, tc:tc + 1],
                                  ALU.subtract, ALU.mult)
                eng.reduce_max(mxsl[:, tc:tc + 1], xlnscr, axis=AX,
                               apply_absolute_value=True).then_inc(vMx, 1)
            spacer(eng)
            eng.reduce_max(mxa[:, 0:1], mxsl[:, 0:NTT], axis=AX)
            spacer(eng)
            eng.tensor_copy(scr[:, 30:31], scr[:, 31:32]).then_inc(vMx, 1)
            if SA < 8:
                return
            eng.wait_ge(dMX, 32)
            eng.reduce_max(rowb[0:1, 132:133], rowb[0:1, 0:128], axis=AX)
            eng.wait_ge(sClp, 2)
            spacer(eng)
            eng.tensor_tensor(rowb[0:1, 133:134], rowb[0:1, 132:133], clp[0:1, 2:3], ALU.mult)
            spacer(eng)
            eng.reciprocal(rowb[0:1, 134:135], rowb[0:1, 133:134])
            eng.tensor_scalar(rowb[0:1, 129:130], rowb[0:1, 133:134], 1.0 / 127.0, None,
                              ALU.mult)
            eng.tensor_scalar(rowb[0:1, 128:129], rowb[0:1, 134:135], 127.0, None, ALU.mult)
            spacer(eng)
            eng.tensor_copy(scr[:, 30:31], scr[:, 31:32]).then_inc(vSc, 1)
            eng.wait_ge(tSc, 1)
            eng.tensor_copy(sc_x[:, 0:2], pM[:, 0:2]).then_inc(vSc2, 1)
            spacer(eng)
            eng.tensor_scalar(scr2[:, 0:NTT], ln2r[:, 0:NTT], sc_x[:, 0:1], None, ALU.mult)
            spacer(eng)
            eng.scalar_tensor_tensor(scr2[:, 32:32 + NTT], ln2m[:, 0:NTT], -1.0,
                                     scr2[:, 0:NTT], ALU.mult, ALU.mult)
            spacer(eng)
            spacer(eng)
            for tc in range(NTT):
                p = tc % 2
                eng.wait_ge(dX1L[p], 16 * (tc // 2 + 1))
                if tc >= 2:
                    eng.wait_ge(dXQS[p], 16 * (tc // 2))
                eng.tensor_scalar(xlnscr, x1_b[p], scr2[:, tc:tc + 1], scr2[:, 32 + tc:33 + tc],
                                  ALU.mult, ALU.add)
                eng.tensor_scalar(xlnscr, xlnscr, RND_C, RND_C, ALU.add, ALU.subtract)
                eng.tensor_scalar(xq_b[p], xlnscr, 127.0, -128.0, ALU.min,
                                  ALU.max).then_inc(vXq, 1)
            for idx in range(2 * NIC + NEC):
                p = idx % 2
                wd_mode = idx >= 2 * NIC
                wid = 2816 if wd_mode else 1024
                buf = wqd_b if wd_mode else wqg_b[p]
                eng.wait_ge(dWF[p], 16 * (idx // 2 + 1))
                if wd_mode:
                    if idx > 2 * NIC:
                        eng.wait_ge(dWDS, 16 * (idx - 2 * NIC))
                elif idx >= 2:
                    eng.wait_ge(dWQS[idx % 2], 16 * (idx // 2))
                eng.reduce_max(scr[:, 12:13], wf_b[p][:, 0:wid], axis=AX,
                               apply_absolute_value=True)
                spacer(eng)
                eng.reciprocal(scr[:, 13:14], scr[:, 12:13])
                if idx < 2 * NIC:
                    dst = (sw_g if idx % 2 == 0 else sw_u)[:, idx // 2:idx // 2 + 1]
                else:
                    dst = sw_d[:, idx - 2 * NIC:idx - 2 * NIC + 1]
                eng.tensor_scalar(dst, scr[:, 12:13], 1.0 / 127.0, None, ALU.mult)
                eng.tensor_scalar(wf_b[p][:, 0:wid], wf_b[p][:, 0:wid],
                                  scr[:, 13:14], 127.0, ALU.mult, ALU.mult)
                eng.tensor_scalar(buf[:, 0:wid], wf_b[p][:, 0:wid],
                                  RND_C, RND_C, ALU.add, ALU.subtract).then_inc(vWq, 1)
                if idx < 2 * NIC and idx % 2 == 1:
                    ic2 = idx // 2
                    eng.tensor_scalar(scg[:, ic2:ic2 + 1], sw_g[:, ic2:ic2 + 1],
                                      sc_x[:, 1:2], None, ALU.mult)
                    eng.tensor_scalar(scu[:, ic2:ic2 + 1], sw_u[:, ic2:ic2 + 1],
                                      sc_x[:, 1:2], None, ALU.mult)
                    eng.tensor_copy(scr[:, 30:31], scr[:, 31:32]).then_inc(vCg, 1)
            if SA < 9:
                return
            eng.wait_ge(vSc2, 1)
            eng.tensor_scalar(scg[:, :], sw_g[:, :], sc_x[:, 1:2], None, ALU.mult)
            eng.tensor_scalar(scu[:, :], sw_u[:, :], sc_x[:, 1:2], None,
                              ALU.mult).then_inc(vCg, 1)
            for u in range(NIC * NTG):
                ic = u // NTG
                B = u % NTG
                sp = u % 4
                hp_ = u % 2
                eng.wait_ge(vCg, ic + 1)
                eng.wait_ge(sSil, u + 1)
                if u >= 2:
                    eng.wait_ge(dHF[hp_], 16 * (u // 2))
                ubank = pbank[1 + 2 * (u % 2)]
                eng.scalar_tensor_tensor(hfb[hp_][:, 0:512], ubank[:, :],
                                         scu[:, ic:ic + 1], sgf[sp][:, 0:512],
                                         ALU.mult, ALU.mult)
                eng.reduce_max(mxsl[:, u:u + 1], hfb[hp_][:, 0:512], axis=AX,
                               apply_absolute_value=True).then_inc(vFF, 1)
            spacer(eng)
            eng.reduce_max(mxa[:, 1:2], mxsl[:, 0:NIC * NTG], axis=AX)
            spacer(eng)
            eng.tensor_copy(scr[:, 30:31], scr[:, 31:32]).then_inc(vFF, 1)
            if SA < 10:
                return
            eng.wait_ge(dMX, 64)
            eng.reduce_max(rowb[0:1, 135:136], rowb[0:1, 0:128], axis=AX)
            spacer(eng)
            eng.tensor_tensor(rowb[0:1, 136:137], rowb[0:1, 135:136], clp[0:1, 3:4], ALU.mult)
            spacer(eng)
            eng.reciprocal(rowb[0:1, 137:138], rowb[0:1, 136:137])
            eng.tensor_scalar(rowb[0:1, 131:132], rowb[0:1, 136:137], 1.0 / 127.0, None,
                              ALU.mult)
            eng.tensor_scalar(rowb[0:1, 130:131], rowb[0:1, 137:138], 127.0, None, ALU.mult)
            spacer(eng)
            eng.tensor_copy(scr[:, 30:31], scr[:, 31:32]).then_inc(vSc3, 1)
            eng.wait_ge(tSc3, 2)
            eng.tensor_copy(sc_x[:, 2:4], pM[:, 0:2])
            spacer(eng)
            eng.tensor_scalar(scr[:, 16:24], sw_d[:, :], sc_x[:, 3:4], None, ALU.mult)
            spacer(eng)
            eng.tensor_copy(scr[:, 30:31], scr[:, 31:32]).then_inc(vSc4, 1)
            eng.tensor_copy(scr[:, 31:32], scr[:, 30:31]).then_inc(vSwd, 1)
            eng.wait_ge(tSw, 1)
            eng.tensor_copy(swd_bc[:, :], pM[:, :]).then_inc(vSw, 1)
            if SA < 11:
                return
            eng.wait_ge(vSw, 1)
            for B in range(NTG + 1):
                if B < NTG:
                    hsb = hint_sb[B % 2]
                    if B >= 2:
                        eng.wait_ge(tDn, 4 * (B - 1))
                    for ic in range(NIC):
                        i = B * NIC + ic
                        p = i % 6
                        eng.wait_ge(dHFL6[p], 16 * (i // 6 + 1))
                        eng.tensor_scalar(xlnscr[:, 0:512], hfb6[p][:, 0:512],
                                          sc_x[:, 2:3], RND_C, ALU.mult, ALU.add)
                        eng.tensor_scalar(xlnscr[:, 0:512], xlnscr[:, 0:512],
                                          RND_C, 127.0, ALU.subtract, ALU.min)
                        eng.tensor_scalar(hsb[ic][:, 0:512], xlnscr[:, 0:512],
                                          -128.0, None, ALU.max).then_inc(vHq, 1)
                if B >= 1:
                    for tci in range(4):
                        tc = (B - 1) * 4 + tci
                        p = tc % 2
                        eng.wait_ge(tDn, tc + 1)
                        eng.wait_ge(dX1L[p], 16 * (16 + tc // 2 + 1))
                        if tc >= 2:
                            eng.wait_ge(dOUT[p], 16 * (tc // 2))
                        for oh in range(2):
                            grp = tc * 2 + oh
                            bk_ = grp % 4
                            eng.tensor_tensor(xpre_b[p][:, oh * 512:(oh + 1) * 512],
                                              pbank[bk_][:, :],
                                              swd_bc[:, oh * 512:(oh + 1) * 512], ALU.mult)
                        eng.tensor_add(xpre_b[p], xpre_b[p], x1_b[p]).then_inc(vX2, 1)

        @block.scalar
        def _(eng):
            for t in range(NTT):
                eng.wait_ge(vVar, t + 1)
                eng.activation(scr[:, 7:8], scr[:, 6:7], AF.Sqrt)
                eng.copy(scr[:, 28:29], scr[:, 29:30]).then_inc(sSqrt, 1)
            if SA < 2:
                return
            for idx in range(NTG * 24):
                sl = idx % 4
                eng.wait_ge(tMM, idx + 1)
                if idx >= 4:
                    eng.wait_ge(vBn4[sl], idx // 4)
                    eng.wait_ge(dH4[sl], 16 * (idx // 4))
                eng.copy(hb4[sl][:, 0:512], pbank[sl][:, :]).then_inc(sCp4[sl], 1)
            if SA < 3:
                return
            eng.wait_ge(vSt2, 1)
            eng.activation(scr2[:, 40:64], scr2[:, 40:64], AF.Sqrt)
            eng.copy(scr[:, 28:29], scr[:, 29:30]).then_inc(sSq3, 1)
            if SA < 4:
                return
            eng.wait_ge(vAB, 1)
            for i in range(NTG * NEC):
                ec = i % NEC
                p = i % 2
                eng.wait_ge(dHL[p], 48 * (i // 2 + 1))
                if i >= 1:
                    eng.wait_ge(vQk, i)
                eng.activation(qb_b, hq_b[p][:, 0:512], AF.Sigmoid,
                               bias=bbn[:, ec:ec + 1], scale=abn[:, ec:ec + 1])
                eng.activation(kb_b, hk_b[p][:, 0:512], AF.Sigmoid,
                               bias=bbn[:, 8 + ec:8 + ec + 1],
                               scale=abn[:, 8 + ec:8 + ec + 1])
                if i >= 2:
                    eng.wait_ge(dVS[p], 16 * (i // 2))
                eng.activation(vb_b[p], hv_b[p][:, 0:512], AF.Sigmoid,
                               bias=bbn[:, 16 + ec:16 + ec + 1],
                               scale=abn[:, 16 + ec:16 + ec + 1]).then_inc(sSig, 3)
            if SA < 6:
                return
            for tc in range(NTT):
                eng.wait_ge(vVar6, NTT + tc + 1)
                eng.activation(scr[:, 7:8], scr[:, 6:7], AF.Sqrt)
                eng.copy(scr[:, 28:29], scr[:, 29:30]).then_inc(sSqrt6, 1)
            if SA < 8:
                return
            eng.wait_ge(dPrm, N_PRM)
            eng.activation(clp[0:1, 2:3], clp[0:1, 0:1], AF.Sigmoid)
            eng.activation(clp[0:1, 3:4], clp[0:1, 1:2], AF.Sigmoid)
            eng.copy(scr[:, 28:29], scr[:, 29:30]).then_inc(sClp, 2)
            if SA < 9:
                return
            for u in range(NIC * NTG):
                ic = u // NTG
                sp = u % 4
                eng.wait_ge(vCg, ic + 1)
                eng.wait_ge(tFF, u + 1)
                if u >= 4:
                    eng.wait_ge(vFF, u - 3)
                gbank = pbank[0 + 2 * (u % 2)]
                eng.activation(sgf[sp][:, 0:512], gbank[:, :], AF.Silu,
                               scale=scg[:, ic:ic + 1]).then_inc(sSil, 1)

        @block.gpsimd
        def _(eng):
            for t in range(NTT):
                p = t % 2
                eng.wait_ge(vApp, t + 1)
                eng.dma_start(out=xln_s[t * 128:(t + 1) * 128, :],
                              in_=xln_b[p]).then_inc(dXLN, 16)
            for idx in range(4 * NEC):
                p = idx % 2
                eng.wait_ge(vCast, idx + 1)
                w_i, ch = idx // NEC, idx % NEC
                eng.dma_start(out=wbf_s[w_i, ch * 128:(ch + 1) * 128, :],
                              in_=wb_b[p]).then_inc(dWS[p], 16)
            if SA < 2:
                return
            for idx in range(NTG * 24):
                sl = idx % 4
                g, r = idx // 24, idx % 24
                br, ec = r // 8, r % 8
                eng.wait_ge(sCp4[sl], idx // 4 + 1)
                eng.dma_start(out=h_s[br, ec * 128:(ec + 1) * 128, g * 512:(g + 1) * 512],
                              in_=hb4[sl][:, 0:512]).then_inc(dH4[sl], 16)
            if SA < 3:
                return
            eng.wait_ge(vSt, 1)
            eng.dma_start(out=st_i[:, :], in_=stat[:, 0:48]).then_inc(dST, 16)
            eng.wait_ge(dST, 16)
            eng.collective_compute("AllReduce", ALU.add,
                                   replica_groups=[list(range(NC_N))],
                                   ins=[st_i.ap()], outs=[st_o.ap()]).then_inc(cCC, 1)
            eng.wait_ge(cCC, 1)
            eng.dma_start(out=gst[:, 0:48], in_=st_o[:, :]).then_inc(dST, 16)
            if SA < 4:
                return
            for i in range(NTG * NEC):
                g, ec = i // NEC, i % NEC
                p = i % 2
                eng.wait_ge(sSig, 3 * (i + 1))
                eng.dma_start(out=v_s[ec * 128:(ec + 1) * 128, g * 512:(g + 1) * 512],
                              in_=vb_b[p][:, 0:512]).then_inc(dVS[p], 16)
            if SA < 6:
                return
            eng.wait_ge(vBp, 1)
            for tc in range(NTT):
                p = tc % 2
                eng.wait_ge(dX6[p], 16 * (16 + tc // 2 + 1))
                if tc >= 2:
                    eng.wait_ge(vX1, tc - 1)
                eng.tensor_tensor(xpre_b[p], xb[p], bp_bc[:, :], ALU.add).then_inc(gXp, 1)
                eng.wait_ge(vX1, tc + 1)
                eng.wait_ge(vVar6, NTT + tc + 1)
                eng.dma_start(out=x1_s[tc * 128:(tc + 1) * 128, :],
                              in_=x1_b[p]).then_inc(dX1S[p], 16)
            if SA < 8:
                return
            eng.wait_ge(vMx, NTT + 1)
            eng.dma_start(out=mx_i.ap(), in_=mxa[:, 0:1]).then_inc(dMX, 16)
            eng.wait_ge(dMX, 16)
            eng.collective_compute("AllReduce", ALU.max,
                                   replica_groups=[list(range(NC_N))],
                                   ins=[mx_i.ap()], outs=[mx_o.ap()]).then_inc(cCC, 1)
            eng.wait_ge(cCC, 2)
            eng.dma_start(out=rowb[0:1, 0:128], in_=mx_o.ap()).then_inc(dMX, 16)
            for tc in range(NTT):
                p = tc % 2
                eng.wait_ge(vXq, tc + 1)
                eng.dma_start(out=xq_s[tc * 128:(tc + 1) * 128, :],
                              in_=xq_b[p]).then_inc(dXQS[p], 16)
            for idx in range(2 * NIC + NEC):
                p = idx % 2
                eng.wait_ge(vWq, idx + 1)
                if idx < 2 * NIC:
                    ic2 = idx // 2
                    tgt = wgq_s if idx % 2 == 0 else wuq_s
                    eng.dma_start(out=tgt[ic2 * 128:(ic2 + 1) * 128, :],
                                  in_=wqg_b[p][:, 0:1024]).then_inc(dWQS[p], 16)
                elif False:
                    pass
                else:
                    ch = idx - 2 * NIC
                    eng.dma_start(out=wdq_s[ch * 128:(ch + 1) * 128, :],
                                  in_=wqd_b[:, 0:2816]).then_inc(dWDS, 16)
            if SA < 9:
                return
            for u in range(NIC * NTG):
                ic, B = u // NTG, u % NTG
                hp_ = u % 2
                eng.wait_ge(vFF, u + 1)
                eng.dma_start(out=hf_s[ic * 128:(ic + 1) * 128, B * 512:(B + 1) * 512],
                              in_=hfb[hp_][:, 0:512]).then_inc(dHF[hp_], 16)
            if SA < 10:
                return
            eng.wait_ge(vFF, NIC * NTG + 1)
            eng.dma_start(out=mh_i.ap(), in_=mxa[:, 1:2]).then_inc(dMX, 16)
            eng.wait_ge(dMX, 48)
            eng.collective_compute("AllReduce", ALU.max,
                                   replica_groups=[list(range(NC_N))],
                                   ins=[mh_i.ap()], outs=[mh_o.ap()]).then_inc(cCC, 1)
            eng.wait_ge(cCC, 3)
            eng.dma_start(out=rowb[0:1, 0:128], in_=mh_o.ap()).then_inc(dMX, 16)
            eng.wait_ge(vSwd, 1)
            for hc in range(NEC):
                eng.dma_start(out=swd_d[hc * 128:(hc + 1) * 128],
                              in_=scr[:, 16 + hc:17 + hc]).then_inc(dSW, 16)
            if SA < 11:
                return
            for tc in range(NTT):
                p = tc % 2
                eng.wait_ge(vX2, tc + 1)
                eng.dma_start(out=out_t[tc * 128:(tc + 1) * 128, :],
                              in_=xpre_b[p]).then_inc(dOUT[p], 16)

    es.close()
    return nc


_NC_CACHE = {}


def _get_nc():
    if "nc" not in _NC_CACHE:
        _NC_CACHE["nc"] = build_nc()
    return _NC_CACHE["nc"]


def kernel(L0_attn_x, L0_attn_Wq, L0_attn_gq, L0_attn_bq, L0_attn_Wk, L0_attn_gk,
           L0_attn_bk, L0_attn_Wv, L0_attn_gv, L0_attn_bv, L0_attn_Wp, L0_attn_bp,
           L0_ffn_clip_x, L0_ffn_clip_down, L0_ffn_w_gate, L0_ffn_w_up, L0_ffn_w_down):
    nc = _get_nc()
    f32 = np.float32
    common = {
        "Wq": np.ascontiguousarray(L0_attn_Wq, f32),
        "Wk": np.ascontiguousarray(L0_attn_Wk, f32),
        "Wv": np.ascontiguousarray(L0_attn_Wv, f32),
        "Wp": np.ascontiguousarray(L0_attn_Wp, f32),
        "gq": np.ascontiguousarray(L0_attn_gq, f32),
        "bq": np.ascontiguousarray(L0_attn_bq, f32),
        "gk": np.ascontiguousarray(L0_attn_gk, f32),
        "bk": np.ascontiguousarray(L0_attn_bk, f32),
        "gv": np.ascontiguousarray(L0_attn_gv, f32),
        "bv": np.ascontiguousarray(L0_attn_bv, f32),
        "bp": np.ascontiguousarray(L0_attn_bp, f32),
        "clip_x": np.ascontiguousarray(L0_ffn_clip_x, f32),
        "clip_dn": np.ascontiguousarray(L0_ffn_clip_down, f32),
        "w_gate": np.ascontiguousarray(L0_ffn_w_gate, f32),
        "w_up": np.ascontiguousarray(L0_ffn_w_up, f32),
        "w_down": np.ascontiguousarray(L0_ffn_w_down, f32),
        "ones_r": np.ones((1, 128), f32),
    }
    xs = np.ascontiguousarray(L0_attn_x, f32)
    in_maps = [dict(common, x=xs[b]) for b in range(NC_N)]
    res = run_bass_kernel_spmd(nc, in_maps, core_ids=list(range(NC_N)))
    return np.stack([np.asarray(r["out"]) for r in res.results], axis=0)


# version 10 (final)
# speedup vs baseline: 1.1190x; 1.0514x over previous
import numpy as np
import ml_dtypes
from contextlib import ExitStack

import concourse.bass as bass
import concourse.mybir as mybir
from concourse.bass_utils import run_bass_kernel_spmd

F32 = mybir.dt.float32
BF16 = mybir.dt.bfloat16
ALU = mybir.AluOpType
AF = mybir.ActivationFunctionType
AX = mybir.AxisListType.X

RND_C = float(1.5 * 2.0 ** 23)
D = 1024
NT = 4096
DI = 2816
NEC = 8
NIC = 22
NTT = 32
NTG = 8
LN_EPS = 1e-5
BN_EPS = 1e-5
NC_N = 8


class Arena:
    def __init__(self, tensor, cols):
        self.t, self.cols, self.off, self.marks = tensor, cols, 0, []

    def alloc(self, n):
        n = (n + 15) // 16 * 16
        assert self.off + n <= self.cols, f"arena overflow {self.off}+{n}>{self.cols}"
        o = self.off
        self.off += n
        return self.t[:, o:o + n]

    def push(self):
        self.marks.append(self.off)

    def pop(self):
        self.off = self.marks.pop()


def build_nc(stop_after=99, debug=False):
    nc = bass.Bass(target_bir_lowering=False, detect_race_conditions=False)
    x_in = nc.dram_tensor("x", [NT, D], F32, kind="ExternalInput")
    Wq = nc.dram_tensor("Wq", [D, D], F32, kind="ExternalInput")
    Wk = nc.dram_tensor("Wk", [D, D], F32, kind="ExternalInput")
    Wv = nc.dram_tensor("Wv", [D, D], F32, kind="ExternalInput")
    Wp = nc.dram_tensor("Wp", [D, D], F32, kind="ExternalInput")
    gq = nc.dram_tensor("gq", [D], F32, kind="ExternalInput")
    bq = nc.dram_tensor("bq", [D], F32, kind="ExternalInput")
    gk = nc.dram_tensor("gk", [D], F32, kind="ExternalInput")
    bk = nc.dram_tensor("bk", [D], F32, kind="ExternalInput")
    gv = nc.dram_tensor("gv", [D], F32, kind="ExternalInput")
    bv = nc.dram_tensor("bv", [D], F32, kind="ExternalInput")
    bp = nc.dram_tensor("bp", [D], F32, kind="ExternalInput")
    clip_x = nc.dram_tensor("clip_x", [1], F32, kind="ExternalInput")
    clip_dn = nc.dram_tensor("clip_dn", [1], F32, kind="ExternalInput")
    w_gate = nc.dram_tensor("w_gate", [DI, D], F32, kind="ExternalInput")
    w_up = nc.dram_tensor("w_up", [DI, D], F32, kind="ExternalInput")
    w_down = nc.dram_tensor("w_down", [D, DI], F32, kind="ExternalInput")
    ones_r = nc.dram_tensor("ones_r", [1, 128], F32, kind="ExternalInput")
    out_t = nc.dram_tensor("out", [NT, D], F32, kind="ExternalOutput")

    sk = "ExternalOutput" if debug else "Internal"
    xln_s = nc.dram_tensor("xln_s", [NT, D], BF16, kind=sk)
    h_s = nc.dram_tensor("h_s", [3, D, NT], F32, kind=sk)
    v_s = nc.dram_tensor("v_s", [D, NT], BF16, kind="Internal")
    x1_s = nc.dram_tensor("x1_s", [NT, D], F32, kind=sk)
    xq_s = nc.dram_tensor("xq_s", [NT, D], BF16, kind=sk)
    hf_s = nc.dram_tensor("hf_s", [DI, NT], F32, kind=sk)
    wbf_s = nc.dram_tensor("wbf_s", [4, D, D], BF16, kind="Internal")
    wgq_s = nc.dram_tensor("wgq_s", [DI, D], BF16, kind=sk)
    wuq_s = nc.dram_tensor("wuq_s", [DI, D], BF16, kind="Internal")
    wdq_s = nc.dram_tensor("wdq_s", [D, DI], BF16, kind="Internal")
    st_i = nc.dram_tensor("st_i", [128, 48], F32, kind="Internal")
    st_o = nc.dram_tensor("st_o", [128, 48], F32, kind="Internal")
    mx_i = nc.dram_tensor("mx_i", [128], F32, kind="Internal")
    mx_o = nc.dram_tensor("mx_o", [128], F32, kind="Internal")
    mh_i = nc.dram_tensor("mh_i", [128], F32, kind="Internal")
    mh_o = nc.dram_tensor("mh_o", [128], F32, kind="Internal")
    swd_d = nc.dram_tensor("swd_d", [D], F32, kind="Internal")

    es = ExitStack()
    sbuf = lambda name, shape, dt: es.enter_context(nc.sbuf_tensor(name, shape, dt))
    psum = lambda name, shape: es.enter_context(nc.psum_tensor(name, shape, F32))
    sem = lambda name: es.enter_context(nc.semaphore(name))
    sem2 = lambda name: [es.enter_context(nc.semaphore(name + "0")),
                         es.enter_context(nc.semaphore(name + "1"))]

    A16 = Arena(sbuf("A16", [128, 61440], BF16), 61440)
    A32 = Arena(sbuf("A32", [128, 13312], F32), 13312)

    gam = sbuf("gam", [128, 24], F32)
    bet = sbuf("bet", [128, 24], F32)
    abn = sbuf("abn", [128, 24], F32)
    bbn = sbuf("bbn", [128, 24], F32)
    bnsl = sbuf("bnsl", [128, 24 * 64], F32)
    stat = sbuf("stat", [128, 64], F32)
    gst = sbuf("gst", [128, 64], F32)
    scr = sbuf("scr", [128, 64], F32)
    scr2 = sbuf("scr2", [128, 64], F32)
    ln2m = sbuf("ln2m", [128, NTT], F32)
    ln2r = sbuf("ln2r", [128, NTT], F32)
    mxa = sbuf("mxa", [128, 4], F32)
    onesb = sbuf("onesb", [1, 128], F32)
    rowb = sbuf("rowb", [1, 144], F32)
    sc_x = sbuf("sc_x", [128, 4], F32)
    bp_row = sbuf("bp_row", [1, D], F32)
    bp_bc = sbuf("bp_bc", [128, D], F32)
    swd_row = sbuf("swd_row", [1, D], F32)
    swd_bc = sbuf("swd_bc", [128, D], F32)
    sw_g = sbuf("sw_g", [128, NIC], F32)
    sw_u = sbuf("sw_u", [128, NIC], F32)
    sw_d = sbuf("sw_d", [128, NEC], F32)
    scg = sbuf("scg", [128, NIC], F32)
    scu = sbuf("scu", [128, NIC], F32)
    clp = sbuf("clp", [1, 16], F32)
    mxsl = sbuf("mxsl", [128, 192], F32)

    pA = psum("pA", [128, 512])
    pB = psum("pB", [128, 512])
    pC = psum("pC", [128, 512])
    pD = psum("pD", [128, 512])
    pG = psum("pG", [128, 1024])
    pM = psum("pM", [128, 1024])
    pbank = [pA, pB, pC, pD]

    dX = sem2("dX")
    dW = sem2("dW")
    dHL = sem2("dHL")
    dVT = sem2("dVT")
    dX1S = sem2("dX1S")
    dX1L = sem2("dX1L")
    dXQS = sem2("dXQS")
    dWF = sem2("dWF")
    dWGT = sem2("dWGT")
    dHF = sem2("dHF")
    dOUT = sem2("dOUT")
    dVS = sem2("dVS")
    dWS = sem2("dWS");  dWQS = sem2("dWQS")
    dPrm = sem("dPrm");  dWT = sem("dWT");  dWPT = sem("dWPT")
    dXLN4 = [sem(f"dXLN4_{i}") for i in range(4)]
    dXLT = sem("dXLT");  dST = sem("dST");  dXQT = sem("dXQT")
    dWDS = sem("dWDS");  dWDT = sem("dWDT");  dMX = sem("dMX")
    dSW = sem("dSW");  dSWR = dXQT
    cCC = sem("cCC")
    vVar = sem("vVar");  sSqrt = sem("sSqrt");  vApp = sem("vApp")
    vCast = sem("vCast")
    tMM = sem("tMM");  _unused = None
    sCp4 = [sem(f"sCp4_{i}") for i in range(4)]
    vBn4 = [sem(f"vBn4_{i}") for i in range(4)]
    dH4 = [sem(f"dH4_{i}") for i in range(4)]
    dHFL6 = [sem(f"dHFL6_{i}") for i in range(6)]
    vSt = sem("vSt");  vSt2 = sem("vSt2");  sSq3 = sem("sSq3");  vAB = sem("vAB")
    sSig = sem("sSig");  vQk = sem("vQk");  tMM4 = sem("tMM4")
    vVV = sem("vVV");  tM5 = sem("tM5");  vM5 = sem("vM5");  tBp = sem("tBp");  vBp = sem("vBp")
    gXp = sem("gXp");  tP6 = sem("tP6");  vX1 = sem("vX1")
    vMx = sem("vMx")
    sClp = sem("sClp");  vSc = sem("vSc");  tSc = sem("tSc");  vSc2 = sem("vSc2")
    vXq = sem("vXq");  vWq = sem("vWq");  vCg = sem("vCg")
    tFF = sem("tFF");  sSil = sem("sSil");  vFF = sem("vFF")
    vSc4 = sem("vSc4");  vSwd = sem("vSwd")
    tSw = sem("tSw");  vSw = sem("vSw")
    vHq = sem("vHq");  tDn = sem("tDn");  vX2 = sem("vX2")
    dX6 = dX
    vVar6 = vVar
    sSqrt6 = sSqrt
    vSc3 = vSc
    tSc3 = tSc

    xb0_off = A32.off
    xb = [A32.alloc(1024), A32.alloc(1024)]
    sqscr = A32.alloc(1024)
    _r0 = A32.off
    hb = [A32.alloc(512), A32.alloc(512)]
    hq_b = [A32.alloc(512), A32.alloc(512)]
    hk_b = [A32.alloc(512), A32.alloc(512)]
    hv_b = [A32.alloc(512), A32.alloc(512)]
    xpre_b = [A32.alloc(1024), A32.alloc(1024)]
    assert A32.off - _r0 >= 2 * 2816
    x1_b = [A32.alloc(1024), A32.alloc(1024)]
    xlnscr = A32.alloc(1024)
    wf_b = [x1_b[0], x1_b[1]]
    sgf = [hq_b[0], hq_b[1], hk_b[0], hk_b[1]]
    hb4 = [hb[0], hb[1], hq_b[0], hq_b[1]]
    hfb6 = [hb[0], hb[1], hq_b[0], hq_b[1], hk_b[0], hk_b[1]]
    hfb = [hv_b[0], hv_b[1]]

    wf_g = xb[0]
    wf_u = xb[1]
    wf_d = A32.t[:, xb0_off:xb0_off + 2816]
    wpT_sb = A16.alloc(8 * 1024)
    A16.push()
    wt_sb = [A16.alloc(64 * 128) for _ in range(3)]
    xlt_b = A16.alloc(8 * 512)
    xln_b = [A16.alloc(1024), A16.alloc(1024)]
    wb_b = [A16.alloc(1024), A16.alloc(1024)]
    A16.pop()
    A16.push()
    qk_res = [A16.alloc(4096) for _ in range(8)]
    qb_b = A16.alloc(512)
    kb_b = A16.alloc(512)
    vb_b = [A16.alloc(512), A16.alloc(512)]
    vt_b = [A16.alloc(1024), A16.alloc(1024)]
    vv_sb = A16.alloc(1024)
    M_sb = [A16.alloc(1024) for _ in range(8)]
    xq_b = [A16.alloc(1024), A16.alloc(1024)]
    wqg_b = [A16.alloc(1024), A16.alloc(1024)]
    wqd_b = A16.alloc(2816)
    A16.pop()
    A16.push()
    xqT_sb = [A16.alloc(4096) for _ in range(8)]
    wgT_b = [A16.alloc(8 * 128), A16.alloc(8 * 128)]
    wuT_b = [A16.alloc(8 * 128), A16.alloc(8 * 128)]
    A16.pop()
    A16.push()
    wdT_sb = A16.alloc(44 * 512)
    hint_sb = [[A16.alloc(512) for _ in range(NIC)], [A16.alloc(512) for _ in range(NIC)]]
    A16.pop()

    SA = stop_after
    N_PRM = 16 * (6 * NEC + 4)


    def spacer(eng):
        eng.tensor_copy(scr[:, 30:31], scr[:, 31:32])

    def emit_ln_stats(eng, xsrc, mean_slot):
        eng.bn_stats(scr[:, 32:38], xsrc[:, 0:512])
        eng.bn_stats(scr[:, 40:46], xsrc[:, 512:1024])
        eng.tensor_add(scr[:, 48:49], scr[:, 33:34], scr[:, 36:37])
        eng.tensor_add(scr[:, 49:50], scr[:, 34:35], scr[:, 37:38])
        eng.tensor_tensor(scr[:, 50:51], scr[:, 33:34], scr[:, 33:34], ALU.mult)
        eng.tensor_tensor(scr[:, 51:52], scr[:, 36:37], scr[:, 36:37], ALU.mult)
        eng.tensor_add(scr[:, 52:53], scr[:, 41:42], scr[:, 44:45])
        eng.tensor_add(scr[:, 53:54], scr[:, 42:43], scr[:, 45:46])
        eng.tensor_tensor(scr[:, 54:55], scr[:, 41:42], scr[:, 41:42], ALU.mult)
        eng.tensor_tensor(scr[:, 55:56], scr[:, 44:45], scr[:, 44:45], ALU.mult)
        eng.tensor_add(scr[:, 56:57], scr[:, 48:49], scr[:, 52:53])
        eng.tensor_add(scr[:, 57:58], scr[:, 50:51], scr[:, 51:52])
        eng.tensor_add(scr[:, 58:59], scr[:, 54:55], scr[:, 55:56])
        eng.tensor_add(scr[:, 59:60], scr[:, 49:50], scr[:, 53:54])
        eng.tensor_scalar(mean_slot, scr[:, 56:57], 0.25, None, ALU.mult)
        eng.tensor_add(scr[:, 60:61], scr[:, 57:58], scr[:, 58:59])
        eng.tensor_tensor(scr[:, 61:62], mean_slot, mean_slot, ALU.mult)
        eng.scalar_tensor_tensor(scr[:, 62:63], scr[:, 60:61], 256.0,
                                 scr[:, 59:60], ALU.mult, ALU.add)
        spacer(eng)
        eng.tensor_scalar(scr[:, 63:64], scr[:, 62:63], 1.0 / D, None, ALU.mult)
        spacer(eng)
        eng.scalar_tensor_tensor(scr[:, 6:7], scr[:, 63:64], 1.0,
                                 scr[:, 61:62], ALU.bypass, ALU.subtract)
        eng.tensor_scalar(scr[:, 6:7], scr[:, 6:7], LN_EPS, None, ALU.add) if False else None
        spacer(eng)
        eng.tensor_scalar(scr[:, 6:7], scr[:, 6:7], LN_EPS, None, ALU.add)

    with nc.Block() as block:

        @block.sync
        def _(eng):
            for i, vec in enumerate([gq, gk, gv]):
                for ec in range(NEC):
                    eng.dma_start(out=gam[:, i * 8 + ec:i * 8 + ec + 1],
                                  in_=vec[ec * 128:(ec + 1) * 128]).then_inc(dPrm, 16)
            for i, vec in enumerate([bq, bk, bv]):
                for ec in range(NEC):
                    eng.dma_start(out=bet[:, i * 8 + ec:i * 8 + ec + 1],
                                  in_=vec[ec * 128:(ec + 1) * 128]).then_inc(dPrm, 16)
            eng.dma_start(out=bp_row[:, :], in_=bp.ap()).then_inc(dPrm, 16)
            eng.dma_start(out=onesb[:, :], in_=ones_r[:, :]).then_inc(dPrm, 16)
            eng.dma_start(out=clp[0:1, 0:1], in_=clip_x.ap()).then_inc(dPrm, 16)
            eng.dma_start(out=clp[0:1, 1:2], in_=clip_dn.ap()).then_inc(dPrm, 16)
            for t in range(NTT):
                p = t % 2
                if t >= 2:
                    eng.wait_ge(vApp, t - 1)
                eng.dma_start(out=xb[p], in_=x_in[t * 128:(t + 1) * 128, :]).then_inc(dX[p], 16)
                for idx in (2 * t, 2 * t + 1):
                    if idx < 4 * NEC:
                        pw = idx % 2
                        if idx >= 2:
                            eng.wait_ge(vCast, idx - 1)
                        W = [Wq, Wk, Wv, Wp][idx // NEC]
                        ch = idx % NEC
                        eng.dma_start(out=wf_b[pw][:, 0:1024],
                                      in_=W[ch * 128:(ch + 1) * 128, :]).then_inc(dW[pw], 16)
            eng.wait_ge(dWS[0], 16 * 12)
            eng.wait_ge(dWS[1], 16 * 12)
            for br in range(3):
                for dc in range(NEC):
                    for ec in range(NEC):
                        eng.dma_start(
                            out=wt_sb[br][:, (dc * 8 + ec) * 128:(dc * 8 + ec + 1) * 128],
                            in_=wbf_s[br, ec * 128:(ec + 1) * 128, dc * 128:(dc + 1) * 128],
                            transpose=True).then_inc(dWT, 16)
            eng.wait_ge(dWS[0], 16 * 16)
            eng.wait_ge(dWS[1], 16 * 16)
            for ecp in range(NEC):
                eng.dma_start(out=wpT_sb[:, ecp * 1024:(ecp + 1) * 1024],
                              in_=wbf_s[3, 0:D, ecp * 128:(ecp + 1) * 128],
                              transpose=True).then_inc(dWPT, 16)
            if SA < 2:
                return
            for g in range(NTG):
                for q4 in range(4):
                    eng.wait_ge(dXLN4[q4], 16 * (g + 1))
                if g >= 1:
                    eng.wait_ge(tMM, 24 * g)
                for dc in range(NEC):
                    eng.dma_start(out=xlt_b[:, dc * 512:(dc + 1) * 512],
                                  in_=xln_s[g * 512:(g + 1) * 512, dc * 128:(dc + 1) * 128],
                                  transpose=True).then_inc(dXLT, 16)
            if SA < 4:
                return
            eng.wait_ge(vBn4[2], 48)
            eng.wait_ge(vBn4[3], 48)
            eng.wait_ge(dH4[2], 16 * 48)
            eng.wait_ge(dH4[3], 16 * 48)
            for g in range(NTG):
                for ec in range(NEC):
                    i = g * NEC + ec
                    p = i % 2
                    if i >= 2:
                        eng.wait_ge(sSig, 3 * (i - 1))
                    eng.dma_start(out=hq_b[p][:, 0:512],
                                  in_=h_s[0, ec * 128:(ec + 1) * 128,
                                          g * 512:(g + 1) * 512]).then_inc(dHL[p], 16)
                    eng.dma_start(out=hk_b[p][:, 0:512],
                                  in_=h_s[1, ec * 128:(ec + 1) * 128,
                                          g * 512:(g + 1) * 512]).then_inc(dHL[p], 16)
                    eng.dma_start(out=hv_b[p][:, 0:512],
                                  in_=h_s[2, ec * 128:(ec + 1) * 128,
                                          g * 512:(g + 1) * 512]).then_inc(dHL[p], 16)
                eng.wait_ge(dVS[0], 16 * 4 * (g + 1))
                eng.wait_ge(dVS[1], 16 * 4 * (g + 1))
                for tci in range(4):
                    tc = g * 4 + tci
                    p = tc % 2
                    if tc >= 2:
                        eng.wait_ge(tMM4, tc - 1)
                    eng.dma_start(out=vt_b[p][:, 0:1024],
                                  in_=v_s[0:D, tc * 128:(tc + 1) * 128],
                                  transpose=True).then_inc(dVT[p], 16)
            if SA < 6:
                return
            for tc in range(NTT):
                p = tc % 2
                if tc >= 2:
                    eng.wait_ge(gXp, tc - 1)
                eng.dma_start(out=xb[p], in_=x_in[tc * 128:(tc + 1) * 128, :]).then_inc(dX6[p], 16)
            if SA < 8:
                return
            eng.wait_ge(vSc2, 1)
            for tc in range(NTT):
                p = tc % 2
                if tc >= 2:
                    eng.wait_ge(vXq, tc - 1)
                eng.dma_start(out=x1_b[p],
                              in_=x1_s[tc * 128:(tc + 1) * 128, :]).then_inc(dX1L[p], 16)
            eng.wait_ge(gXp, NTT)
            for idx in range(2 * NIC + NEC):
                if idx < 2 * NIC:
                    ic2, w_i = idx // 2, idx % 2
                    p = w_i
                    if idx >= 2:
                        eng.wait_ge(vWq, idx - 1)
                    Wt_ = w_gate if w_i == 0 else w_up
                    eng.dma_start(out=(wf_g if w_i == 0 else wf_u),
                                  in_=Wt_[ic2 * 128:(ic2 + 1) * 128, :]).then_inc(dWF[p], 16)
                else:
                    ch = idx - 2 * NIC
                    p = idx % 2
                    eng.wait_ge(vWq, idx)
                    eng.dma_start(out=wf_d,
                                  in_=w_down[ch * 128:(ch + 1) * 128, :]).then_inc(dWF[p], 16)
            if SA < 9:
                return
            for g in range(NTG):
                eng.wait_ge(dXQS[0], 16 * min(2 * (g + 1), 16))
                eng.wait_ge(dXQS[1], 16 * min(2 * (g + 1), 16))
                for hc in range(NEC):
                    eng.dma_start(out=xqT_sb[hc][:, g * 512:(g + 1) * 512],
                                  in_=xq_s[g * 512:(g + 1) * 512, hc * 128:(hc + 1) * 128],
                                  transpose=True).then_inc(dXQT, 16)
            for ic in range(NIC):
                p = ic % 2
                eng.wait_ge(dWQS[0], 16 * (ic + 1))
                eng.wait_ge(dWQS[1], 16 * (ic + 1))
                if ic >= 2:
                    eng.wait_ge(tFF, NTG * (ic - 1))
                for hc in range(NEC):
                    eng.dma_start(out=wgT_b[p][:, hc * 128:(hc + 1) * 128],
                                  in_=wgq_s[ic * 128:(ic + 1) * 128,
                                            hc * 128:(hc + 1) * 128],
                                  transpose=True).then_inc(dWGT[p], 16)
                    eng.dma_start(out=wuT_b[p][:, hc * 128:(hc + 1) * 128],
                                  in_=wuq_s[ic * 128:(ic + 1) * 128,
                                            hc * 128:(hc + 1) * 128],
                                  transpose=True).then_inc(dWGT[p], 16)
            if SA < 11:
                return
            eng.wait_ge(tFF, NIC * NTG)
            eng.wait_ge(dWDS, 16 * NEC)
            for ic in range(NIC):
                for oh in range(2):
                    eng.dma_start(out=wdT_sb[:, (ic * 2 + oh) * 512:(ic * 2 + oh + 1) * 512],
                                  in_=wdq_s[oh * 512:(oh + 1) * 512,
                                            ic * 128:(ic + 1) * 128],
                                  transpose=True).then_inc(dWDT, 16)
            eng.wait_ge(dSW, 16 * NEC)
            eng.dma_start(out=swd_row[:, :], in_=swd_d.ap()).then_inc(dSWR, 16)
            eng.wait_ge(vSc4, 1)
            for B in range(NTG + 1):
                if B < NTG:
                    for ic in range(NIC):
                        i = B * NIC + ic
                        p = i % 6
                        if i >= 6:
                            eng.wait_ge(vHq, i - 5)
                        eng.dma_start(out=hfb6[p][:, 0:512],
                                      in_=hf_s[ic * 128:(ic + 1) * 128,
                                               B * 512:(B + 1) * 512]).then_inc(dHFL6[p], 16)
                if B >= 1:
                    for tci in range(4):
                        tc = (B - 1) * 4 + tci
                        p = tc % 2
                        if tc >= 2:
                            eng.wait_ge(vX2, tc - 1)
                        eng.dma_start(out=x1_b[p],
                                      in_=x1_s[tc * 128:(tc + 1) * 128, :]).then_inc(dX1L[p], 16)
            eng.wait_ge(dOUT[0], 16 * 16)
            eng.wait_ge(dOUT[1], 16 * 16)

        @block.tensor
        def _(eng):
            if SA < 2:
                return
            eng.wait_ge(dWT, 16 * 192)
            for g in range(NTG):
                eng.wait_ge(dXLT, 16 * 8 * (g + 1))
                for br in range(3):
                    for ec in range(NEC):
                        idx = g * 24 + br * 8 + ec
                        bk_ = idx % 4
                        if idx >= 4:
                            eng.wait_ge(sCp4[bk_], idx // 4)
                        for dc in range(NEC):
                            mm = eng.matmul(pbank[bk_][:, :],
                                       wt_sb[br][:, (dc * 8 + ec) * 128:(dc * 8 + ec + 1) * 128],
                                       xlt_b[:, dc * 512:(dc + 1) * 512],
                                       start=(dc == 0), stop=(dc == NEC - 1))
                        mm.then_inc(tMM, 1)
            if SA < 4:
                return
            for tc in range(NTT):
                p = tc % 2
                eng.wait_ge(dVT[p], 16 * (tc // 2 + 1))
                for hp in range(8):
                    mm = eng.matmul(pG[:, hp * 128:(hp + 1) * 128],
                               vt_b[p][:, hp * 128:(hp + 1) * 128],
                               vt_b[p][:, hp * 128:(hp + 1) * 128],
                               start=(tc == 0 and hp in (0, 4)),
                               stop=(tc == NTT - 1),
                               skip_group_check=True)
                mm.then_inc(tMM4, 1)
            if SA < 5:
                return
            eng.wait_ge(dWPT, 16 * 8)
            eng.wait_ge(vVV, 1)
            for hp in range(8):
                if hp >= 1:
                    eng.wait_ge(vM5, hp)
                for w in range(2):
                    po = w * 64
                    lhs = vv_sb[po:po + 64, hp * 128 + po:hp * 128 + po + 64]
                    for oh in range(2):
                        mm = eng.matmul(pM[po:po + 64, oh * 512:(oh + 1) * 512],
                                   lhs,
                                   wpT_sb[po:po + 64, hp * 1024 + oh * 512:hp * 1024 + (oh + 1) * 512],
                                   start=True, stop=True,
                                   skip_group_check=True)
                mm.then_inc(tM5, 1)
            eng.wait_ge(dPrm, N_PRM)
            eng.matmul(pG[:, 0:512], onesb[0:1, :], bp_row[0:1, 0:512], start=True, stop=True)
            eng.matmul(pG[:, 512:1024], onesb[0:1, :], bp_row[0:1, 512:1024],
                       start=True, stop=True).then_inc(tBp, 1)
            if SA < 6:
                return
            eng.wait_ge(vM5, 8)
            eng.wait_ge(vQk, 64)
            for tc in range(NTT):
                b0 = (tc % 2) * 2
                if tc >= 2:
                    eng.wait_ge(vX1, tc - 1)
                for oh in range(2):
                    for cc in range(NEC):
                        mm = eng.matmul(pbank[b0 + oh][:, :],
                                   qk_res[cc][:, tc * 128:(tc + 1) * 128],
                                   M_sb[cc][:, oh * 512:(oh + 1) * 512],
                                   start=(cc == 0), stop=(cc == NEC - 1))
                mm.then_inc(tP6, 1)
            if SA < 8:
                return
            eng.wait_ge(vSc, 1)
            eng.matmul(pM[:, 0:2], onesb[0:1, :], rowb[0:1, 128:130],
                       start=True, stop=True).then_inc(tSc, 1)
            if SA < 9:
                return
            for ic in range(NIC):
                p = ic % 2
                eng.wait_ge(dWGT[p], 16 * 16 * (ic // 2 + 1))
                for B in range(NTG):
                    u = ic * NTG + B
                    gbank = pbank[0 + 2 * (u % 2)]
                    ubank = pbank[1 + 2 * (u % 2)]
                    eng.wait_ge(dXQT, 16 * 8 * (B + 1))
                    if u >= 2:
                        eng.wait_ge(sSil, u - 1)
                        eng.wait_ge(vFF, u - 1)
                    for hc in range(NEC):
                        eng.matmul(gbank[:, :], wgT_b[p][:, hc * 128:(hc + 1) * 128],
                                   xqT_sb[hc][:, B * 512:(B + 1) * 512],
                                   start=(hc == 0), stop=(hc == NEC - 1))
                    for hc in range(NEC):
                        mm = eng.matmul(ubank[:, :], wuT_b[p][:, hc * 128:(hc + 1) * 128],
                                   xqT_sb[hc][:, B * 512:(B + 1) * 512],
                                   start=(hc == 0), stop=(hc == NEC - 1))
                    mm.then_inc(tFF, 1)
            if SA < 10:
                return
            eng.wait_ge(vSc3, 2)
            eng.matmul(pM[:, 0:2], onesb[0:1, :], rowb[0:1, 130:132],
                       start=True, stop=True).then_inc(tSc3, 1)
            eng.wait_ge(vSwd, 1)
            eng.wait_ge(dSWR, 16 * 65)
            eng.matmul(pM[:, 0:512], onesb[0:1, :], swd_row[0:1, 0:512], start=True, stop=True)
            eng.matmul(pM[:, 512:1024], onesb[0:1, :], swd_row[0:1, 512:1024],
                       start=True, stop=True).then_inc(tSw, 1)
            if SA < 11:
                return
            eng.wait_ge(dWDT, 16 * 44)
            for B in range(NTG):
                eng.wait_ge(vHq, NIC * (B + 1))
                hsb = hint_sb[B % 2]
                for tci in range(4):
                    tc = B * 4 + tci
                    for oh in range(2):
                        grp = tc * 2 + oh
                        bk_ = grp % 4
                        if grp >= 4:
                            eng.wait_ge(vX2, (grp - 4) // 2 + 1)
                        for ic in range(NIC):
                            mm = eng.matmul(pbank[bk_][:, :],
                                       hsb[ic][:, tci * 128:(tci + 1) * 128],
                                       wdT_sb[:, (ic * 2 + oh) * 512:(ic * 2 + oh + 1) * 512],
                                       start=(ic == 0), stop=(ic == NIC - 1))
                    mm.then_inc(tDn, 1)

        @block.vector
        def _(eng):
            eng.memset(scr[:, 28:32], 0.0)
            for t in range(NTT):
                p = t % 2
                eng.wait_ge(dX[p], 16 * (t // 2 + 1))
                emit_ln_stats(eng, xb[p], scr[:, 1:2])
                spacer(eng)
                spacer(eng).then_inc(vVar, 1) if False else None
                eng.tensor_copy(scr[:, 30:31], scr[:, 31:32]).then_inc(vVar, 1)
                eng.wait_ge(sSqrt, t + 1)
                eng.reciprocal(scr[:, 8:9], scr[:, 7:8])
                spacer(eng)
                if t >= 2:
                    eng.wait_ge(dXLN4[(t - 2) % 4], 16 * ((t - 2) // 4 + 1))
                eng.tensor_scalar(xln_b[p], xb[p], scr[:, 1:2], scr[:, 8:9],
                                  ALU.subtract, ALU.mult).then_inc(vApp, 1)
                for idx in (2 * t, 2 * t + 1):
                    if idx < 4 * NEC:
                        pw = idx % 2
                        eng.wait_ge(dW[pw], 16 * (idx // 2 + 1))
                        if idx >= 2:
                            eng.wait_ge(dWS[pw], 16 * (idx // 2))
                        eng.tensor_copy(wb_b[pw], wf_b[pw][:, 0:1024]).then_inc(vCast, 1)
            if SA < 2:
                return
            for idx in range(NTG * 24):
                sl = idx % 4
                g = idx // 24
                r = idx % 24
                br, ec = r // 8, r % 8
                eng.wait_ge(sCp4[sl], idx // 4 + 1)
                base = (br * 8 + ec) * 64 + g * 8
                eng.bn_stats(bnsl[:, base:base + 6], hb4[sl][:, 0:512]).then_inc(vBn4[sl], 1)
            if SA < 3:
                return
            for br in range(3):
                for ec in range(NEC):
                    base = (br * 8 + ec) * 64
                    bme = bass.AP(bnsl, base + 1, [[1536, 128], [8, 8]])
                    bmo = bass.AP(bnsl, base + 4, [[1536, 128], [8, 8]])
                    bce = bass.AP(bnsl, base + 2, [[1536, 128], [8, 8]])
                    bco = bass.AP(bnsl, base + 5, [[1536, 128], [8, 8]])
                    eng.tensor_tensor(scr[:, 16:24], bme, bmo, ALU.add)
                    eng.tensor_tensor(scr[:, 32:40], bce, bco, ALU.add)
                    eng.tensor_tensor(scr[:, 40:48], bme, bme, ALU.mult)
                    eng.tensor_tensor(scr[:, 48:56], bmo, bmo, ALU.mult)
                    eng.reduce_sum(scr[:, 24:25], scr[:, 16:24], axis=AX)
                    eng.reduce_sum(scr[:, 25:26], scr[:, 32:40], axis=AX)
                    eng.tensor_add(scr[:, 56:64], scr[:, 40:48], scr[:, 48:56])
                    col = br * 8 + ec
                    eng.tensor_scalar(stat[:, col:col + 1], scr[:, 24:25], 256.0,
                                      None, ALU.mult)
                    eng.reduce_sum(scr[:, 26:27], scr[:, 56:64], axis=AX)
                    spacer(eng)
                    eng.scalar_tensor_tensor(stat[:, 24 + col:25 + col], scr[:, 26:27], 256.0,
                                             scr[:, 25:26], ALU.mult, ALU.add)
            spacer(eng)
            eng.tensor_copy(scr[:, 30:31], scr[:, 31:32]).then_inc(vSt, 1)
            eng.wait_ge(dST, 32)
            NTOT = float(NC_N * NT)
            eng.tensor_scalar(scr2[:, 0:24], gst[:, 0:24], 1.0 / NTOT, None, ALU.mult)
            eng.tensor_scalar(scr2[:, 24:48], gst[:, 24:48], 1.0 / NTOT, None, ALU.mult)
            eng.tensor_tensor(scr[:, 0:24], scr2[:, 0:24], scr2[:, 0:24], ALU.mult)
            spacer(eng)
            eng.scalar_tensor_tensor(scr2[:, 40:64], scr2[:, 24:48], 1.0,
                                     scr[:, 0:24], ALU.bypass, ALU.subtract)
            spacer(eng)
            eng.tensor_scalar(scr2[:, 40:64], scr2[:, 40:64], BN_EPS, None, ALU.add)
            spacer(eng)
            eng.tensor_copy(scr[:, 30:31], scr[:, 31:32]).then_inc(vSt2, 1)
            eng.wait_ge(sSq3, 1)
            eng.reciprocal(scr[:, 0:24], scr2[:, 40:64])
            spacer(eng)
            eng.scalar_tensor_tensor(abn[:, 0:24], gam[:, 0:24], 4.0,
                                     scr[:, 0:24], ALU.mult, ALU.mult)
            spacer(eng)
            eng.tensor_tensor(scr2[:, 40:64], abn[:, 0:24], scr2[:, 0:24], ALU.mult)
            spacer(eng)
            eng.scalar_tensor_tensor(bbn[:, 0:24], bet[:, 0:24], 4.0,
                                     scr2[:, 40:64], ALU.mult, ALU.subtract)
            spacer(eng)
            eng.tensor_scalar(bbn[:, 0:24], bbn[:, 0:24], 4.0, None, ALU.subtract)
            spacer(eng)
            eng.tensor_copy(scr[:, 30:31], scr[:, 31:32]).then_inc(vAB, 1)
            if SA < 4:
                return
            for i in range(NTG * NEC):
                g, ec = i // NEC, i % NEC
                eng.wait_ge(sSig, 3 * i + 2)
                eng.tensor_tensor(qk_res[ec][:, g * 512:(g + 1) * 512],
                                  qb_b, kb_b, ALU.mult).then_inc(vQk, 1)
            if SA < 5:
                return
            eng.wait_ge(tMM4, NTT)
            eng.tensor_scalar(vv_sb, pG[:, :], 0.125, None, ALU.mult).then_inc(vVV, 1)
            for hp in range(8):
                eng.wait_ge(tM5, hp + 1)
                eng.tensor_copy(M_sb[hp], pM[:, :]).then_inc(vM5, 1)
            eng.wait_ge(tBp, 1)
            eng.tensor_copy(bp_bc[:, :], pG[:, :]).then_inc(vBp, 1)
            if SA < 6:
                return
            for tc in range(NTT):
                p = tc % 2
                b0 = (tc % 2) * 2
                eng.wait_ge(tP6, tc + 1)
                eng.wait_ge(gXp, tc + 1)
                if tc >= 2:
                    eng.wait_ge(dX1S[p], 16 * (tc // 2))
                eng.tensor_tensor(x1_b[p][:, 0:512], pbank[b0][:, :],
                                  xpre_b[p][:, 0:512], ALU.add)
                eng.tensor_tensor(x1_b[p][:, 512:1024], pbank[b0 + 1][:, :],
                                  xpre_b[p][:, 512:1024], ALU.add).then_inc(vX1, 1)
                emit_ln_stats(eng, x1_b[p], ln2m[:, tc:tc + 1])
                spacer(eng)
                eng.tensor_copy(scr[:, 30:31], scr[:, 31:32]).then_inc(vVar6, 1)
                eng.wait_ge(sSqrt6, NTT + tc + 1)
                eng.reciprocal(ln2r[:, tc:tc + 1], scr[:, 7:8])
                spacer(eng)
                eng.tensor_scalar(xlnscr, x1_b[p], ln2m[:, tc:tc + 1], ln2r[:, tc:tc + 1],
                                  ALU.subtract, ALU.mult)
                eng.reduce_max(mxsl[:, tc:tc + 1], xlnscr, axis=AX,
                               apply_absolute_value=True).then_inc(vMx, 1)
            spacer(eng)
            eng.reduce_max(mxa[:, 0:1], mxsl[:, 0:NTT], axis=AX)
            spacer(eng)
            eng.tensor_copy(scr[:, 30:31], scr[:, 31:32]).then_inc(vMx, 1)
            if SA < 8:
                return
            eng.wait_ge(dMX, 32)
            eng.reduce_max(rowb[0:1, 132:133], rowb[0:1, 0:128], axis=AX)
            eng.wait_ge(sClp, 2)
            spacer(eng)
            eng.tensor_tensor(rowb[0:1, 133:134], rowb[0:1, 132:133], clp[0:1, 2:3], ALU.mult)
            spacer(eng)
            eng.reciprocal(rowb[0:1, 134:135], rowb[0:1, 133:134])
            eng.tensor_scalar(rowb[0:1, 129:130], rowb[0:1, 133:134], 1.0 / 127.0, None,
                              ALU.mult)
            eng.tensor_scalar(rowb[0:1, 128:129], rowb[0:1, 134:135], 127.0, None, ALU.mult)
            spacer(eng)
            eng.tensor_copy(scr[:, 30:31], scr[:, 31:32]).then_inc(vSc, 1)
            eng.wait_ge(tSc, 1)
            eng.tensor_copy(sc_x[:, 0:2], pM[:, 0:2]).then_inc(vSc2, 1)
            spacer(eng)
            eng.tensor_scalar(scr2[:, 0:NTT], ln2r[:, 0:NTT], sc_x[:, 0:1], None, ALU.mult)
            spacer(eng)
            eng.scalar_tensor_tensor(scr2[:, 32:32 + NTT], ln2m[:, 0:NTT], -1.0,
                                     scr2[:, 0:NTT], ALU.mult, ALU.mult)
            spacer(eng)
            spacer(eng)
            for tc in range(NTT):
                p = tc % 2
                eng.wait_ge(dX1L[p], 16 * (tc // 2 + 1))
                if tc >= 2:
                    eng.wait_ge(dXQS[p], 16 * (tc // 2))
                eng.tensor_scalar(xlnscr, x1_b[p], scr2[:, tc:tc + 1], scr2[:, 32 + tc:33 + tc],
                                  ALU.mult, ALU.add)
                eng.tensor_scalar(xlnscr, xlnscr, RND_C, RND_C, ALU.add, ALU.subtract)
                eng.tensor_scalar(xq_b[p], xlnscr, 127.0, -128.0, ALU.min,
                                  ALU.max).then_inc(vXq, 1)
            for idx in range(2 * NIC + NEC):
                p = idx % 2
                wd_mode = idx >= 2 * NIC
                wid = 2816 if wd_mode else 1024
                buf = wqd_b if wd_mode else wqg_b[p]
                wfv = wf_d if wd_mode else (wf_g if idx % 2 == 0 else wf_u)
                eng.wait_ge(dWF[p], 16 * (idx // 2 + 1))
                if wd_mode:
                    if idx > 2 * NIC:
                        eng.wait_ge(dWDS, 16 * (idx - 2 * NIC))
                elif idx >= 2:
                    eng.wait_ge(dWQS[idx % 2], 16 * (idx // 2))
                eng.reduce_max(scr[:, 12:13], wfv[:, 0:wid], axis=AX,
                               apply_absolute_value=True)
                spacer(eng)
                eng.reciprocal(scr[:, 13:14], scr[:, 12:13])
                if idx < 2 * NIC:
                    dst = (sw_g if idx % 2 == 0 else sw_u)[:, idx // 2:idx // 2 + 1]
                else:
                    dst = sw_d[:, idx - 2 * NIC:idx - 2 * NIC + 1]
                eng.tensor_scalar(dst, scr[:, 12:13], 1.0 / 127.0, None, ALU.mult)
                eng.tensor_scalar(wfv[:, 0:wid], wfv[:, 0:wid],
                                  scr[:, 13:14], 127.0, ALU.mult, ALU.mult)
                eng.tensor_scalar(buf[:, 0:wid], wfv[:, 0:wid],
                                  RND_C, RND_C, ALU.add, ALU.subtract).then_inc(vWq, 1)
                if idx < 2 * NIC and idx % 2 == 1:
                    ic2 = idx // 2
                    eng.tensor_scalar(scg[:, ic2:ic2 + 1], sw_g[:, ic2:ic2 + 1],
                                      sc_x[:, 1:2], None, ALU.mult)
                    eng.tensor_scalar(scu[:, ic2:ic2 + 1], sw_u[:, ic2:ic2 + 1],
                                      sc_x[:, 1:2], None, ALU.mult)
                    eng.tensor_copy(scr[:, 30:31], scr[:, 31:32]).then_inc(vCg, 1)
            if SA < 9:
                return
            eng.wait_ge(vSc2, 1)
            eng.tensor_scalar(scg[:, :], sw_g[:, :], sc_x[:, 1:2], None, ALU.mult)
            eng.tensor_scalar(scu[:, :], sw_u[:, :], sc_x[:, 1:2], None,
                              ALU.mult).then_inc(vCg, 1)
            for u in range(NIC * NTG):
                ic = u // NTG
                B = u % NTG
                sp = u % 4
                hp_ = u % 2
                eng.wait_ge(vCg, ic + 1)
                eng.wait_ge(sSil, u + 1)
                if u >= 2:
                    eng.wait_ge(dHF[hp_], 16 * (u // 2))
                ubank = pbank[1 + 2 * (u % 2)]
                eng.scalar_tensor_tensor(hfb[hp_][:, 0:512], ubank[:, :],
                                         scu[:, ic:ic + 1], sgf[sp][:, 0:512],
                                         ALU.mult, ALU.mult)
                eng.reduce_max(mxsl[:, u:u + 1], hfb[hp_][:, 0:512], axis=AX,
                               apply_absolute_value=True).then_inc(vFF, 1)
            spacer(eng)
            eng.reduce_max(mxa[:, 1:2], mxsl[:, 0:NIC * NTG], axis=AX)
            spacer(eng)
            eng.tensor_copy(scr[:, 30:31], scr[:, 31:32]).then_inc(vFF, 1)
            if SA < 10:
                return
            eng.wait_ge(dMX, 64)
            eng.reduce_max(rowb[0:1, 135:136], rowb[0:1, 0:128], axis=AX)
            spacer(eng)
            eng.tensor_tensor(rowb[0:1, 136:137], rowb[0:1, 135:136], clp[0:1, 3:4], ALU.mult)
            spacer(eng)
            eng.reciprocal(rowb[0:1, 137:138], rowb[0:1, 136:137])
            eng.tensor_scalar(rowb[0:1, 131:132], rowb[0:1, 136:137], 1.0 / 127.0, None,
                              ALU.mult)
            eng.tensor_scalar(rowb[0:1, 130:131], rowb[0:1, 137:138], 127.0, None, ALU.mult)
            spacer(eng)
            eng.tensor_copy(scr[:, 30:31], scr[:, 31:32]).then_inc(vSc3, 1)
            eng.wait_ge(tSc3, 2)
            eng.tensor_copy(sc_x[:, 2:4], pM[:, 0:2])
            spacer(eng)
            eng.tensor_scalar(scr[:, 16:24], sw_d[:, :], sc_x[:, 3:4], None, ALU.mult)
            spacer(eng)
            eng.tensor_copy(scr[:, 30:31], scr[:, 31:32]).then_inc(vSc4, 1)
            eng.tensor_copy(scr[:, 31:32], scr[:, 30:31]).then_inc(vSwd, 1)
            eng.wait_ge(tSw, 1)
            eng.tensor_copy(swd_bc[:, :], pM[:, :]).then_inc(vSw, 1)
            if SA < 11:
                return
            eng.wait_ge(vSw, 1)
            for B in range(NTG + 1):
                if B < NTG:
                    hsb = hint_sb[B % 2]
                    if B >= 2:
                        eng.wait_ge(tDn, 4 * (B - 1))
                    for ic in range(NIC):
                        i = B * NIC + ic
                        p = i % 6
                        eng.wait_ge(dHFL6[p], 16 * (i // 6 + 1))
                        eng.tensor_scalar(xlnscr[:, 0:512], hfb6[p][:, 0:512],
                                          sc_x[:, 2:3], RND_C, ALU.mult, ALU.add)
                        eng.tensor_scalar(xlnscr[:, 0:512], xlnscr[:, 0:512],
                                          RND_C, 127.0, ALU.subtract, ALU.min)
                        eng.tensor_scalar(hsb[ic][:, 0:512], xlnscr[:, 0:512],
                                          -128.0, None, ALU.max).then_inc(vHq, 1)
                if B >= 1:
                    for tci in range(4):
                        tc = (B - 1) * 4 + tci
                        p = tc % 2
                        eng.wait_ge(tDn, tc + 1)
                        eng.wait_ge(dX1L[p], 16 * (16 + tc // 2 + 1))
                        if tc >= 2:
                            eng.wait_ge(dOUT[p], 16 * (tc // 2))
                        for oh in range(2):
                            grp = tc * 2 + oh
                            bk_ = grp % 4
                            eng.tensor_tensor(xpre_b[p][:, oh * 512:(oh + 1) * 512],
                                              pbank[bk_][:, :],
                                              swd_bc[:, oh * 512:(oh + 1) * 512], ALU.mult)
                        eng.tensor_add(xpre_b[p], xpre_b[p], x1_b[p]).then_inc(vX2, 1)

        @block.scalar
        def _(eng):
            for t in range(NTT):
                eng.wait_ge(vVar, t + 1)
                eng.activation(scr[:, 7:8], scr[:, 6:7], AF.Sqrt)
                eng.copy(scr[:, 28:29], scr[:, 29:30]).then_inc(sSqrt, 1)
            if SA < 2:
                return
            for idx in range(NTG * 24):
                sl = idx % 4
                eng.wait_ge(tMM, idx + 1)
                if idx >= 4:
                    eng.wait_ge(vBn4[sl], idx // 4)
                    eng.wait_ge(dH4[sl], 16 * (idx // 4))
                eng.copy(hb4[sl][:, 0:512], pbank[sl][:, :]).then_inc(sCp4[sl], 1)
            if SA < 3:
                return
            eng.wait_ge(vSt2, 1)
            eng.activation(scr2[:, 40:64], scr2[:, 40:64], AF.Sqrt)
            eng.copy(scr[:, 28:29], scr[:, 29:30]).then_inc(sSq3, 1)
            if SA < 4:
                return
            eng.wait_ge(vAB, 1)
            for i in range(NTG * NEC):
                ec = i % NEC
                p = i % 2
                eng.wait_ge(dHL[p], 48 * (i // 2 + 1))
                if i >= 1:
                    eng.wait_ge(vQk, i)
                eng.activation(qb_b, hq_b[p][:, 0:512], AF.Sigmoid,
                               bias=bbn[:, ec:ec + 1], scale=abn[:, ec:ec + 1])
                eng.activation(kb_b, hk_b[p][:, 0:512], AF.Sigmoid,
                               bias=bbn[:, 8 + ec:8 + ec + 1],
                               scale=abn[:, 8 + ec:8 + ec + 1])
                if i >= 2:
                    eng.wait_ge(dVS[p], 16 * (i // 2))
                eng.activation(vb_b[p], hv_b[p][:, 0:512], AF.Sigmoid,
                               bias=bbn[:, 16 + ec:16 + ec + 1],
                               scale=abn[:, 16 + ec:16 + ec + 1]).then_inc(sSig, 3)
                if False:
                        _g, _ec = (i + 2) // NEC, (i + 2) % NEC
                        _p = (i + 2) % 2
                        eng.dma_start(out=hq_b[_p][:, 0:512],
                                      in_=h_s[0, _ec * 128:(_ec + 1) * 128,
                                                  _g * 512:(_g + 1) * 512]).then_inc(dHL[_p], 16)
                        eng.dma_start(out=hk_b[_p][:, 0:512],
                                      in_=h_s[1, _ec * 128:(_ec + 1) * 128,
                                                  _g * 512:(_g + 1) * 512]).then_inc(dHL[_p], 16)
                        eng.dma_start(out=hv_b[_p][:, 0:512],
                                      in_=h_s[2, _ec * 128:(_ec + 1) * 128,
                                                  _g * 512:(_g + 1) * 512]).then_inc(dHL[_p], 16)
            if SA < 6:
                return
            for tc in range(NTT):
                eng.wait_ge(vVar6, NTT + tc + 1)
                eng.activation(scr[:, 7:8], scr[:, 6:7], AF.Sqrt)
                eng.copy(scr[:, 28:29], scr[:, 29:30]).then_inc(sSqrt6, 1)
            if SA < 8:
                return
            eng.wait_ge(dPrm, N_PRM)
            eng.activation(clp[0:1, 2:3], clp[0:1, 0:1], AF.Sigmoid)
            eng.activation(clp[0:1, 3:4], clp[0:1, 1:2], AF.Sigmoid)
            eng.copy(scr[:, 28:29], scr[:, 29:30]).then_inc(sClp, 2)
            if SA < 9:
                return
            for u in range(NIC * NTG):
                ic = u // NTG
                sp = u % 4
                eng.wait_ge(vCg, ic + 1)
                eng.wait_ge(tFF, u + 1)
                if u >= 4:
                    eng.wait_ge(vFF, u - 3)
                gbank = pbank[0 + 2 * (u % 2)]
                eng.activation(sgf[sp][:, 0:512], gbank[:, :], AF.Silu,
                               scale=scg[:, ic:ic + 1]).then_inc(sSil, 1)

        @block.gpsimd
        def _(eng):
            for t in range(NTT):
                p = t % 2
                eng.wait_ge(vApp, t + 1)
                if t >= 4:
                    eng.wait_ge(dXLN4[t % 4], 16 * (t // 4))
                eng.dma_start(out=xln_s[t * 128:(t + 1) * 128, :],
                              in_=xln_b[p]).then_inc(dXLN4[t % 4], 16)
                for idx in (2 * t, 2 * t + 1):
                    if idx < 4 * NEC:
                        pw = idx % 2
                        eng.wait_ge(vCast, idx + 1)
                        w_i, ch = idx // NEC, idx % NEC
                        eng.dma_start(out=wbf_s[w_i, ch * 128:(ch + 1) * 128, :],
                                      in_=wb_b[pw]).then_inc(dWS[pw], 16)
            if SA < 2:
                return
            for idx in range(NTG * 24):
                sl = idx % 4
                g, r = idx // 24, idx % 24
                br, ec = r // 8, r % 8
                eng.wait_ge(sCp4[sl], idx // 4 + 1)
                eng.dma_start(out=h_s[br, ec * 128:(ec + 1) * 128, g * 512:(g + 1) * 512],
                              in_=hb4[sl][:, 0:512]).then_inc(dH4[sl], 16)
            if SA < 3:
                return
            eng.wait_ge(vSt, 1)
            eng.dma_start(out=st_i[:, :], in_=stat[:, 0:48]).then_inc(dST, 16)
            eng.wait_ge(dST, 16)
            eng.collective_compute("AllReduce", ALU.add,
                                   replica_groups=[list(range(NC_N))],
                                   ins=[st_i.ap()], outs=[st_o.ap()]).then_inc(cCC, 1)
            eng.wait_ge(cCC, 1)
            eng.dma_start(out=gst[:, 0:48], in_=st_o[:, :]).then_inc(dST, 16)
            if SA < 4:
                return
            for i in range(NTG * NEC):
                g, ec = i // NEC, i % NEC
                p = i % 2
                eng.wait_ge(sSig, 3 * (i + 1))
                eng.dma_start(out=v_s[ec * 128:(ec + 1) * 128, g * 512:(g + 1) * 512],
                              in_=vb_b[p][:, 0:512]).then_inc(dVS[p], 16)
            if SA < 6:
                return
            eng.wait_ge(vBp, 1)
            for tc in range(NTT):
                p = tc % 2
                eng.wait_ge(dX6[p], 16 * (16 + tc // 2 + 1))
                if tc >= 2:
                    eng.wait_ge(vX1, tc - 1)
                eng.tensor_tensor(xpre_b[p], xb[p], bp_bc[:, :], ALU.add).then_inc(gXp, 1)
                eng.wait_ge(vX1, tc + 1)
                eng.wait_ge(vVar6, NTT + tc + 1)
                eng.dma_start(out=x1_s[tc * 128:(tc + 1) * 128, :],
                              in_=x1_b[p]).then_inc(dX1S[p], 16)
            if SA < 8:
                return
            eng.wait_ge(vMx, NTT + 1)
            eng.dma_start(out=mx_i.ap(), in_=mxa[:, 0:1]).then_inc(dMX, 16)
            eng.wait_ge(dMX, 16)
            eng.collective_compute("AllReduce", ALU.max,
                                   replica_groups=[list(range(NC_N))],
                                   ins=[mx_i.ap()], outs=[mx_o.ap()]).then_inc(cCC, 1)
            eng.wait_ge(cCC, 2)
            eng.dma_start(out=rowb[0:1, 0:128], in_=mx_o.ap()).then_inc(dMX, 16)
            for tc in range(NTT):
                p = tc % 2
                eng.wait_ge(vXq, tc + 1)
                eng.dma_start(out=xq_s[tc * 128:(tc + 1) * 128, :],
                              in_=xq_b[p]).then_inc(dXQS[p], 16)
            for idx in range(2 * NIC + NEC):
                p = idx % 2
                eng.wait_ge(vWq, idx + 1)
                if idx < 2 * NIC:
                    ic2 = idx // 2
                    tgt = wgq_s if idx % 2 == 0 else wuq_s
                    eng.dma_start(out=tgt[ic2 * 128:(ic2 + 1) * 128, :],
                                  in_=wqg_b[p][:, 0:1024]).then_inc(dWQS[p], 16)
                elif False:
                    pass
                else:
                    ch = idx - 2 * NIC
                    eng.dma_start(out=wdq_s[ch * 128:(ch + 1) * 128, :],
                                  in_=wqd_b[:, 0:2816]).then_inc(dWDS, 16)
            if SA < 9:
                return
            for u in range(NIC * NTG):
                ic, B = u // NTG, u % NTG
                hp_ = u % 2
                eng.wait_ge(vFF, u + 1)
                eng.dma_start(out=hf_s[ic * 128:(ic + 1) * 128, B * 512:(B + 1) * 512],
                              in_=hfb[hp_][:, 0:512]).then_inc(dHF[hp_], 16)
            if SA < 10:
                return
            eng.wait_ge(vFF, NIC * NTG + 1)
            eng.dma_start(out=mh_i.ap(), in_=mxa[:, 1:2]).then_inc(dMX, 16)
            eng.wait_ge(dMX, 48)
            eng.collective_compute("AllReduce", ALU.max,
                                   replica_groups=[list(range(NC_N))],
                                   ins=[mh_i.ap()], outs=[mh_o.ap()]).then_inc(cCC, 1)
            eng.wait_ge(cCC, 3)
            eng.dma_start(out=rowb[0:1, 0:128], in_=mh_o.ap()).then_inc(dMX, 16)
            eng.wait_ge(vSwd, 1)
            for hc in range(NEC):
                eng.dma_start(out=swd_d[hc * 128:(hc + 1) * 128],
                              in_=scr[:, 16 + hc:17 + hc]).then_inc(dSW, 16)
            if SA < 11:
                return
            for tc in range(NTT):
                p = tc % 2
                eng.wait_ge(vX2, tc + 1)
                eng.dma_start(out=out_t[tc * 128:(tc + 1) * 128, :],
                              in_=xpre_b[p]).then_inc(dOUT[p], 16)

    es.close()
    return nc


_NC_CACHE = {}


def _get_nc():
    if "nc" not in _NC_CACHE:
        _NC_CACHE["nc"] = build_nc()
    return _NC_CACHE["nc"]


def kernel(L0_attn_x, L0_attn_Wq, L0_attn_gq, L0_attn_bq, L0_attn_Wk, L0_attn_gk,
           L0_attn_bk, L0_attn_Wv, L0_attn_gv, L0_attn_bv, L0_attn_Wp, L0_attn_bp,
           L0_ffn_clip_x, L0_ffn_clip_down, L0_ffn_w_gate, L0_ffn_w_up, L0_ffn_w_down):
    nc = _get_nc()
    f32 = np.float32
    common = {
        "Wq": np.ascontiguousarray(L0_attn_Wq, f32),
        "Wk": np.ascontiguousarray(L0_attn_Wk, f32),
        "Wv": np.ascontiguousarray(L0_attn_Wv, f32),
        "Wp": np.ascontiguousarray(L0_attn_Wp, f32),
        "gq": np.ascontiguousarray(L0_attn_gq, f32),
        "bq": np.ascontiguousarray(L0_attn_bq, f32),
        "gk": np.ascontiguousarray(L0_attn_gk, f32),
        "bk": np.ascontiguousarray(L0_attn_bk, f32),
        "gv": np.ascontiguousarray(L0_attn_gv, f32),
        "bv": np.ascontiguousarray(L0_attn_bv, f32),
        "bp": np.ascontiguousarray(L0_attn_bp, f32),
        "clip_x": np.ascontiguousarray(L0_ffn_clip_x, f32),
        "clip_dn": np.ascontiguousarray(L0_ffn_clip_down, f32),
        "w_gate": np.ascontiguousarray(L0_ffn_w_gate, f32),
        "w_up": np.ascontiguousarray(L0_ffn_w_up, f32),
        "w_down": np.ascontiguousarray(L0_ffn_w_down, f32),
        "ones_r": np.ones((1, 128), f32),
    }
    xs = np.ascontiguousarray(L0_attn_x, f32)
    in_maps = [dict(common, x=xs[b]) for b in range(NC_N)]
    res = run_bass_kernel_spmd(nc, in_maps, core_ids=list(range(NC_N)))
    return np.stack([np.asarray(r["out"]) for r in res.results], axis=0)


# version 11
# speedup vs baseline: 1.1289x; 1.0089x over previous
import numpy as np
import ml_dtypes
from contextlib import ExitStack

import concourse.bass as bass
import concourse.mybir as mybir
from concourse.bass_utils import run_bass_kernel_spmd

F32 = mybir.dt.float32
BF16 = mybir.dt.bfloat16
ALU = mybir.AluOpType
AF = mybir.ActivationFunctionType
AX = mybir.AxisListType.X

RND_C = float(1.5 * 2.0 ** 23)
D = 1024
NT = 4096
DI = 2816
NEC = 8
NIC = 22
NTT = 32
NTG = 8
LN_EPS = 1e-5
BN_EPS = 1e-5
NC_N = 8


class Arena:
    def __init__(self, tensor, cols):
        self.t, self.cols, self.off, self.marks = tensor, cols, 0, []

    def alloc(self, n):
        n = (n + 15) // 16 * 16
        assert self.off + n <= self.cols, f"arena overflow {self.off}+{n}>{self.cols}"
        o = self.off
        self.off += n
        return self.t[:, o:o + n]

    def push(self):
        self.marks.append(self.off)

    def pop(self):
        self.off = self.marks.pop()


def build_nc(stop_after=99, debug=False):
    nc = bass.Bass(target_bir_lowering=False, detect_race_conditions=False)
    x_in = nc.dram_tensor("x", [NT, D], F32, kind="ExternalInput")
    Wq = nc.dram_tensor("Wq", [D, D], F32, kind="ExternalInput")
    Wk = nc.dram_tensor("Wk", [D, D], F32, kind="ExternalInput")
    Wv = nc.dram_tensor("Wv", [D, D], F32, kind="ExternalInput")
    Wp = nc.dram_tensor("Wp", [D, D], F32, kind="ExternalInput")
    gq = nc.dram_tensor("gq", [D], F32, kind="ExternalInput")
    bq = nc.dram_tensor("bq", [D], F32, kind="ExternalInput")
    gk = nc.dram_tensor("gk", [D], F32, kind="ExternalInput")
    bk = nc.dram_tensor("bk", [D], F32, kind="ExternalInput")
    gv = nc.dram_tensor("gv", [D], F32, kind="ExternalInput")
    bv = nc.dram_tensor("bv", [D], F32, kind="ExternalInput")
    bp = nc.dram_tensor("bp", [D], F32, kind="ExternalInput")
    clip_x = nc.dram_tensor("clip_x", [1], F32, kind="ExternalInput")
    clip_dn = nc.dram_tensor("clip_dn", [1], F32, kind="ExternalInput")
    w_gate = nc.dram_tensor("w_gate", [DI, D], F32, kind="ExternalInput")
    w_up = nc.dram_tensor("w_up", [DI, D], F32, kind="ExternalInput")
    w_down = nc.dram_tensor("w_down", [D, DI], F32, kind="ExternalInput")
    ones_r = nc.dram_tensor("ones_r", [1, 128], F32, kind="ExternalInput")
    out_t = nc.dram_tensor("out", [NT, D], F32, kind="ExternalOutput")

    sk = "ExternalOutput" if debug else "Internal"
    xln_s = nc.dram_tensor("xln_s", [NT, D], BF16, kind=sk)
    h_s = nc.dram_tensor("h_s", [3, D, NT], F32, kind=sk)
    v_s = nc.dram_tensor("v_s", [D, NT], BF16, kind="Internal")
    x1_s = nc.dram_tensor("x1_s", [NT, D], F32, kind=sk)
    xq_s = nc.dram_tensor("xq_s", [NT, D], BF16, kind=sk)
    hf_s = nc.dram_tensor("hf_s", [DI, NT], F32, kind=sk)
    wbf_s = nc.dram_tensor("wbf_s", [4, D, D], BF16, kind="Internal")
    wgq_s = nc.dram_tensor("wgq_s", [DI, D], BF16, kind=sk)
    wuq_s = nc.dram_tensor("wuq_s", [DI, D], BF16, kind="Internal")
    wdq_s = nc.dram_tensor("wdq_s", [D, DI], BF16, kind="Internal")
    st_i = nc.dram_tensor("st_i", [128, 48], F32, kind="Internal")
    st_o = nc.dram_tensor("st_o", [128, 48], F32, kind="Internal")
    mx_i = nc.dram_tensor("mx_i", [128], F32, kind="Internal")
    mx_o = nc.dram_tensor("mx_o", [128], F32, kind="Internal")
    mh_i = nc.dram_tensor("mh_i", [128], F32, kind="Internal")
    mh_o = nc.dram_tensor("mh_o", [128], F32, kind="Internal")
    swd_d = nc.dram_tensor("swd_d", [D], F32, kind="Internal")

    es = ExitStack()
    sbuf = lambda name, shape, dt: es.enter_context(nc.sbuf_tensor(name, shape, dt))
    psum = lambda name, shape: es.enter_context(nc.psum_tensor(name, shape, F32))
    sem = lambda name: es.enter_context(nc.semaphore(name))
    sem2 = lambda name: [es.enter_context(nc.semaphore(name + "0")),
                         es.enter_context(nc.semaphore(name + "1"))]

    A16 = Arena(sbuf("A16", [128, 61440], BF16), 61440)
    A32 = Arena(sbuf("A32", [128, 13312], F32), 13312)

    gam = sbuf("gam", [128, 24], F32)
    bet = sbuf("bet", [128, 24], F32)
    abn = sbuf("abn", [128, 24], F32)
    bbn = sbuf("bbn", [128, 24], F32)
    bnsl = sbuf("bnsl", [128, 24 * 64], F32)
    stat = sbuf("stat", [128, 64], F32)
    gst = sbuf("gst", [128, 64], F32)
    scr = sbuf("scr", [128, 64], F32)
    scr2 = sbuf("scr2", [128, 64], F32)
    ln2m = sbuf("ln2m", [128, NTT], F32)
    ln2r = sbuf("ln2r", [128, NTT], F32)
    mxa = sbuf("mxa", [128, 4], F32)
    onesb = sbuf("onesb", [1, 128], F32)
    rowb = sbuf("rowb", [1, 144], F32)
    sc_x = sbuf("sc_x", [128, 4], F32)
    bp_row = sbuf("bp_row", [1, D], F32)
    bp_bc = sbuf("bp_bc", [128, D], F32)
    swd_row = sbuf("swd_row", [1, D], F32)
    swd_bc = sbuf("swd_bc", [128, D], F32)
    sw_g = sbuf("sw_g", [128, NIC], F32)
    sw_u = sbuf("sw_u", [128, NIC], F32)
    sw_d = sbuf("sw_d", [128, NEC], F32)
    scg = sbuf("scg", [128, NIC], F32)
    scu = sbuf("scu", [128, NIC], F32)
    clp = sbuf("clp", [1, 16], F32)
    mxsl = sbuf("mxsl", [128, 192], F32)

    pA = psum("pA", [128, 512])
    pB = psum("pB", [128, 512])
    pC = psum("pC", [128, 512])
    pD = psum("pD", [128, 512])
    pG = psum("pG", [128, 1024])
    pM = psum("pM", [128, 1024])
    pbank = [pA, pB, pC, pD]

    dX = sem2("dX")
    dW = sem2("dW")
    dHL = sem2("dHL")
    dVT = sem2("dVT")
    dX1S = sem2("dX1S")
    dX1L = sem2("dX1L")
    dXQS = sem2("dXQS")
    dWF = sem2("dWF")
    dWGT = sem2("dWGT")
    dHF = sem2("dHF")
    dOUT = sem2("dOUT")
    dVS = sem2("dVS")
    dWS = sem2("dWS");  dWQS = sem2("dWQS")
    dPrm = sem("dPrm");  dWT = sem("dWT");  dWPT = sem("dWPT")
    dXLN4 = [sem(f"dXLN4_{i}") for i in range(4)]
    dXLT = sem("dXLT");  dST = sem("dST");  dXQT = sem("dXQT")
    dWDS = sem("dWDS");  dWDT = sem("dWDT");  dMX = sem("dMX")
    dSW = sem("dSW");  dSWR = dXQT
    cCC = sem("cCC")
    vVar = sem("vVar");  sSqrt = sem("sSqrt");  vApp = sem("vApp")
    vCast = sem("vCast")
    tMM = sem("tMM");  _unused = None
    sCp4 = [sem(f"sCp4_{i}") for i in range(4)]
    vBn4 = [sem(f"vBn4_{i}") for i in range(4)]
    dH4 = [sem(f"dH4_{i}") for i in range(4)]
    dHFL6 = [sem(f"dHFL6_{i}") for i in range(6)]
    vSt = sem("vSt");  vSt2 = sem("vSt2");  sSq3 = sem("sSq3");  vAB = sem("vAB")
    sSig = sem("sSig");  vQk = sem("vQk");  tMM4 = sem("tMM4")
    vVV = sem("vVV");  tM5 = sem("tM5");  vM5 = sem("vM5");  tBp = sem("tBp");  vBp = sem("vBp")
    gXp = sem("gXp");  tP6 = sem("tP6");  vX1 = sem("vX1")
    vMx = sem("vMx")
    sClp = sem("sClp");  vSc = sem("vSc");  tSc = sem("tSc");  vSc2 = sem("vSc2")
    vXq = sem("vXq");  vWq = sem("vWq");  vCg = sem("vCg")
    tFF = sem("tFF");  sSil = sem("sSil");  vFF = sem("vFF")
    vSc4 = sem("vSc4");  vSwd = sem("vSwd")
    tSw = sem("tSw");  vSw = sem("vSw")
    vHq = sem("vHq");  tDn = sem("tDn");  vX2 = sem("vX2")
    dX6 = dX
    vVar6 = vVar
    sSqrt6 = sSqrt
    vSc3 = vSc
    tSc3 = tSc

    xb0_off = A32.off
    xb = [A32.alloc(1024), A32.alloc(1024)]
    sqscr = A32.alloc(1024)
    _r0 = A32.off
    hb = [A32.alloc(512), A32.alloc(512)]
    hq_b = [A32.alloc(512), A32.alloc(512)]
    hk_b = [A32.alloc(512), A32.alloc(512)]
    hv_b = [A32.alloc(512), A32.alloc(512)]
    xpre_b = [A32.alloc(1024), A32.alloc(1024)]
    assert A32.off - _r0 >= 2 * 2816
    x1_b = [A32.alloc(1024), A32.alloc(1024)]
    xlnscr = A32.alloc(1024)
    wf_b = [x1_b[0], x1_b[1]]
    sgf = [hq_b[0], hq_b[1], hk_b[0], hk_b[1]]
    hb4 = [hb[0], hb[1], hq_b[0], hq_b[1]]
    hfb6 = [hb[0], hb[1], hq_b[0], hq_b[1], hk_b[0], hk_b[1]]
    hfb = [hv_b[0], hv_b[1]]

    wf_g = xb[0]
    wf_u = xb[1]
    wf_d = A32.t[:, xb0_off:xb0_off + 2816]
    wpT_sb = A16.alloc(8 * 1024)
    A16.push()
    wt_sb = [A16.alloc(64 * 128) for _ in range(3)]
    xlt_b2 = [A16.alloc(8 * 512), A16.alloc(8 * 512)]
    xlt_b = xlt_b2[0]
    xln_b = [A16.alloc(1024), A16.alloc(1024)]
    wb_b = [A16.alloc(1024), A16.alloc(1024)]
    A16.pop()
    A16.push()
    qk_res = [A16.alloc(4096) for _ in range(8)]
    qb_b = A16.alloc(512)
    kb_b = A16.alloc(512)
    vb_b = [A16.alloc(512), A16.alloc(512)]
    vt_b = [A16.alloc(1024), A16.alloc(1024)]
    vv_sb = A16.alloc(1024)
    M_sb = [A16.alloc(1024) for _ in range(8)]
    xq_b = [A16.alloc(1024), A16.alloc(1024)]
    wqg_b = [A16.alloc(1024), A16.alloc(1024)]
    wqd_b = A16.alloc(2816)
    A16.pop()
    A16.push()
    xqT_sb = [A16.alloc(4096) for _ in range(8)]
    wgT_b = [A16.alloc(8 * 128), A16.alloc(8 * 128)]
    wuT_b = [A16.alloc(8 * 128), A16.alloc(8 * 128)]
    A16.pop()
    A16.push()
    wdT_sb = A16.alloc(44 * 512)
    hint_sb = [[A16.alloc(512) for _ in range(NIC)], [A16.alloc(512) for _ in range(NIC)]]
    A16.pop()

    SA = stop_after
    N_PRM = 16 * (6 * NEC + 4)


    def spacer(eng):
        eng.tensor_copy(scr[:, 30:31], scr[:, 31:32])

    def emit_ln_stats(eng, xsrc, mean_slot):
        eng.bn_stats(scr[:, 32:38], xsrc[:, 0:512])
        eng.bn_stats(scr[:, 40:46], xsrc[:, 512:1024])
        eng.tensor_add(scr[:, 48:49], scr[:, 33:34], scr[:, 36:37])
        eng.tensor_add(scr[:, 49:50], scr[:, 34:35], scr[:, 37:38])
        eng.tensor_tensor(scr[:, 50:51], scr[:, 33:34], scr[:, 33:34], ALU.mult)
        eng.tensor_tensor(scr[:, 51:52], scr[:, 36:37], scr[:, 36:37], ALU.mult)
        eng.tensor_add(scr[:, 52:53], scr[:, 41:42], scr[:, 44:45])
        eng.tensor_add(scr[:, 53:54], scr[:, 42:43], scr[:, 45:46])
        eng.tensor_tensor(scr[:, 54:55], scr[:, 41:42], scr[:, 41:42], ALU.mult)
        eng.tensor_tensor(scr[:, 55:56], scr[:, 44:45], scr[:, 44:45], ALU.mult)
        eng.tensor_add(scr[:, 56:57], scr[:, 48:49], scr[:, 52:53])
        eng.tensor_add(scr[:, 57:58], scr[:, 50:51], scr[:, 51:52])
        eng.tensor_add(scr[:, 58:59], scr[:, 54:55], scr[:, 55:56])
        eng.tensor_add(scr[:, 59:60], scr[:, 49:50], scr[:, 53:54])
        eng.tensor_scalar(mean_slot, scr[:, 56:57], 0.25, None, ALU.mult)
        eng.tensor_add(scr[:, 60:61], scr[:, 57:58], scr[:, 58:59])
        eng.tensor_tensor(scr[:, 61:62], mean_slot, mean_slot, ALU.mult)
        eng.scalar_tensor_tensor(scr[:, 62:63], scr[:, 60:61], 256.0,
                                 scr[:, 59:60], ALU.mult, ALU.add)
        spacer(eng)
        eng.tensor_scalar(scr[:, 63:64], scr[:, 62:63], 1.0 / D, None, ALU.mult)
        spacer(eng)
        eng.scalar_tensor_tensor(scr[:, 6:7], scr[:, 63:64], 1.0,
                                 scr[:, 61:62], ALU.bypass, ALU.subtract)
        eng.tensor_scalar(scr[:, 6:7], scr[:, 6:7], LN_EPS, None, ALU.add) if False else None
        spacer(eng)
        eng.tensor_scalar(scr[:, 6:7], scr[:, 6:7], LN_EPS, None, ALU.add)

    with nc.Block() as block:

        @block.sync
        def _(eng):
            for i, vec in enumerate([gq, gk, gv]):
                for ec in range(NEC):
                    eng.dma_start(out=gam[:, i * 8 + ec:i * 8 + ec + 1],
                                  in_=vec[ec * 128:(ec + 1) * 128]).then_inc(dPrm, 16)
            for i, vec in enumerate([bq, bk, bv]):
                for ec in range(NEC):
                    eng.dma_start(out=bet[:, i * 8 + ec:i * 8 + ec + 1],
                                  in_=vec[ec * 128:(ec + 1) * 128]).then_inc(dPrm, 16)
            eng.dma_start(out=bp_row[:, :], in_=bp.ap()).then_inc(dPrm, 16)
            eng.dma_start(out=onesb[:, :], in_=ones_r[:, :]).then_inc(dPrm, 16)
            eng.dma_start(out=clp[0:1, 0:1], in_=clip_x.ap()).then_inc(dPrm, 16)
            eng.dma_start(out=clp[0:1, 1:2], in_=clip_dn.ap()).then_inc(dPrm, 16)
            for t in range(NTT):
                p = t % 2
                if t >= 2:
                    eng.wait_ge(vApp, t - 1)
                eng.dma_start(out=xb[p], in_=x_in[t * 128:(t + 1) * 128, :]).then_inc(dX[p], 16)
                for idx in (2 * t, 2 * t + 1):
                    if idx < 4 * NEC:
                        pw = idx % 2
                        if idx >= 2:
                            eng.wait_ge(vCast, idx - 1)
                        W = [Wq, Wk, Wv, Wp][idx // NEC]
                        ch = idx % NEC
                        eng.dma_start(out=wf_b[pw][:, 0:1024],
                                      in_=W[ch * 128:(ch + 1) * 128, :]).then_inc(dW[pw], 16)
            eng.wait_ge(dWS[0], 16 * 12)
            eng.wait_ge(dWS[1], 16 * 12)
            for br in range(3):
                for dc in range(NEC):
                    for ec in range(NEC):
                        eng.dma_start(
                            out=wt_sb[br][:, (dc * 8 + ec) * 128:(dc * 8 + ec + 1) * 128],
                            in_=wbf_s[br, ec * 128:(ec + 1) * 128, dc * 128:(dc + 1) * 128],
                            transpose=True).then_inc(dWT, 16)
            eng.wait_ge(dWS[0], 16 * 16)
            eng.wait_ge(dWS[1], 16 * 16)
            for ecp in range(NEC):
                eng.dma_start(out=wpT_sb[:, ecp * 1024:(ecp + 1) * 1024],
                              in_=wbf_s[3, 0:D, ecp * 128:(ecp + 1) * 128],
                              transpose=True).then_inc(dWPT, 16)
            if SA < 2:
                return
            for g in range(NTG):
                for q4 in range(4):
                    eng.wait_ge(dXLN4[q4], 16 * (g + 1))
                if g >= 2:
                    eng.wait_ge(tMM, 24 * (g - 1))
                for dc in range(NEC):
                    eng.dma_start(out=xlt_b2[g % 2][:, dc * 512:(dc + 1) * 512],
                                  in_=xln_s[g * 512:(g + 1) * 512, dc * 128:(dc + 1) * 128],
                                  transpose=True).then_inc(dVT[g % 2], 16)
            if SA < 4:
                return
            eng.wait_ge(vBn4[2], 48)
            eng.wait_ge(vBn4[3], 48)
            eng.wait_ge(dH4[2], 16 * 48)
            eng.wait_ge(dH4[3], 16 * 48)
            for g in range(NTG):
                for ec in range(NEC):
                    i = g * NEC + ec
                    p = i % 2
                    if i >= 2:
                        eng.wait_ge(sSig, 3 * (i - 1))
                    eng.dma_start(out=hq_b[p][:, 0:512],
                                  in_=h_s[0, ec * 128:(ec + 1) * 128,
                                          g * 512:(g + 1) * 512]).then_inc(dHL[p], 16)
                    eng.dma_start(out=hk_b[p][:, 0:512],
                                  in_=h_s[1, ec * 128:(ec + 1) * 128,
                                          g * 512:(g + 1) * 512]).then_inc(dHL[p], 16)
                    eng.dma_start(out=hv_b[p][:, 0:512],
                                  in_=h_s[2, ec * 128:(ec + 1) * 128,
                                          g * 512:(g + 1) * 512]).then_inc(dHL[p], 16)
                eng.wait_ge(dVS[0], 16 * 4 * (g + 1))
                eng.wait_ge(dVS[1], 16 * 4 * (g + 1))
                for tci in range(4):
                    tc = g * 4 + tci
                    p = tc % 2
                    if tc >= 2:
                        eng.wait_ge(tMM4, tc - 1)
                    eng.dma_start(out=vt_b[p][:, 0:1024],
                                  in_=v_s[0:D, tc * 128:(tc + 1) * 128],
                                  transpose=True).then_inc(dVT[p], 16)
            if SA < 6:
                return
            for tc in range(NTT):
                p = tc % 2
                if tc >= 2:
                    eng.wait_ge(gXp, tc - 1)
                eng.dma_start(out=xb[p], in_=x_in[tc * 128:(tc + 1) * 128, :]).then_inc(dX6[p], 16)
            if SA < 8:
                return
            eng.wait_ge(vSc2, 1)
            for tc in range(NTT):
                p = tc % 2
                if tc >= 2:
                    eng.wait_ge(vXq, tc - 1)
                eng.dma_start(out=x1_b[p],
                              in_=x1_s[tc * 128:(tc + 1) * 128, :]).then_inc(dX1L[p], 16)
            eng.wait_ge(gXp, NTT)
            for idx in range(2 * NIC + NEC):
                if idx < 2 * NIC:
                    ic2, w_i = idx // 2, idx % 2
                    p = w_i
                    if idx >= 2:
                        eng.wait_ge(vWq, idx - 1)
                    Wt_ = w_gate if w_i == 0 else w_up
                    eng.dma_start(out=(wf_g if w_i == 0 else wf_u),
                                  in_=Wt_[ic2 * 128:(ic2 + 1) * 128, :]).then_inc(dWF[p], 16)
                else:
                    ch = idx - 2 * NIC
                    p = idx % 2
                    eng.wait_ge(vWq, idx)
                    eng.dma_start(out=wf_d,
                                  in_=w_down[ch * 128:(ch + 1) * 128, :]).then_inc(dWF[p], 16)
            if SA < 9:
                return
            for g in range(NTG):
                eng.wait_ge(dXQS[0], 16 * min(2 * (g + 1), 16))
                eng.wait_ge(dXQS[1], 16 * min(2 * (g + 1), 16))
                for hc in range(NEC):
                    eng.dma_start(out=xqT_sb[hc][:, g * 512:(g + 1) * 512],
                                  in_=xq_s[g * 512:(g + 1) * 512, hc * 128:(hc + 1) * 128],
                                  transpose=True).then_inc(dXQT, 16)
            for ic in range(NIC):
                p = ic % 2
                eng.wait_ge(dWQS[0], 16 * (ic + 1))
                eng.wait_ge(dWQS[1], 16 * (ic + 1))
                if ic >= 2:
                    eng.wait_ge(tFF, NTG * (ic - 1))
                for hc in range(NEC):
                    eng.dma_start(out=wgT_b[p][:, hc * 128:(hc + 1) * 128],
                                  in_=wgq_s[ic * 128:(ic + 1) * 128,
                                            hc * 128:(hc + 1) * 128],
                                  transpose=True).then_inc(dWGT[p], 16)
                    eng.dma_start(out=wuT_b[p][:, hc * 128:(hc + 1) * 128],
                                  in_=wuq_s[ic * 128:(ic + 1) * 128,
                                            hc * 128:(hc + 1) * 128],
                                  transpose=True).then_inc(dWGT[p], 16)
            if SA < 11:
                return
            eng.wait_ge(tFF, NIC * NTG)
            eng.wait_ge(dWDS, 16 * NEC)
            for ic in range(NIC):
                for oh in range(2):
                    eng.dma_start(out=wdT_sb[:, (ic * 2 + oh) * 512:(ic * 2 + oh + 1) * 512],
                                  in_=wdq_s[oh * 512:(oh + 1) * 512,
                                            ic * 128:(ic + 1) * 128],
                                  transpose=True).then_inc(dWDT, 16)
            eng.wait_ge(dSW, 16 * NEC)
            eng.dma_start(out=swd_row[:, :], in_=swd_d.ap()).then_inc(dSWR, 16)
            eng.wait_ge(vSc4, 1)
            for B in range(NTG + 1):
                if B < NTG:
                    for ic in range(NIC):
                        i = B * NIC + ic
                        p = i % 6
                        if i >= 6:
                            eng.wait_ge(vHq, i - 5)
                        eng.dma_start(out=hfb6[p][:, 0:512],
                                      in_=hf_s[ic * 128:(ic + 1) * 128,
                                               B * 512:(B + 1) * 512]).then_inc(dHFL6[p], 16)
                if B >= 1:
                    for tci in range(4):
                        tc = (B - 1) * 4 + tci
                        p = tc % 2
                        if tc >= 2:
                            eng.wait_ge(vX2, tc - 1)
                        eng.dma_start(out=x1_b[p],
                                      in_=x1_s[tc * 128:(tc + 1) * 128, :]).then_inc(dX1L[p], 16)
            eng.wait_ge(dOUT[0], 16 * 16)
            eng.wait_ge(dOUT[1], 16 * 16)

        @block.tensor
        def _(eng):
            if SA < 2:
                return
            eng.wait_ge(dWT, 16 * 192)
            for g in range(NTG):
                eng.wait_ge(dVT[g % 2], 16 * 8 * (g // 2 + 1))
                for br in range(3):
                    for ec in range(NEC):
                        idx = g * 24 + br * 8 + ec
                        bk_ = idx % 4
                        if idx >= 4:
                            eng.wait_ge(sCp4[bk_], idx // 4)
                        for dc in range(NEC):
                            mm = eng.matmul(pbank[bk_][:, :],
                                       wt_sb[br][:, (dc * 8 + ec) * 128:(dc * 8 + ec + 1) * 128],
                                       xlt_b2[g % 2][:, dc * 512:(dc + 1) * 512],
                                       start=(dc == 0), stop=(dc == NEC - 1))
                        mm.then_inc(tMM, 1)
            if SA < 4:
                return
            for tc in range(NTT):
                p = tc % 2
                eng.wait_ge(dVT[p], 16 * (32 + tc // 2 + 1))
                for hp in range(8):
                    mm = eng.matmul(pG[:, hp * 128:(hp + 1) * 128],
                               vt_b[p][:, hp * 128:(hp + 1) * 128],
                               vt_b[p][:, hp * 128:(hp + 1) * 128],
                               start=(tc == 0 and hp in (0, 4)),
                               stop=(tc == NTT - 1),
                               skip_group_check=True)
                mm.then_inc(tMM4, 1)
            if SA < 5:
                return
            eng.wait_ge(dWPT, 16 * 8)
            eng.wait_ge(vVV, 1)
            for hp in range(8):
                pdst = pM if hp % 2 == 0 else pG
                if hp >= 2:
                    eng.wait_ge(vM5, hp - 1)
                for w in range(2):
                    po = w * 64
                    lhs = vv_sb[po:po + 64, hp * 128 + po:hp * 128 + po + 64]
                    for oh in range(2):
                        mm = eng.matmul(pdst[po:po + 64, oh * 512:(oh + 1) * 512],
                                   lhs,
                                   wpT_sb[po:po + 64, hp * 1024 + oh * 512:hp * 1024 + (oh + 1) * 512],
                                   start=True, stop=True,
                                   skip_group_check=True)
                mm.then_inc(tM5, 1)
            eng.wait_ge(dPrm, N_PRM)
            eng.wait_ge(vM5, 8)
            eng.matmul(pG[:, 0:512], onesb[0:1, :], bp_row[0:1, 0:512], start=True, stop=True)
            eng.matmul(pG[:, 512:1024], onesb[0:1, :], bp_row[0:1, 512:1024],
                       start=True, stop=True).then_inc(tBp, 1)
            if SA < 6:
                return
            eng.wait_ge(vM5, 8)
            eng.wait_ge(vQk, 64)
            for tc in range(NTT):
                b0 = (tc % 2) * 2
                if tc >= 2:
                    eng.wait_ge(vX1, tc - 1)
                for oh in range(2):
                    for cc in range(NEC):
                        mm = eng.matmul(pbank[b0 + oh][:, :],
                                   qk_res[cc][:, tc * 128:(tc + 1) * 128],
                                   M_sb[cc][:, oh * 512:(oh + 1) * 512],
                                   start=(cc == 0), stop=(cc == NEC - 1))
                mm.then_inc(tP6, 1)
            if SA < 8:
                return
            eng.wait_ge(vSc, 1)
            eng.matmul(pM[:, 0:2], onesb[0:1, :], rowb[0:1, 128:130],
                       start=True, stop=True).then_inc(tSc, 1)
            if SA < 9:
                return
            for ic in range(NIC):
                p = ic % 2
                eng.wait_ge(dWGT[p], 16 * 16 * (ic // 2 + 1))
                for B in range(NTG):
                    u = ic * NTG + B
                    gbank = pbank[0 + 2 * (u % 2)]
                    ubank = pbank[1 + 2 * (u % 2)]
                    eng.wait_ge(dXQT, 16 * 8 * (B + 1))
                    if u >= 2:
                        eng.wait_ge(sSil, u - 1)
                        eng.wait_ge(vFF, u - 1)
                    for hc in range(NEC):
                        eng.matmul(gbank[:, :], wgT_b[p][:, hc * 128:(hc + 1) * 128],
                                   xqT_sb[hc][:, B * 512:(B + 1) * 512],
                                   start=(hc == 0), stop=(hc == NEC - 1))
                    for hc in range(NEC):
                        mm = eng.matmul(ubank[:, :], wuT_b[p][:, hc * 128:(hc + 1) * 128],
                                   xqT_sb[hc][:, B * 512:(B + 1) * 512],
                                   start=(hc == 0), stop=(hc == NEC - 1))
                    mm.then_inc(tFF, 1)
            if SA < 10:
                return
            eng.wait_ge(vSc3, 2)
            eng.matmul(pM[:, 0:2], onesb[0:1, :], rowb[0:1, 130:132],
                       start=True, stop=True).then_inc(tSc3, 1)
            eng.wait_ge(vSwd, 1)
            eng.wait_ge(dSWR, 16 * 65)
            eng.matmul(pM[:, 0:512], onesb[0:1, :], swd_row[0:1, 0:512], start=True, stop=True)
            eng.matmul(pM[:, 512:1024], onesb[0:1, :], swd_row[0:1, 512:1024],
                       start=True, stop=True).then_inc(tSw, 1)
            if SA < 11:
                return
            eng.wait_ge(dWDT, 16 * 44)
            for B in range(NTG):
                eng.wait_ge(vHq, NIC * (B + 1))
                hsb = hint_sb[B % 2]
                for tci in range(4):
                    tc = B * 4 + tci
                    for oh in range(2):
                        grp = tc * 2 + oh
                        bk_ = grp % 4
                        if grp >= 4:
                            eng.wait_ge(vX2, (grp - 4) // 2 + 1)
                        for ic in range(NIC):
                            mm = eng.matmul(pbank[bk_][:, :],
                                       hsb[ic][:, tci * 128:(tci + 1) * 128],
                                       wdT_sb[:, (ic * 2 + oh) * 512:(ic * 2 + oh + 1) * 512],
                                       start=(ic == 0), stop=(ic == NIC - 1))
                    mm.then_inc(tDn, 1)

        @block.vector
        def _(eng):
            eng.memset(scr[:, 28:32], 0.0)
            for t in range(NTT):
                p = t % 2
                eng.wait_ge(dX[p], 16 * (t // 2 + 1))
                emit_ln_stats(eng, xb[p], scr[:, 1:2])
                spacer(eng)
                spacer(eng).then_inc(vVar, 1) if False else None
                eng.tensor_copy(scr[:, 30:31], scr[:, 31:32]).then_inc(vVar, 1)
                eng.wait_ge(sSqrt, t + 1)
                eng.reciprocal(scr[:, 8:9], scr[:, 7:8])
                spacer(eng)
                if t >= 2:
                    eng.wait_ge(dXLN4[(t - 2) % 4], 16 * ((t - 2) // 4 + 1))
                eng.tensor_scalar(xln_b[p], xb[p], scr[:, 1:2], scr[:, 8:9],
                                  ALU.subtract, ALU.mult).then_inc(vApp, 1)
                for idx in (2 * t, 2 * t + 1):
                    if idx < 4 * NEC:
                        pw = idx % 2
                        eng.wait_ge(dW[pw], 16 * (idx // 2 + 1))
                        if idx >= 2:
                            eng.wait_ge(dWS[pw], 16 * (idx // 2))
                        eng.tensor_copy(wb_b[pw], wf_b[pw][:, 0:1024]).then_inc(vCast, 1)
            if SA < 2:
                return
            for idx in range(NTG * 24):
                sl = idx % 4
                g = idx // 24
                r = idx % 24
                br, ec = r // 8, r % 8
                eng.wait_ge(sCp4[sl], idx // 4 + 1)
                base = (br * 8 + ec) * 64 + g * 8
                eng.bn_stats(bnsl[:, base:base + 6], hb4[sl][:, 0:512]).then_inc(vBn4[sl], 1)
            if SA < 3:
                return
            for br in range(3):
                for ec in range(NEC):
                    base = (br * 8 + ec) * 64
                    bme = bass.AP(bnsl, base + 1, [[1536, 128], [8, 8]])
                    bmo = bass.AP(bnsl, base + 4, [[1536, 128], [8, 8]])
                    bce = bass.AP(bnsl, base + 2, [[1536, 128], [8, 8]])
                    bco = bass.AP(bnsl, base + 5, [[1536, 128], [8, 8]])
                    eng.tensor_tensor(scr[:, 16:24], bme, bmo, ALU.add)
                    eng.tensor_tensor(scr[:, 32:40], bce, bco, ALU.add)
                    eng.tensor_tensor(scr[:, 40:48], bme, bme, ALU.mult)
                    eng.tensor_tensor(scr[:, 48:56], bmo, bmo, ALU.mult)
                    eng.reduce_sum(scr[:, 24:25], scr[:, 16:24], axis=AX)
                    eng.reduce_sum(scr[:, 25:26], scr[:, 32:40], axis=AX)
                    eng.tensor_add(scr[:, 56:64], scr[:, 40:48], scr[:, 48:56])
                    col = br * 8 + ec
                    eng.tensor_scalar(stat[:, col:col + 1], scr[:, 24:25], 256.0,
                                      None, ALU.mult)
                    eng.reduce_sum(scr[:, 26:27], scr[:, 56:64], axis=AX)
                    spacer(eng)
                    eng.scalar_tensor_tensor(stat[:, 24 + col:25 + col], scr[:, 26:27], 256.0,
                                             scr[:, 25:26], ALU.mult, ALU.add)
            spacer(eng)
            eng.tensor_copy(scr[:, 30:31], scr[:, 31:32]).then_inc(vSt, 1)
            eng.wait_ge(dST, 32)
            NTOT = float(NC_N * NT)
            eng.tensor_scalar(scr2[:, 0:24], gst[:, 0:24], 1.0 / NTOT, None, ALU.mult)
            eng.tensor_scalar(scr2[:, 24:48], gst[:, 24:48], 1.0 / NTOT, None, ALU.mult)
            eng.tensor_tensor(scr[:, 0:24], scr2[:, 0:24], scr2[:, 0:24], ALU.mult)
            spacer(eng)
            eng.scalar_tensor_tensor(scr2[:, 40:64], scr2[:, 24:48], 1.0,
                                     scr[:, 0:24], ALU.bypass, ALU.subtract)
            spacer(eng)
            eng.tensor_scalar(scr2[:, 40:64], scr2[:, 40:64], BN_EPS, None, ALU.add)
            spacer(eng)
            eng.tensor_copy(scr[:, 30:31], scr[:, 31:32]).then_inc(vSt2, 1)
            eng.wait_ge(sSq3, 1)
            eng.reciprocal(scr[:, 0:24], scr2[:, 40:64])
            spacer(eng)
            eng.scalar_tensor_tensor(abn[:, 0:24], gam[:, 0:24], 4.0,
                                     scr[:, 0:24], ALU.mult, ALU.mult)
            spacer(eng)
            eng.tensor_tensor(scr2[:, 40:64], abn[:, 0:24], scr2[:, 0:24], ALU.mult)
            spacer(eng)
            eng.scalar_tensor_tensor(bbn[:, 0:24], bet[:, 0:24], 4.0,
                                     scr2[:, 40:64], ALU.mult, ALU.subtract)
            spacer(eng)
            eng.tensor_scalar(bbn[:, 0:24], bbn[:, 0:24], 4.0, None, ALU.subtract)
            spacer(eng)
            eng.tensor_copy(scr[:, 30:31], scr[:, 31:32]).then_inc(vAB, 1)
            if SA < 4:
                return
            for i in range(NTG * NEC):
                g, ec = i // NEC, i % NEC
                eng.wait_ge(sSig, 3 * i + 2)
                eng.tensor_tensor(qk_res[ec][:, g * 512:(g + 1) * 512],
                                  qb_b, kb_b, ALU.mult).then_inc(vQk, 1)
            if SA < 5:
                return
            eng.wait_ge(tMM4, NTT)
            eng.tensor_scalar(vv_sb, pG[:, :], 0.125, None, ALU.mult).then_inc(vVV, 1)
            for hp in range(8):
                eng.wait_ge(tM5, hp + 1)
                eng.tensor_copy(M_sb[hp], (pM if hp % 2 == 0 else pG)[:, :]).then_inc(vM5, 1)
            eng.wait_ge(tBp, 1)
            eng.tensor_copy(bp_bc[:, :], pG[:, :]).then_inc(vBp, 1)
            if SA < 6:
                return
            for tc in range(NTT):
                p = tc % 2
                b0 = (tc % 2) * 2
                eng.wait_ge(tP6, tc + 1)
                eng.wait_ge(gXp, tc + 1)
                if tc >= 2:
                    eng.wait_ge(dX1S[p], 16 * (tc // 2))
                eng.tensor_tensor(x1_b[p][:, 0:512], pbank[b0][:, :],
                                  xpre_b[p][:, 0:512], ALU.add)
                eng.tensor_tensor(x1_b[p][:, 512:1024], pbank[b0 + 1][:, :],
                                  xpre_b[p][:, 512:1024], ALU.add).then_inc(vX1, 1)
                emit_ln_stats(eng, x1_b[p], ln2m[:, tc:tc + 1])
                spacer(eng)
                eng.tensor_copy(scr[:, 30:31], scr[:, 31:32]).then_inc(vVar6, 1)
                eng.wait_ge(sSqrt6, NTT + tc + 1)
                eng.reciprocal(ln2r[:, tc:tc + 1], scr[:, 7:8])
                spacer(eng)
                eng.tensor_scalar(xlnscr, x1_b[p], ln2m[:, tc:tc + 1], ln2r[:, tc:tc + 1],
                                  ALU.subtract, ALU.mult)
                eng.reduce_max(mxsl[:, tc:tc + 1], xlnscr, axis=AX,
                               apply_absolute_value=True).then_inc(vMx, 1)
            spacer(eng)
            eng.reduce_max(mxa[:, 0:1], mxsl[:, 0:NTT], axis=AX)
            spacer(eng)
            eng.tensor_copy(scr[:, 30:31], scr[:, 31:32]).then_inc(vMx, 1)
            if SA < 8:
                return
            eng.wait_ge(dMX, 32)
            eng.reduce_max(rowb[0:1, 132:133], rowb[0:1, 0:128], axis=AX)
            eng.wait_ge(sClp, 2)
            spacer(eng)
            eng.tensor_tensor(rowb[0:1, 133:134], rowb[0:1, 132:133], clp[0:1, 2:3], ALU.mult)
            spacer(eng)
            eng.reciprocal(rowb[0:1, 134:135], rowb[0:1, 133:134])
            eng.tensor_scalar(rowb[0:1, 129:130], rowb[0:1, 133:134], 1.0 / 127.0, None,
                              ALU.mult)
            eng.tensor_scalar(rowb[0:1, 128:129], rowb[0:1, 134:135], 127.0, None, ALU.mult)
            spacer(eng)
            eng.tensor_copy(scr[:, 30:31], scr[:, 31:32]).then_inc(vSc, 1)
            eng.wait_ge(tSc, 1)
            eng.tensor_copy(sc_x[:, 0:2], pM[:, 0:2]).then_inc(vSc2, 1)
            spacer(eng)
            eng.tensor_scalar(scr2[:, 0:NTT], ln2r[:, 0:NTT], sc_x[:, 0:1], None, ALU.mult)
            spacer(eng)
            eng.scalar_tensor_tensor(scr2[:, 32:32 + NTT], ln2m[:, 0:NTT], -1.0,
                                     scr2[:, 0:NTT], ALU.mult, ALU.mult)
            spacer(eng)
            spacer(eng)
            for tc in range(NTT):
                p = tc % 2
                eng.wait_ge(dX1L[p], 16 * (tc // 2 + 1))
                if tc >= 2:
                    eng.wait_ge(dXQS[p], 16 * (tc // 2))
                eng.tensor_scalar(xlnscr, x1_b[p], scr2[:, tc:tc + 1], scr2[:, 32 + tc:33 + tc],
                                  ALU.mult, ALU.add)
                eng.tensor_scalar(xlnscr, xlnscr, RND_C, RND_C, ALU.add, ALU.subtract)
                eng.tensor_scalar(xq_b[p], xlnscr, 127.0, -128.0, ALU.min,
                                  ALU.max).then_inc(vXq, 1)
            for idx in range(2 * NIC + NEC):
                p = idx % 2
                wd_mode = idx >= 2 * NIC
                wid = 2816 if wd_mode else 1024
                buf = wqd_b if wd_mode else wqg_b[p]
                wfv = wf_d if wd_mode else (wf_g if idx % 2 == 0 else wf_u)
                eng.wait_ge(dWF[p], 16 * (idx // 2 + 1))
                if wd_mode:
                    if idx > 2 * NIC:
                        eng.wait_ge(dWDS, 16 * (idx - 2 * NIC))
                elif idx >= 2:
                    eng.wait_ge(dWQS[idx % 2], 16 * (idx // 2))
                eng.reduce_max(scr[:, 12:13], wfv[:, 0:wid], axis=AX,
                               apply_absolute_value=True)
                spacer(eng)
                eng.reciprocal(scr[:, 13:14], scr[:, 12:13])
                if idx < 2 * NIC:
                    dst = (sw_g if idx % 2 == 0 else sw_u)[:, idx // 2:idx // 2 + 1]
                else:
                    dst = sw_d[:, idx - 2 * NIC:idx - 2 * NIC + 1]
                eng.tensor_scalar(dst, scr[:, 12:13], 1.0 / 127.0, None, ALU.mult)
                eng.tensor_scalar(wfv[:, 0:wid], wfv[:, 0:wid],
                                  scr[:, 13:14], 127.0, ALU.mult, ALU.mult)
                eng.tensor_scalar(buf[:, 0:wid], wfv[:, 0:wid],
                                  RND_C, RND_C, ALU.add, ALU.subtract).then_inc(vWq, 1)
                if idx < 2 * NIC and idx % 2 == 1:
                    ic2 = idx // 2
                    eng.tensor_scalar(scg[:, ic2:ic2 + 1], sw_g[:, ic2:ic2 + 1],
                                      sc_x[:, 1:2], None, ALU.mult)
                    eng.tensor_scalar(scu[:, ic2:ic2 + 1], sw_u[:, ic2:ic2 + 1],
                                      sc_x[:, 1:2], None, ALU.mult)
                    eng.tensor_copy(scr[:, 30:31], scr[:, 31:32]).then_inc(vCg, 1)
            if SA < 9:
                return
            eng.wait_ge(vSc2, 1)
            eng.tensor_scalar(scg[:, :], sw_g[:, :], sc_x[:, 1:2], None, ALU.mult)
            eng.tensor_scalar(scu[:, :], sw_u[:, :], sc_x[:, 1:2], None,
                              ALU.mult).then_inc(vCg, 1)
            for u in range(NIC * NTG):
                ic = u // NTG
                B = u % NTG
                sp = u % 4
                hp_ = u % 2
                eng.wait_ge(vCg, ic + 1)
                eng.wait_ge(sSil, u + 1)
                if u >= 2:
                    eng.wait_ge(dHF[hp_], 16 * (u // 2))
                ubank = pbank[1 + 2 * (u % 2)]
                eng.scalar_tensor_tensor(hfb[hp_][:, 0:512], ubank[:, :],
                                         scu[:, ic:ic + 1], sgf[sp][:, 0:512],
                                         ALU.mult, ALU.mult)
                eng.reduce_max(mxsl[:, u:u + 1], hfb[hp_][:, 0:512], axis=AX,
                               apply_absolute_value=True).then_inc(vFF, 1)
            spacer(eng)
            eng.reduce_max(mxa[:, 1:2], mxsl[:, 0:NIC * NTG], axis=AX)
            spacer(eng)
            eng.tensor_copy(scr[:, 30:31], scr[:, 31:32]).then_inc(vFF, 1)
            if SA < 10:
                return
            eng.wait_ge(dMX, 64)
            eng.reduce_max(rowb[0:1, 135:136], rowb[0:1, 0:128], axis=AX)
            spacer(eng)
            eng.tensor_tensor(rowb[0:1, 136:137], rowb[0:1, 135:136], clp[0:1, 3:4], ALU.mult)
            spacer(eng)
            eng.reciprocal(rowb[0:1, 137:138], rowb[0:1, 136:137])
            eng.tensor_scalar(rowb[0:1, 131:132], rowb[0:1, 136:137], 1.0 / 127.0, None,
                              ALU.mult)
            eng.tensor_scalar(rowb[0:1, 130:131], rowb[0:1, 137:138], 127.0, None, ALU.mult)
            spacer(eng)
            eng.tensor_copy(scr[:, 30:31], scr[:, 31:32]).then_inc(vSc3, 1)
            eng.wait_ge(tSc3, 2)
            eng.tensor_copy(sc_x[:, 2:4], pM[:, 0:2])
            spacer(eng)
            eng.tensor_scalar(scr[:, 16:24], sw_d[:, :], sc_x[:, 3:4], None, ALU.mult)
            spacer(eng)
            eng.tensor_copy(scr[:, 30:31], scr[:, 31:32]).then_inc(vSc4, 1)
            eng.tensor_copy(scr[:, 31:32], scr[:, 30:31]).then_inc(vSwd, 1)
            eng.wait_ge(tSw, 1)
            eng.tensor_copy(swd_bc[:, :], pM[:, :]).then_inc(vSw, 1)
            if SA < 11:
                return
            eng.wait_ge(vSw, 1)
            for B in range(NTG + 1):
                if B < NTG:
                    hsb = hint_sb[B % 2]
                    if B >= 2:
                        eng.wait_ge(tDn, 4 * (B - 1))
                    for ic in range(NIC):
                        i = B * NIC + ic
                        p = i % 6
                        eng.wait_ge(dHFL6[p], 16 * (i // 6 + 1))
                        eng.tensor_scalar(xlnscr[:, 0:512], hfb6[p][:, 0:512],
                                          sc_x[:, 2:3], RND_C, ALU.mult, ALU.add)
                        eng.tensor_scalar(xlnscr[:, 0:512], xlnscr[:, 0:512],
                                          RND_C, 127.0, ALU.subtract, ALU.min)
                        eng.tensor_scalar(hsb[ic][:, 0:512], xlnscr[:, 0:512],
                                          -128.0, None, ALU.max).then_inc(vHq, 1)
                if B >= 1:
                    for tci in range(4):
                        tc = (B - 1) * 4 + tci
                        p = tc % 2
                        eng.wait_ge(tDn, tc + 1)
                        eng.wait_ge(dX1L[p], 16 * (16 + tc // 2 + 1))
                        if tc >= 2:
                            eng.wait_ge(dOUT[p], 16 * (tc // 2))
                        for oh in range(2):
                            grp = tc * 2 + oh
                            bk_ = grp % 4
                            eng.tensor_tensor(xpre_b[p][:, oh * 512:(oh + 1) * 512],
                                              pbank[bk_][:, :],
                                              swd_bc[:, oh * 512:(oh + 1) * 512], ALU.mult)
                        eng.tensor_add(xpre_b[p], xpre_b[p], x1_b[p]).then_inc(vX2, 1)

        @block.scalar
        def _(eng):
            for t in range(NTT):
                eng.wait_ge(vVar, t + 1)
                eng.activation(scr[:, 7:8], scr[:, 6:7], AF.Sqrt)
                eng.copy(scr[:, 28:29], scr[:, 29:30]).then_inc(sSqrt, 1)
            if SA < 2:
                return
            for idx in range(NTG * 24):
                sl = idx % 4
                eng.wait_ge(tMM, idx + 1)
                if idx >= 4:
                    eng.wait_ge(vBn4[sl], idx // 4)
                    eng.wait_ge(dH4[sl], 16 * (idx // 4))
                eng.copy(hb4[sl][:, 0:512], pbank[sl][:, :]).then_inc(sCp4[sl], 1)
            if SA < 3:
                return
            eng.wait_ge(vSt2, 1)
            eng.activation(scr2[:, 40:64], scr2[:, 40:64], AF.Sqrt)
            eng.copy(scr[:, 28:29], scr[:, 29:30]).then_inc(sSq3, 1)
            if SA < 4:
                return
            eng.wait_ge(vAB, 1)
            for i in range(NTG * NEC):
                ec = i % NEC
                p = i % 2
                eng.wait_ge(dHL[p], 48 * (i // 2 + 1))
                if i >= 1:
                    eng.wait_ge(vQk, i)
                eng.activation(qb_b, hq_b[p][:, 0:512], AF.Sigmoid,
                               bias=bbn[:, ec:ec + 1], scale=abn[:, ec:ec + 1])
                eng.activation(kb_b, hk_b[p][:, 0:512], AF.Sigmoid,
                               bias=bbn[:, 8 + ec:8 + ec + 1],
                               scale=abn[:, 8 + ec:8 + ec + 1])
                if i >= 2:
                    eng.wait_ge(dVS[p], 16 * (i // 2))
                eng.activation(vb_b[p], hv_b[p][:, 0:512], AF.Sigmoid,
                               bias=bbn[:, 16 + ec:16 + ec + 1],
                               scale=abn[:, 16 + ec:16 + ec + 1]).then_inc(sSig, 3)
                if False:
                        _g, _ec = (i + 2) // NEC, (i + 2) % NEC
                        _p = (i + 2) % 2
                        eng.dma_start(out=hq_b[_p][:, 0:512],
                                      in_=h_s[0, _ec * 128:(_ec + 1) * 128,
                                                  _g * 512:(_g + 1) * 512]).then_inc(dHL[_p], 16)
                        eng.dma_start(out=hk_b[_p][:, 0:512],
                                      in_=h_s[1, _ec * 128:(_ec + 1) * 128,
                                                  _g * 512:(_g + 1) * 512]).then_inc(dHL[_p], 16)
                        eng.dma_start(out=hv_b[_p][:, 0:512],
                                      in_=h_s[2, _ec * 128:(_ec + 1) * 128,
                                                  _g * 512:(_g + 1) * 512]).then_inc(dHL[_p], 16)
            if SA < 6:
                return
            for tc in range(NTT):
                eng.wait_ge(vVar6, NTT + tc + 1)
                eng.activation(scr[:, 7:8], scr[:, 6:7], AF.Sqrt)
                eng.copy(scr[:, 28:29], scr[:, 29:30]).then_inc(sSqrt6, 1)
            if SA < 8:
                return
            eng.wait_ge(dPrm, N_PRM)
            eng.activation(clp[0:1, 2:3], clp[0:1, 0:1], AF.Sigmoid)
            eng.activation(clp[0:1, 3:4], clp[0:1, 1:2], AF.Sigmoid)
            eng.copy(scr[:, 28:29], scr[:, 29:30]).then_inc(sClp, 2)
            if SA < 9:
                return
            for u in range(NIC * NTG):
                ic = u // NTG
                sp = u % 4
                eng.wait_ge(vCg, ic + 1)
                eng.wait_ge(tFF, u + 1)
                if u >= 4:
                    eng.wait_ge(vFF, u - 3)
                gbank = pbank[0 + 2 * (u % 2)]
                eng.activation(sgf[sp][:, 0:512], gbank[:, :], AF.Silu,
                               scale=scg[:, ic:ic + 1]).then_inc(sSil, 1)

        @block.gpsimd
        def _(eng):
            for t in range(NTT):
                p = t % 2
                eng.wait_ge(vApp, t + 1)
                if t >= 4:
                    eng.wait_ge(dXLN4[t % 4], 16 * (t // 4))
                eng.dma_start(out=xln_s[t * 128:(t + 1) * 128, :],
                              in_=xln_b[p]).then_inc(dXLN4[t % 4], 16)
                for idx in (2 * t, 2 * t + 1):
                    if idx < 4 * NEC:
                        pw = idx % 2
                        eng.wait_ge(vCast, idx + 1)
                        w_i, ch = idx // NEC, idx % NEC
                        eng.dma_start(out=wbf_s[w_i, ch * 128:(ch + 1) * 128, :],
                                      in_=wb_b[pw]).then_inc(dWS[pw], 16)
            if SA < 2:
                return
            for idx in range(NTG * 24):
                sl = idx % 4
                g, r = idx // 24, idx % 24
                br, ec = r // 8, r % 8
                eng.wait_ge(sCp4[sl], idx // 4 + 1)
                eng.dma_start(out=h_s[br, ec * 128:(ec + 1) * 128, g * 512:(g + 1) * 512],
                              in_=hb4[sl][:, 0:512]).then_inc(dH4[sl], 16)
            if SA < 3:
                return
            eng.wait_ge(vSt, 1)
            eng.dma_start(out=st_i[:, :], in_=stat[:, 0:48]).then_inc(dST, 16)
            eng.wait_ge(dST, 16)
            eng.collective_compute("AllReduce", ALU.add,
                                   replica_groups=[list(range(NC_N))],
                                   ins=[st_i.ap()], outs=[st_o.ap()]).then_inc(cCC, 1)
            eng.wait_ge(cCC, 1)
            eng.dma_start(out=gst[:, 0:48], in_=st_o[:, :]).then_inc(dST, 16)
            if SA < 4:
                return
            for i in range(NTG * NEC):
                g, ec = i // NEC, i % NEC
                p = i % 2
                eng.wait_ge(sSig, 3 * (i + 1))
                eng.dma_start(out=v_s[ec * 128:(ec + 1) * 128, g * 512:(g + 1) * 512],
                              in_=vb_b[p][:, 0:512]).then_inc(dVS[p], 16)
            if SA < 6:
                return
            eng.wait_ge(vBp, 1)
            for tc in range(NTT):
                p = tc % 2
                eng.wait_ge(dX6[p], 16 * (16 + tc // 2 + 1))
                if tc >= 2:
                    eng.wait_ge(vX1, tc - 1)
                eng.tensor_tensor(xpre_b[p], xb[p], bp_bc[:, :], ALU.add).then_inc(gXp, 1)
                eng.wait_ge(vX1, tc + 1)
                eng.wait_ge(vVar6, NTT + tc + 1)
                eng.dma_start(out=x1_s[tc * 128:(tc + 1) * 128, :],
                              in_=x1_b[p]).then_inc(dX1S[p], 16)
            if SA < 8:
                return
            eng.wait_ge(vMx, NTT + 1)
            eng.dma_start(out=mx_i.ap(), in_=mxa[:, 0:1]).then_inc(dMX, 16)
            eng.wait_ge(dMX, 16)
            eng.collective_compute("AllReduce", ALU.max,
                                   replica_groups=[list(range(NC_N))],
                                   ins=[mx_i.ap()], outs=[mx_o.ap()]).then_inc(cCC, 1)
            eng.wait_ge(cCC, 2)
            eng.dma_start(out=rowb[0:1, 0:128], in_=mx_o.ap()).then_inc(dMX, 16)
            for tc in range(NTT):
                p = tc % 2
                eng.wait_ge(vXq, tc + 1)
                eng.dma_start(out=xq_s[tc * 128:(tc + 1) * 128, :],
                              in_=xq_b[p]).then_inc(dXQS[p], 16)
            for idx in range(2 * NIC + NEC):
                p = idx % 2
                eng.wait_ge(vWq, idx + 1)
                if idx < 2 * NIC:
                    ic2 = idx // 2
                    tgt = wgq_s if idx % 2 == 0 else wuq_s
                    eng.dma_start(out=tgt[ic2 * 128:(ic2 + 1) * 128, :],
                                  in_=wqg_b[p][:, 0:1024]).then_inc(dWQS[p], 16)
                elif False:
                    pass
                else:
                    ch = idx - 2 * NIC
                    eng.dma_start(out=wdq_s[ch * 128:(ch + 1) * 128, :],
                                  in_=wqd_b[:, 0:2816]).then_inc(dWDS, 16)
            if SA < 9:
                return
            for u in range(NIC * NTG):
                ic, B = u // NTG, u % NTG
                hp_ = u % 2
                eng.wait_ge(vFF, u + 1)
                eng.dma_start(out=hf_s[ic * 128:(ic + 1) * 128, B * 512:(B + 1) * 512],
                              in_=hfb[hp_][:, 0:512]).then_inc(dHF[hp_], 16)
            if SA < 10:
                return
            eng.wait_ge(vFF, NIC * NTG + 1)
            eng.dma_start(out=mh_i.ap(), in_=mxa[:, 1:2]).then_inc(dMX, 16)
            eng.wait_ge(dMX, 48)
            eng.collective_compute("AllReduce", ALU.max,
                                   replica_groups=[list(range(NC_N))],
                                   ins=[mh_i.ap()], outs=[mh_o.ap()]).then_inc(cCC, 1)
            eng.wait_ge(cCC, 3)
            eng.dma_start(out=rowb[0:1, 0:128], in_=mh_o.ap()).then_inc(dMX, 16)
            eng.wait_ge(vSwd, 1)
            for hc in range(NEC):
                eng.dma_start(out=swd_d[hc * 128:(hc + 1) * 128],
                              in_=scr[:, 16 + hc:17 + hc]).then_inc(dSW, 16)
            if SA < 11:
                return
            for tc in range(NTT):
                p = tc % 2
                eng.wait_ge(vX2, tc + 1)
                eng.dma_start(out=out_t[tc * 128:(tc + 1) * 128, :],
                              in_=xpre_b[p]).then_inc(dOUT[p], 16)

    es.close()
    return nc


_NC_CACHE = {}


def _get_nc():
    if "nc" not in _NC_CACHE:
        _NC_CACHE["nc"] = build_nc()
    return _NC_CACHE["nc"]


def kernel(L0_attn_x, L0_attn_Wq, L0_attn_gq, L0_attn_bq, L0_attn_Wk, L0_attn_gk,
           L0_attn_bk, L0_attn_Wv, L0_attn_gv, L0_attn_bv, L0_attn_Wp, L0_attn_bp,
           L0_ffn_clip_x, L0_ffn_clip_down, L0_ffn_w_gate, L0_ffn_w_up, L0_ffn_w_down):
    nc = _get_nc()
    f32 = np.float32
    common = {
        "Wq": np.ascontiguousarray(L0_attn_Wq, f32),
        "Wk": np.ascontiguousarray(L0_attn_Wk, f32),
        "Wv": np.ascontiguousarray(L0_attn_Wv, f32),
        "Wp": np.ascontiguousarray(L0_attn_Wp, f32),
        "gq": np.ascontiguousarray(L0_attn_gq, f32),
        "bq": np.ascontiguousarray(L0_attn_bq, f32),
        "gk": np.ascontiguousarray(L0_attn_gk, f32),
        "bk": np.ascontiguousarray(L0_attn_bk, f32),
        "gv": np.ascontiguousarray(L0_attn_gv, f32),
        "bv": np.ascontiguousarray(L0_attn_bv, f32),
        "bp": np.ascontiguousarray(L0_attn_bp, f32),
        "clip_x": np.ascontiguousarray(L0_ffn_clip_x, f32),
        "clip_dn": np.ascontiguousarray(L0_ffn_clip_down, f32),
        "w_gate": np.ascontiguousarray(L0_ffn_w_gate, f32),
        "w_up": np.ascontiguousarray(L0_ffn_w_up, f32),
        "w_down": np.ascontiguousarray(L0_ffn_w_down, f32),
        "ones_r": np.ones((1, 128), f32),
    }
    xs = np.ascontiguousarray(L0_attn_x, f32)
    in_maps = [dict(common, x=xs[b]) for b in range(NC_N)]
    res = run_bass_kernel_spmd(nc, in_maps, core_ids=list(range(NC_N)))
    return np.stack([np.asarray(r["out"]) for r in res.results], axis=0)
